# version 09
# baseline (speedup 1.0000x reference)
import math
import contextlib
import numpy as np
import concourse.bass as bass
import concourse.mybir as mybir
from concourse.bass_utils import run_bass_kernel_spmd
from concourse.aot_env import allow_ici_all_to_all

F32 = mybir.dt.float32
BF16 = mybir.dt.bfloat16
AF = mybir.ActivationFunctionType
ALU = mybir.AluOpType
AX = mybir.AxisListType.X

NCORE = 8
T = 2048
TG = 16384
D = 2048
KD = D // 128
NI = 8192
KI = NI // 128
DI = 4096
KDI = DI // 128
DS = 16
DTR = 128
RC = 12582912.0
EPS = 1e-5
NSEG = T // 512

MODE = "f32"


CVTAB = [("bin0", 2048), ("bdec0", 2048), ("dbias0", 2048), ("bgate0", 2048),
         ("bout0", 2048), ("b1", 8192), ("lng", 8192), ("lnb", 8192),
         ("b2", 2048), ("bin1", 2048),
         ("cwf0", 4096), ("cwb0", 4096), ("cwf1", 4096), ("cwb1", 4096),
         ("cwf2", 4096), ("cwb2", 4096), ("cwf3", 4096), ("cwb3", 4096),
         ("cbf", 4096), ("cbb", 4096), ("dtbf", 4096), ("dtbb", 4096),
         ("Df", 4096), ("Db", 4096), ("lngr", 2048), ("lnbr", 2048),
         ("bout1", 2048), ("bffn", 2048), ("ones", 128)]
CM = {}
NCV = 0
for _nm, _n in CVTAB:
    CM[_nm] = NCV
    NCV += _n // 128


def _pp(v, n_ch):
    return np.ascontiguousarray(np.asarray(v, np.float32).reshape(n_ch // 128, 128).T)


def _prep_inputs(I):
    x = np.asarray(I["L0_attn_x"], np.float32)[0]
    tr = lambda a: np.ascontiguousarray(np.asarray(a, np.float32).T)
    cc = np.ascontiguousarray

    com = {}
    com["wl0in"] = cc(np.asarray(I["L0_attn_W_in"], np.float32))
    com["wl0dec"] = cc(np.asarray(I["L0_attn_W_decay"], np.float32))
    com["wl0gate"] = cc(np.asarray(I["L0_attn_W_gate"], np.float32))
    com["wl0out"] = cc(np.asarray(I["L0_attn_W_out"], np.float32))
    com["w1T"] = tr(I["L0_ffn_w1"])
    com["w2T"] = tr(I["L0_ffn_w2"])
    com["wmodT0"] = tr(I["L0_wmod"])
    com["wmodT1"] = tr(I["L1_wmod"])
    com["win1T"] = tr(I["L1_attn_w_in"])
    com["wfmT"] = tr(I["L1_attn_fm_in_w"])
    com["wbmT"] = tr(I["L1_attn_bm_in_w"])
    com["xprojfT"] = tr(I["L1_attn_fm_xproj_w"])
    com["xprojbT"] = tr(I["L1_attn_bm_xproj_w"])
    com["dtwfT"] = tr(I["L1_attn_fm_dt_w"])
    com["dtwbT"] = tr(I["L1_attn_bm_dt_w"])
    com["outwfT"] = tr(I["L1_attn_fm_out_w"])
    com["outwbT"] = tr(I["L1_attn_bm_out_w"])
    com["wout1T"] = tr(I["L1_attn_w_out"])
    com["wffnT"] = tr(I["L1_ffn_weight"])
    com["wffnN"] = cc(np.asarray(I["L1_ffn_weight"], np.float32))

    vals = {
        "bin0": I["L0_attn_b_in"], "bdec0": I["L0_attn_b_decay"],
        "dbias0": I["L0_attn_decay_bias"], "bgate0": I["L0_attn_b_gate"],
        "bout0": I["L0_attn_b_out"], "b1": I["L0_ffn_b1"],
        "lng": I["L0_ffn_ln_g"], "lnb": I["L0_ffn_ln_b"], "b2": I["L0_ffn_b2"],
        "bin1": I["L1_attn_b_in"], "cbf": I["L1_attn_fm_conv_b"],
        "cbb": I["L1_attn_bm_conv_b"], "dtbf": I["L1_attn_fm_dt_b"],
        "dtbb": I["L1_attn_bm_dt_b"], "Df": I["L1_attn_fm_D"],
        "Db": I["L1_attn_bm_D"], "lngr": I["L1_attn_ln_g"],
        "lnbr": I["L1_attn_ln_b"], "bout1": I["L1_attn_b_out"],
        "bffn": I["L1_ffn_bias"], "ones": np.ones(128, np.float32),
    }
    for k in range(4):
        vals[f"cwf{k}"] = np.asarray(I["L1_attn_fm_conv_w"], np.float32)[:, k]
        vals[f"cwb{k}"] = np.asarray(I["L1_attn_bm_conv_w"], np.float32)[:, k]
    cvec = np.concatenate([_pp(vals[nm], n) for nm, n in CVTAB], axis=1)
    colmap = CM

    def blkw(wT, Kt, Mt):
        a = np.asarray(wT, np.float32).reshape(Kt, 128, Mt, 128)
        return np.ascontiguousarray(a.transpose(2, 1, 0, 3).reshape(Mt, 128, Kt * 128))

    com["wl0in_b"] = blkw(com["wl0in"], 16, 16)
    com["wl0dec_b"] = blkw(com["wl0dec"], 16, 16)
    com["wl0gate_b"] = blkw(com["wl0gate"], 16, 16)
    com["wl0out_b"] = blkw(com["wl0out"], 16, 16)
    com["w1Tb"] = blkw(com["w1T"], 16, 64)
    com["w2Tb"] = blkw(com["w2T"], 64, 16)
    com["win1Tb"] = blkw(com["win1T"], 16, 16)
    com["wfmTb"] = blkw(com["wfmT"], 16, 64)
    com["wbmTb"] = blkw(com["wbmT"], 16, 64)
    com["outwfTb"] = blkw(com["outwfT"], 32, 16)
    com["outwbTb"] = blkw(com["outwbT"], 32, 16)
    com["wout1Tb"] = blkw(com["wout1T"], 16, 16)
    com["wffnTb"] = blkw(com["wffnT"], 16, 16)
    for d_, nm in (("f", "xprojfT"), ("b", "xprojbT")):
        xp = com[nm]
        a = xp[:, 0:128].reshape(32, 128, 128)
        com[f"xproj{d_}A"] = np.ascontiguousarray(
            a.transpose(1, 0, 2).reshape(128, 32 * 128))
        bb = xp[:, 128:160].reshape(32, 128, 32)
        com[f"xproj{d_}B"] = np.ascontiguousarray(
            bb.transpose(1, 0, 2).reshape(128, 32 * 32))

    crow = np.stack([np.asarray(I["L0_c"], np.float32)[0],
                     np.asarray(I["L1_c"], np.float32)[0]])
    eye = np.eye(128, dtype=np.float32)

    alog_f = np.asarray(I["L1_attn_fm_A_log"], np.float32)
    alog_b = np.asarray(I["L1_attn_bm_A_log"], np.float32)

    maps = []
    for k in range(NCORE):
        m = dict(com)
        m["cvec"] = cvec
        m["crow"] = crow
        m["eye"] = eye
        m["xT"] = cc(x[T * k:T * (k + 1)].T)
        for nm, al in (("alogf", alog_f), ("alogb", alog_b)):
            sh = al[512 * k:512 * (k + 1)]
            a = np.zeros((128, 64), np.float32)
            for dt_ in range(4):
                a[:, 16 * dt_:16 * (dt_ + 1)] = sh[128 * dt_:128 * (dt_ + 1)]
            m[nm] = a
        sel = np.zeros((128, 24), np.float32)
        if k > 0:
            sel[:, k - 1] = 1.0
        if k < 7:
            sel[:, 8 + (k + 1)] = 1.0
        m["selv"] = sel
        maps.append(m)
    return maps, colmap


class Bld:
    def __init__(self, nc, es):
        self.nc = nc
        self.es = es
        self.sem = {}
        self.c = {}
        for n in ("LD", "SP", "TD", "VD", "SD", "GD", "CC",
                  "WLD", "WFREE", "QD", "QFREE", "EV", "LD2"):
            self.sem[n] = es.enter_context(nc.semaphore(n))
            self.c[n] = 0

    def inc(self, inst, name, k=1):
        inst.then_inc(self.sem[name], k)
        self.c[name] += k
        return inst

    def bump(self, name, k=1):
        self.c[name] += k

    def w(self, eng, name, v=None):
        eng.wait_ge(self.sem[name], self.c[name] if v is None else v)

    def barrier(self, block, engines=("sync", "vector", "scalar", "tensor", "gpsimd")):
        cc = dict(self.c)
        for en in engines:
            def f(e, cc=cc):
                for n, v in cc.items():
                    if v:
                        e.wait_ge(self.sem[n], v)
            getattr(block, en)(f)


def _ceil(a, b):
    return (a + b - 1) // b


def _build(upto=99):
    nc = bass.Bass("TRN2", target_bir_lowering=False)
    nc.container_num_cores = 8
    es = contextlib.ExitStack()
    nc._es = es

    inp = {}
    for nm, shp in (
        ("xT", [D, T]), ("wl0in", [D, D]), ("wl0dec", [D, D]),
        ("wl0gate", [D, D]), ("wl0out", [D, D]), ("w1T", [D, NI]),
        ("w2T", [NI, D]), ("wmodT0", [6 * D, D]), ("wmodT1", [6 * D, D]),
        ("win1T", [D, D]), ("wfmT", [D, 2 * DI]), ("wbmT", [D, 2 * DI]),
        ("xprojfT", [DI, 160]), ("xprojbT", [DI, 160]),
        ("dtwfT", [DTR, DI]), ("dtwbT", [DTR, DI]),
        ("outwfT", [DI, D]), ("outwbT", [DI, D]), ("wout1T", [D, D]),
        ("wffnT", [D, D]), ("wffnN", [D, D]),
        ("cvec", [128, NCV]), ("crow", [2, D]), ("eye", [128, 128]),
        ("alogf", [128, 64]), ("alogb", [128, 64]), ("selv", [128, 24]),
        ("wl0in_b", [16, 128, 2048]), ("wl0dec_b", [16, 128, 2048]),
        ("wl0gate_b", [16, 128, 2048]), ("wl0out_b", [16, 128, 2048]),
        ("w1Tb", [64, 128, 2048]), ("w2Tb", [16, 128, 8192]),
        ("win1Tb", [16, 128, 2048]), ("wfmTb", [64, 128, 2048]),
        ("wbmTb", [64, 128, 2048]), ("outwfTb", [16, 128, 4096]),
        ("outwbTb", [16, 128, 4096]), ("wout1Tb", [16, 128, 2048]),
        ("wffnTb", [16, 128, 2048]),
        ("xprojfA", [128, 32 * 128]), ("xprojfB", [128, 32 * 32]),
        ("xprojbA", [128, 32 * 128]), ("xprojbB", [128, 32 * 32]),
    ):
        inp[nm] = nc.dram_tensor(nm, shp, F32, kind="ExternalInput").ap()
    out_x = nc.dram_tensor("out_x", [D, T], F32, kind="ExternalOutput").ap()

    scr = {}
    for nm, shp, dt_ in (
        ("rows", [48, T], F32),
        ("d0", [D, T], F32), ("u0", [D, T], F32), ("g0", [D, T], F32),
        ("x1", [D, T], F32), ("h1", [NI, T], F32), ("x2", [D, T], F32),
        ("hin", [D, T], F32), ("x3", [D, T], F32),
        ("w2q", [16, 128, 8192], BF16),
        ("xif", [DI, T], F32), ("xib", [DI, T], F32),
        ("zsf", [DI, T], F32), ("zsb", [DI, T], F32),
        ("dtf", [DI, T], F32), ("dtb", [DI, T], F32),
        ("dxf", [DI, T], F32), ("dxb", [DI, T], F32),
        ("y3f", [DI, T], F32), ("y3b", [DI, T], F32),
        ("fout", [D, T], F32), ("bout", [D, T], F32),
        ("carry_i", [2, 128, 16], F32), ("carry_o", [NCORE, 2, 128, 16], F32),
        ("halo_i", [128, 96], F32), ("halo_o", [NCORE, 128, 96], F32),
        ("bc_i", [2, 32, T], F32), ("bc_o", [NCORE, 2, 32, T], F32),
        ("dt_shf", [NCORE, 512, T], F32), ("dx_shf", [NCORE, 512, T], F32),
        ("dt_shb", [NCORE, 512, T], F32), ("dx_shb", [NCORE, 512, T], F32),
        ("ysh_f", [NCORE, 512, T], F32), ("ysh_b", [NCORE, 512, T], F32),
        ("yo_f", [NCORE, 512, T], F32), ("yo_b", [NCORE, 512, T], F32),
    ):
        scr[nm] = nc.dram_tensor(nm, shp, dt_).ap()

    b = Bld(nc, es)
    blk = es.enter_context(nc.Block())
    _sbn = [0]

    def sbuf(stack, shape, dt_=F32):
        _sbn[0] += 1
        return stack.enter_context(nc.sbuf_tensor(f"sb{_sbn[0]}", shape, dt_))

    cv = es.enter_context(nc.sbuf_tensor("cv", [128, NCV], F32))
    dv = es.enter_context(nc.sbuf_tensor("dv", [128, 512], F32))
    eyeb = es.enter_context(nc.sbuf_tensor("eyeb", [128, 128], F32))
    selb = es.enter_context(nc.sbuf_tensor("selb", [128, 24], F32))
    alf = es.enter_context(nc.sbuf_tensor("alf", [128, 64], F32))
    alb = es.enter_context(nc.sbuf_tensor("alb", [128, 64], F32))
    pt = [es.enter_context(nc.psum_tensor(f"pt{i}", [128, 512], F32))
          for i in range(8)]

    DV = {}
    _dvoff = [0]
    for nm, w_ in (("c1pca0", KD), ("sa0", KD), ("ga0", KD), ("c1pcb0", KD),
                   ("sb0", KD), ("gb0", KD), ("c1pca1", KD), ("sa1", KD),
                   ("ga1", KD), ("c1pcb1", KD), ("sb1", KD), ("gb1", KD),
                   ("bdecx", KD), ("gabout0", KD), ("gbb2", KD),
                   ("gabout1", KD), ("gbbffn", KD), ("Af", 64), ("Ab", 64)):
        DV[nm] = _dvoff[0]
        _dvoff[0] += w_
    assert _dvoff[0] <= 512

    def CV(name, i=0):
        return cv[:, CM[name] + i:CM[name] + i + 1]

    def DVC(name, i=0):
        return dv[:, DV[name] + i:DV[name] + i + 1]

    def s0():
        base = dict(b.c)

        def loads(e):
            for nm, dst in (("cvec", cv), ("eye", eyeb), ("selv", selb),
                            ("alogf", alf), ("alogb", alb)):
                b.inc(e.dma_start(out=dst[:], in_=inp[nm]), "LD", 16)
        blk.sync(loads)

        s0st = contextlib.ExitStack()
        scrow = s0st.enter_context(nc.sbuf_tensor("scrow", [1, 2 * D], F32))

        def sc0(e):
            b.w(e, "LD")
            b.inc(e.dma_start(out=scrow[:], in_=inp["crow"].rearrange("a b -> (a b)").unsqueeze(0)), "LD", 16)
            b.w(e, "LD")
            i1 = e.activation(scrow[:], scrow[:], AF.Silu)
            b.inc(i1, "SD")
            for src, off in ((alf, DV["Af"]), (alb, DV["Ab"])):
                i2 = e.activation(dv[:, off:off + 64], src[:], AF.Exp)
                b.inc(i2, "SD")
        blk.scalar(sc0)

        def v0(e):
            b.w(e, "SD")
            for nm in ("Af", "Ab"):
                i = e.tensor_scalar(dv[:, DV[nm]:DV[nm] + 64],
                                    dv[:, DV[nm]:DV[nm] + 64], -1.0, None, ALU.mult)
                b.inc(i, "VD")
        blk.vector(v0)

        def gst(e):
            b.w(e, "SD")
            b.inc(e.dma_start(out=scr["rows"][0:2, :].rearrange("a b -> (a b)").unsqueeze(0), in_=scrow[:]), "SP", 16)
        blk.gpsimd(gst)

        scb = s0st.enter_context(nc.sbuf_tensor("scb", [128, D], F32))
        wmb = [s0st.enter_context(nc.sbuf_tensor(f"wmb{i}", [128, D], F32))
               for i in range(2)]
        ttro = s0st.enter_context(nc.sbuf_tensor("ttro", [128, D], F32))
        modv = [s0st.enter_context(nc.sbuf_tensor(f"modv{i}", [128, 96], F32))
                for i in range(2)]
        base2 = dict(b.c)

        def ld1(e):
            b.w(e, "SP")
            for l in range(2):
                if l == 1:
                    e.wait_ge(b.sem["VD"], base2["VD"] + 96)
                b.inc(e.dma_start(
                    out=scb[:], in_=scr["rows"][l:l + 1, :].broadcast_to([128, D])),
                    "LD", 16)
                for j in range(96):
                    idx = 96 * l + j
                    if idx >= 2:
                        e.wait_ge(b.sem["VD"], base2["VD"] + idx - 1)
                    b.inc(e.dma_start(out=wmb[idx % 2][:],
                                      in_=inp[f"wmodT{l}"][128 * j:128 * (j + 1), :]),
                          "LD", 16)
        blk.sync(ld1)

        def v1(e):
            for l in range(2):
                for j in range(96):
                    idx = 96 * l + j
                    e.wait_ge(b.sem["LD"], base2["LD"] + 16 * (2 + idx + l))
                    e.tensor_tensor(ttro[:], wmb[idx % 2][:], scb[:], ALU.mult)
                    i = e.reduce_sum(modv[l][:, j:j + 1], ttro[:], axis=AX)
                    b.inc(i, "VD")
        blk.vector(v1)

        def v2(e):
            b.w(e, "VD")
            fk = [b.c["VD"]]

            def fence(inst):
                fk[0] += 1
                inst.then_inc(b.sem["VD"], 1)
                b.c["VD"] += 1
                e.wait_ge(b.sem["VD"], fk[0])
            for l in range(2):
                sfx = str(l)
                g = modv[l]
                i = e.tensor_scalar(dv[:, DV["c1pca" + sfx]:DV["c1pca" + sfx] + KD],
                                    g[:, 16:32], 1.0, None, ALU.add)
                i = e.tensor_scalar(dv[:, DV["c1pcb" + sfx]:DV["c1pcb" + sfx] + KD],
                                    g[:, 64:80], 1.0, None, ALU.add)
                i = e.tensor_copy(dv[:, DV["sa" + sfx]:DV["sa" + sfx] + KD], g[:, 0:16])
                i = e.tensor_copy(dv[:, DV["sb" + sfx]:DV["sb" + sfx] + KD], g[:, 48:64])
                for gnm, c0 in (("ga" + sfx, 32), ("gb" + sfx, 80)):
                    gg = g[:, c0:c0 + 16]
                    dd = dv[:, DV[gnm]:DV[gnm] + KD]
                    fence(e.tensor_scalar(ttro[:, 0:16], gg, 0.0, None, ALU.is_ge))
                    fence(e.tensor_scalar(ttro[:, 16:32], ttro[:, 0:16], 2.0, -1.0,
                                          ALU.mult, ALU.add))
                    fence(e.tensor_scalar(ttro[:, 32:48], gg, -1.0, None, ALU.mult))
                    fence(e.tensor_tensor(ttro[:, 48:64], gg, ttro[:, 32:48],
                                          ALU.max))
                    fence(e.tensor_scalar(ttro[:, 64:80], ttro[:, 48:64], 0.25, None,
                                          ALU.max))
                    fence(e.tensor_tensor(dd, ttro[:, 16:32], ttro[:, 64:80],
                                          ALU.mult))
            e.tensor_tensor(dv[:, DV["bdecx"]:DV["bdecx"] + KD],
                            cv[:, CM["bdec0"]:CM["bdec0"] + KD],
                            cv[:, CM["dbias0"]:CM["dbias0"] + KD], ALU.add)
            e.tensor_tensor(dv[:, DV["gabout0"]:DV["gabout0"] + KD],
                            dv[:, DV["ga0"]:DV["ga0"] + KD],
                            cv[:, CM["bout0"]:CM["bout0"] + KD], ALU.mult)
            e.tensor_tensor(dv[:, DV["gbb2"]:DV["gbb2"] + KD],
                            dv[:, DV["gb0"]:DV["gb0"] + KD],
                            cv[:, CM["b2"]:CM["b2"] + KD], ALU.mult)
            e.tensor_tensor(dv[:, DV["gabout1"]:DV["gabout1"] + KD],
                            dv[:, DV["ga1"]:DV["ga1"] + KD],
                            cv[:, CM["bout1"]:CM["bout1"] + KD], ALU.mult)
            i = e.tensor_tensor(dv[:, DV["gbbffn"]:DV["gbbffn"] + KD],
                                dv[:, DV["gb1"]:DV["gb1"] + KD],
                                cv[:, CM["bffn"]:CM["bffn"] + KD], ALU.mult)
            b.inc(i, "VD")
        blk.vector(v2)
        b.barrier(blk)
        s0st.close()
    s0()

    onescol = cv[:, CM["ones"]:CM["ones"] + 1]

    ROWS = {"mu1": 2, "rstd1": 3, "mu2": 4, "rstd2": 5, "r1": 6, "m0": 7,
            "f1": 8, "ff1": 9, "muh": 10, "rstdh": 11, "f2": 12, "ff2": 13,
            "mu3": 14, "rstd3": 15, "mur": 16, "rstdr": 17, "mu4": 18,
            "rstd4": 19, "sw1": 20, "sw2": 21}

    def row_ap(nm, n0=0, n1=T):
        r = ROWS[nm]
        return scr["rows"][r:r + 1, n0:n1]

    def ln_stats(src_ap_fn, nK, mu_nm, rstd_nm, eps=EPS, denom=None):
        dn = denom or (nK * 128)
        base = dict(b.c)
        with contextlib.ExitStack() as st_:
            xb = [sbuf(st_, [128, T], F32)
                  for _ in range(2)]
            xq = [sbuf(st_, [128, T], F32)
                  for _ in range(2)]
            rr = sbuf(st_, [1, 4 * T], F32)

            def ld(e):
                for m in range(nK):
                    if m >= 2:
                        e.wait_ge(b.sem["TD"], base["TD"] + m - 1)
                    b.inc(e.dma_start(out=xb[m % 2][:], in_=src_ap_fn(m)), "LD", 16)
            blk.sync(ld)

            def sq(e):
                for m in range(nK):
                    e.wait_ge(b.sem["LD"], base["LD"] + 16 * (m + 1))
                    if m >= 2:
                        e.wait_ge(b.sem["TD"], base["TD"] + m - 1)
                    b.inc(e.activation(xq[m % 2][:], xb[m % 2][:], AF.Square), "SD")
            blk.scalar(sq)

            def mm(e):
                for m in range(nK):
                    e.wait_ge(b.sem["LD"], base["LD"] + 16 * (m + 1))
                    e.wait_ge(b.sem["SD"], base["SD"] + m + 1)
                    for n in range(NSEG):
                        e.matmul(pt[n][0:1, :], onescol, xb[m % 2][:, 512 * n:512 * (n + 1)],
                                 start=(m == 0), stop=(m == nK - 1))
                        i = e.matmul(pt[4 + n][0:1, :], onescol,
                                     xq[m % 2][:, 512 * n:512 * (n + 1)],
                                     start=(m == 0), stop=(m == nK - 1))
                    b.inc(i, "TD")
            blk.tensor(mm)

            def ev_s(e):
                b.w(e, "TD")
                for n in range(NSEG):
                    e.mul(rr[0:1, 512 * n:512 * (n + 1)], pt[n][0:1, :], 1.0 / dn)
                    e.mul(rr[0:1, T + 512 * n:T + 512 * (n + 1)], pt[4 + n][0:1, :],
                          1.0 / dn)
                b.inc(e.activation(rr[0:1, 2 * T:3 * T], rr[0:1, 0:T], AF.Square),
                      "SD")
            blk.scalar(ev_s)

            def ev_v(e):
                b.w(e, "SD")
                e.tensor_tensor(rr[0:1, T:2 * T], rr[0:1, T:2 * T],
                                rr[0:1, 2 * T:3 * T], ALU.subtract)
                e.tensor_scalar(rr[0:1, T:2 * T], rr[0:1, T:2 * T], eps, None, ALU.add)
                b.inc(e.reciprocal(rr[0:1, 2 * T:3 * T], rr[0:1, T:2 * T]), "VD")
            blk.vector(ev_v)

            def ev_s2(e):
                b.w(e, "VD")
                b.inc(e.activation(rr[0:1, T:2 * T], rr[0:1, 2 * T:3 * T], AF.Sqrt),
                      "SD")
            blk.scalar(ev_s2)

            def gst(e):
                b.w(e, "SD")
                b.inc(e.dma_start(out=row_ap(mu_nm), in_=rr[0:1, 0:T]), "SP", 16)
                b.inc(e.dma_start(out=row_ap(rstd_nm), in_=rr[0:1, T:2 * T]), "SP", 16)
            blk.gpsimd(gst)
            b.barrier(blk, ("sync", "vector", "scalar", "tensor", "gpsimd"))

    def bcast_row(dst, nm, e_sync):
        b.inc(e_sync.dma_start(
            out=dst[:], in_=row_ap(nm).broadcast_to([128, T])), "LD", 16)

    _BLK = {"wl0in": "wl0in_b", "wl0dec": "wl0dec_b", "wl0gate": "wl0gate_b",
            "wl0out": "wl0out_b", "w1T": "w1Tb", "w2T": "w2Tb",
            "win1T": "win1Tb", "wfmT": "wfmTb", "wbmT": "wbmTb",
            "outwfT": "outwfTb", "outwbT": "outwbTb", "wout1T": "wout1Tb",
            "wffnT": "wffnTb"}

    def wblk_ap(w_nm, m, kdim):
        return inp[_BLK[w_nm]][m]

    def s1():
        if upto < 1:
            return
        ln_stats(lambda m: inp["xT"][128 * m:128 * (m + 1), :], KD, "mu1", "rstd1")

        TL = T // 2
        NS2 = TL // 512
        Pl = es.enter_context(nc.sbuf_tensor("Pl", [128, 2 * KD], F32))
        Sb2 = es.enter_context(nc.sbuf_tensor("Sb2", [128, 2 * KD], F32))
        Pb = es.enter_context(nc.sbuf_tensor("Pb", [128, KD], F32))
        Sb = es.enter_context(nc.sbuf_tensor("Sb", [128, KD], F32))
        cmine = es.enter_context(nc.sbuf_tensor("cmine", [128, KD], F32))
        smid = es.enter_context(nc.sbuf_tensor("smid", [128, KD], F32))
        zcol = es.enter_context(nc.sbuf_tensor("zcol", [128, 1], F32))

        def vz(e):
            b.inc(e.memset(zcol[:], 0.0), "VD")
        blk.vector(vz)

        for half in range(2):
            t0 = TL * half
            with contextlib.ExitStack() as st_:
                hL = sbuf(st_, [128, KD * TL])
                mub = sbuf(st_, [128, TL])
                rsb = sbuf(st_, [128, TL])
                xb = [sbuf(st_, [128, TL]) for _ in range(2)]
                hs = [sbuf(st_, [128, TL]) for _ in range(2)]
                base = dict(b.c)

                def ldb(e):
                    b.inc(e.dma_start(out=mub[:], in_=row_ap("mu1", t0, t0 + TL)
                                      .broadcast_to([128, TL])), "LD", 16)
                    b.inc(e.dma_start(out=rsb[:], in_=row_ap("rstd1", t0, t0 + TL)
                                      .broadcast_to([128, TL])), "LD", 16)
                    for m in range(KD):
                        if m >= 2:
                            e.wait_ge(b.sem["VD"], base["VD"] + m - 1)
                        b.inc(e.dma_start(out=xb[m % 2][:],
                                          in_=inp["xT"][128 * m:128 * (m + 1),
                                                        t0:t0 + TL]), "LD", 16)
                blk.sync(ldb)

                def vv(e):
                    for m in range(KD):
                        e.wait_ge(b.sem["LD"], base["LD"] + 16 * (m + 3))
                        if m >= 2:
                            e.wait_ge(b.sem["SD"], base["SD"] + m - 1)
                        e.tensor_tensor(hs[m % 2][:], xb[m % 2][:], mub[:], ALU.subtract)
                        b.inc(e.tensor_tensor(hs[m % 2][:], hs[m % 2][:], rsb[:],
                                              ALU.mult), "VD")
                blk.vector(vv)

                def ss(e):
                    for m in range(KD):
                        e.wait_ge(b.sem["VD"], base["VD"] + m + 1)
                        b.inc(e.activation(hL[:, TL * m:TL * (m + 1)], hs[m % 2][:],
                                           AF.Identity, bias=DVC("sa0", m),
                                           scale=DVC("c1pca0", m)), "SD")
                blk.scalar(ss)
                b.barrier(blk)

                base = dict(b.c)
                with contextlib.ExitStack() as s2_:
                    wib = [sbuf(s2_, [128, KD * 128]) for _ in range(2)]
                    wdb = [sbuf(s2_, [128, KD * 128]) for _ in range(2)]
                    wgb = [sbuf(s2_, [128, KD * 128]) for _ in range(2)]
                    ib = [sbuf(s2_, [128, TL]) for _ in range(2)]
                    db = [sbuf(s2_, [128, TL]) for _ in range(2)]
                    gb = [sbuf(s2_, [128, TL]) for _ in range(2)]
                    ub = [sbuf(s2_, [128, TL]) for _ in range(2)]
                    ts_ = sbuf(s2_, [128, TL])
                    ssc = sbuf(s2_, [128, TL])

                    def ld2(e):
                        for m in range(KD):
                            if m >= 2:
                                e.wait_ge(b.sem["TD"], base["TD"] + 3 * (m - 1))
                            for wn, wb_ in (("wl0in", wib), ("wl0dec", wdb),
                                            ("wl0gate", wgb)):
                                b.inc(e.dma_start(out=wb_[m % 2][:],
                                                  in_=wblk_ap(wn, m, D)), "LD", 16)
                    blk.sync(ld2)

                    def mm2(e):
                        for m in range(KD):
                            e.wait_ge(b.sem["LD"], base["LD"] + 48 * (m + 1))
                            if m >= 1:
                                e.wait_ge(b.sem["SD"], base["SD"] + 7 * m)
                            for w_, b0 in ((wib[m % 2], 0), (wdb[m % 2], NS2)):
                                for n in range(NS2):
                                    for k in range(KD):
                                        i = e.matmul(
                                            pt[b0 + n][:],
                                            w_[:, 128 * k:128 * (k + 1)],
                                            hL[:, TL * k + 512 * n:TL * k + 512 * (n + 1)],
                                            start=(k == 0), stop=(k == KD - 1))
                                b.inc(i, "TD")
                            e.wait_ge(b.sem["SD"], base["SD"] + 7 * m + NS2)
                            for n in range(NS2):
                                for k in range(KD):
                                    i = e.matmul(
                                        pt[n][:],
                                        wgb[m % 2][:, 128 * k:128 * (k + 1)],
                                        hL[:, TL * k + 512 * n:TL * k + 512 * (n + 1)],
                                        start=(k == 0), stop=(k == KD - 1))
                            b.inc(i, "TD")
                    blk.tensor(mm2)

                    def ev2(e):
                        for m in range(KD):
                            if m >= 2:
                                e.wait_ge(b.sem["VD"], base["VD"] + 2 * (m - 1))
                                e.wait_ge(b.sem["SP"], base["SP"] + 48 * (m - 1))
                            e.wait_ge(b.sem["TD"], base["TD"] + 3 * m + 1)
                            for n in range(NS2):
                                b.inc(e.activation(ib[m % 2][:, 512 * n:512 * (n + 1)],
                                                   pt[n][:], AF.Silu,
                                                   bias=CV("bin0", m)), "SD")
                            e.wait_ge(b.sem["TD"], base["TD"] + 3 * m + 2)
                            for n in range(NS2):
                                b.inc(e.activation(db[m % 2][:, 512 * n:512 * (n + 1)],
                                                   pt[NS2 + n][:], AF.Sigmoid,
                                                   bias=DVC("bdecx", m)), "SD")
                            e.wait_ge(b.sem["TD"], base["TD"] + 3 * m + 3)
                            for n in range(NS2):
                                b.inc(e.activation(gb[m % 2][:, 512 * n:512 * (n + 1)],
                                                   pt[n][:], AF.Sigmoid,
                                                   bias=CV("bgate0", m)), "SD")
                            e.wait_ge(b.sem["VD"], base["VD"] + 2 * m)
                            b.inc(e.activation(ssc[:], db[m % 2][:], AF.Ln), "SD")
                    blk.scalar(ev2)

                    def v2(e):
                        for m in range(KD):
                            e.wait_ge(b.sem["SD"], base["SD"] + 7 * m + 2 * NS2)
                            e.tensor_tensor(ts_[:], db[m % 2][:], ib[m % 2][:], ALU.mult)
                            b.inc(e.tensor_tensor(ub[m % 2][:], ib[m % 2][:], ts_[:],
                                                  ALU.subtract), "VD")
                            e.wait_ge(b.sem["SD"], base["SD"] + 7 * m + 7)
                            e.reduce_sum(Pl[:, 2 * m + half:2 * m + half + 1],
                                         ssc[:], axis=AX)
                            e.tensor_tensor_scan(ssc[:], db[m % 2][:], ub[m % 2][:],
                                                 zcol[:], ALU.mult, ALU.add)
                            b.inc(e.tensor_copy(Sb2[:, 2 * m + half:2 * m + half + 1],
                                                ssc[:, TL - 1:TL]), "VD")
                    blk.vector(v2)

                    def g2(e):
                        for m in range(KD):
                            e.wait_ge(b.sem["VD"], base["VD"] + 2 * (m + 1))
                            e.wait_ge(b.sem["SD"], base["SD"] + 7 * m + 7)
                            b.inc(e.dma_start(
                                out=scr["d0"][128 * m:128 * (m + 1), t0:t0 + TL],
                                in_=db[m % 2][:]), "SP", 16)
                            b.inc(e.dma_start(
                                out=scr["u0"][128 * m:128 * (m + 1), t0:t0 + TL],
                                in_=ub[m % 2][:]), "SP", 16)
                            b.inc(e.dma_start(
                                out=scr["g0"][128 * m:128 * (m + 1), t0:t0 + TL],
                                in_=gb[m % 2][:]), "SP", 16)
                    blk.gpsimd(g2)
                    b.barrier(blk)

        def vc(e):
            b.w(e, "SD")
            b.w(e, "VD")
            b.inc(e.tensor_tensor(Pb[:], Pl[:, 0::2], Pl[:, 1::2], ALU.add), "VD")
            e.wait_ge(b.sem["VD"], b.c["VD"])
            b.inc(e.tensor_copy(ssc_carry[:, 0:KD], Pl[:, 1::2]), "VD")
            e.wait_ge(b.sem["VD"], b.c["VD"])
        ssc_carry = es.enter_context(nc.sbuf_tensor("ssc_carry", [128, 2 * KD], F32))
        blk.vector(vc)

        def sc_c(e):
            b.w(e, "VD")
            b.inc(e.activation(Pb[:], Pb[:], AF.Exp), "SD")
            b.inc(e.activation(ssc_carry[:, 0:KD], ssc_carry[:, 0:KD], AF.Exp), "SD")
        blk.scalar(sc_c)

        def vc2(e):
            b.w(e, "SD")
            b.inc(e.tensor_tensor(Sb[:], ssc_carry[:, 0:KD], Sb2[:, 0::2], ALU.mult),
                  "VD")
            e.wait_ge(b.sem["VD"], b.c["VD"])
            b.inc(e.tensor_tensor(Sb[:], Sb[:], Sb2[:, 1::2], ALU.add), "VD")
            e.wait_ge(b.sem["VD"], b.c["VD"])
        blk.vector(vc2)
        b.barrier(blk, ("gpsimd",))

        def gx(e):
            b.inc(e.dma_start(out=scr["carry_i"][0], in_=Pb[:]), "SP", 16)
            b.inc(e.dma_start(out=scr["carry_i"][1], in_=Sb[:]), "SP", 16)
            b.w(e, "SP")
            b.inc(e.collective_compute(
                "AllGather", ALU.bypass, replica_groups=[list(range(NCORE))],
                ins=[scr["carry_i"]], outs=[scr["carry_o"]]), "CC")
            b.w(e, "CC")
        blk.gpsimd(gx)

        psall = es.enter_context(nc.sbuf_tensor("psall", [128, 8 * 2 * KD], F32))

        def lx(e):
            b.w(e, "CC")
            b.inc(e.dma_start(
                out=psall[:],
                in_=scr["carry_o"].rearrange("j two p s -> p j two s")), "LD", 16)
        blk.sync(lx)

        def vx(e):
            b.w(e, "LD")
            fk = [b.c["VD"]]

            def fence(inst):
                fk[0] += 1
                inst.then_inc(b.sem["VD"], 1)
                b.c["VD"] += 1
                e.wait_ge(b.sem["VD"], fk[0])
            fence(e.memset(cmine[:], 0.0))
            fence(e.memset(ssc_carry[:, 0:KD], 0.0))
            for j in range(7):
                Pj = psall[:, (2 * j + 0) * KD:(2 * j + 0) * KD + KD]
                Sj = psall[:, (2 * j + 1) * KD:(2 * j + 1) * KD + KD]
                fence(e.tensor_tensor(ssc_carry[:, KD:2 * KD], Pj,
                                      ssc_carry[:, 0:KD], ALU.mult))
                fence(e.tensor_tensor(ssc_carry[:, 0:KD], Sj,
                                      ssc_carry[:, KD:2 * KD], ALU.add))
                fence(e.tensor_scalar(ssc_carry[:, KD:2 * KD], ssc_carry[:, 0:KD],
                                      selb[:, j:j + 1], None, ALU.mult))
                fence(e.tensor_tensor(cmine[:], cmine[:], ssc_carry[:, KD:2 * KD],
                                      ALU.add))
        blk.vector(vx)
        b.barrier(blk)

        for half in range(2):
            t0 = TL * half
            init_src = cmine if half == 0 else smid
            with contextlib.ExitStack() as s3_:
                yL = sbuf(s3_, [128, KD * TL])
                db = [sbuf(s3_, [128, TL]) for _ in range(2)]
                ub = [sbuf(s3_, [128, TL]) for _ in range(2)]
                gb = [sbuf(s3_, [128, TL]) for _ in range(2)]
                ssc = sbuf(s3_, [128, TL])
                base = dict(b.c)

                def ld3(e):
                    for m in range(KD):
                        if m >= 2:
                            e.wait_ge(b.sem["VD"], base["VD"] + 2 * (m - 1))
                        for snm, bb in (("d0", db), ("u0", ub), ("g0", gb)):
                            b.inc(e.dma_start(
                                out=bb[m % 2][:],
                                in_=scr[snm][128 * m:128 * (m + 1), t0:t0 + TL]),
                                "LD", 16)
                blk.sync(ld3)

                def v3(e):
                    for m in range(KD):
                        e.wait_ge(b.sem["LD"], base["LD"] + 48 * (m + 1))
                        e.tensor_tensor_scan(ssc[:], db[m % 2][:], ub[m % 2][:],
                                             init_src[:, m:m + 1], ALU.mult, ALU.add)
                        e.tensor_copy(smid[:, m:m + 1], ssc[:, TL - 1:TL])
                        b.inc(e.tensor_tensor(yL[:, TL * m:TL * (m + 1)], gb[m % 2][:],
                                              ssc[:], ALU.mult), "VD")
                        b.c["VD"] += 1
                        e.sem_inc(b.sem["VD"], 1)
                blk.vector(v3)
                b.barrier(blk, ("sync", "vector", "tensor", "scalar"))

                base = dict(b.c)
                with contextlib.ExitStack() as s4_:
                    wob = [sbuf(s4_, [128, KD * 128]) for _ in range(2)]
                    xb = [sbuf(s4_, [128, TL]) for _ in range(2)]
                    tb = [sbuf(s4_, [128, TL]) for _ in range(2)]
                    x1b = [sbuf(s4_, [128, TL]) for _ in range(2)]

                    def ld4(e):
                        for m in range(KD):
                            if m >= 2:
                                e.wait_ge(b.sem["TD"], base["TD"] + m - 1)
                                e.wait_ge(b.sem["VD"], base["VD"] + m - 1)
                            b.inc(e.dma_start(out=wob[m % 2][:],
                                              in_=wblk_ap("wl0out", m, D)), "LD", 16)
                            b.inc(e.dma_start(out=xb[m % 2][:],
                                              in_=inp["xT"][128 * m:128 * (m + 1),
                                                            t0:t0 + TL]), "LD", 16)
                    blk.sync(ld4)

                    def mm4(e):
                        for m in range(KD):
                            e.wait_ge(b.sem["LD"], base["LD"] + 32 * m + 16)
                            if m >= 1:
                                e.wait_ge(b.sem["SD"], base["SD"] + NS2 * m)
                            for n in range(NS2):
                                for k in range(KD):
                                    i = e.matmul(
                                        pt[n][:], wob[m % 2][:, 128 * k:128 * (k + 1)],
                                        yL[:, TL * k + 512 * n:TL * k + 512 * (n + 1)],
                                        start=(k == 0), stop=(k == KD - 1))
                            b.inc(i, "TD")
                    blk.tensor(mm4)

                    def ev4(e):
                        for m in range(KD):
                            e.wait_ge(b.sem["TD"], base["TD"] + m + 1)
                            if m >= 2:
                                e.wait_ge(b.sem["VD"], base["VD"] + m - 1)
                            for n in range(NS2):
                                b.inc(e.activation(tb[m % 2][:, 512 * n:512 * (n + 1)],
                                                   pt[n][:], AF.Identity,
                                                   bias=DVC("gabout0", m),
                                                   scale=DVC("ga0", m)), "SD")
                    blk.scalar(ev4)

                    def v4(e):
                        for m in range(KD):
                            e.wait_ge(b.sem["SD"], base["SD"] + NS2 * (m + 1))
                            e.wait_ge(b.sem["LD"], base["LD"] + 32 * (m + 1))
                            if m >= 2:
                                e.wait_ge(b.sem["SP"], base["SP"] + 16 * (m - 1))
                            b.inc(e.tensor_tensor(x1b[m % 2][:], xb[m % 2][:],
                                                  tb[m % 2][:], ALU.add), "VD")
                    blk.vector(v4)

                    def g4(e):
                        for m in range(KD):
                            e.wait_ge(b.sem["VD"], base["VD"] + m + 1)
                            b.inc(e.dma_start(
                                out=scr["x1"][128 * m:128 * (m + 1), t0:t0 + TL],
                                in_=x1b[m % 2][:]), "SP", 16)
                    blk.gpsimd(g4)
                    b.barrier(blk)
    s1()

    def vfence(e, inst):
        b.c["VD"] += 1
        inst.then_inc(b.sem["VD"], 1)
        e.wait_ge(b.sem["VD"], b.c["VD"])

    TL = T // 2
    NS2 = TL // 512

    def wmean_pass(wnm, ntl, wid, rowk):
        with contextlib.ExitStack() as st_:
            wb = [sbuf(st_, [128, wid]) for _ in range(2)]
            wsum = sbuf(st_, [128, ntl + 1])
            rowb = sbuf(st_, [1, 256])
            base = dict(b.c)

            def ld(e):
                for m in range(ntl):
                    if m >= 2:
                        e.wait_ge(b.sem["VD"], base["VD"] + m - 1)
                    b.inc(e.dma_start(out=wb[m % 2][:],
                                      in_=inp[wnm][128 * m:128 * (m + 1), :]),
                          "LD", 16)
            blk.sync(ld)

            def vv(e):
                for m in range(ntl):
                    e.wait_ge(b.sem["LD"], base["LD"] + 16 * (m + 1))
                    b.inc(e.reduce_sum(wsum[:, m:m + 1], wb[m % 2][:], axis=AX,
                                       apply_absolute_value=True), "VD")
                e.wait_ge(b.sem["VD"], b.c["VD"])
                vfence(e, e.reduce_sum(wsum[:, ntl:ntl + 1], wsum[:, 0:ntl],
                                       axis=AX))
            blk.vector(vv)

            def mmx(e):
                b.w(e, "VD")
                b.inc(e.matmul(pt[0][0:1, 0:128], wsum[:, ntl:ntl + 1], eyeb[:],
                               start=True, stop=True), "TD")
            blk.tensor(mmx)

            def vv2(e):
                b.w(e, "TD")
                vfence(e, e.tensor_copy(rowb[0:1, 0:128], pt[0][0:1, 0:128]))
                vfence(e, e.reduce_sum(rowb[0:1, 128:129], rowb[0:1, 0:128], axis=AX))
                vfence(e, e.tensor_scalar(rowb[0:1, 130:131], rowb[0:1, 128:129],
                                          1.0 / (ntl * 128.0 * wid), 1e-5,
                                          ALU.mult, ALU.max))
                vfence(e, e.reciprocal(rowb[0:1, 129:130], rowb[0:1, 130:131]))
            blk.vector(vv2)

            def gst(e):
                b.w(e, "VD")
                b.inc(e.dma_start(out=row_ap(rowk, 0, 2), in_=rowb[0:1, 129:131]),
                      "SP", 16)
            blk.gpsimd(gst)
            b.barrier(blk)

    def pmax_to_row(mac, rownm, t0, tl, base_off=0):
        nq = tl // 128
        with contextlib.ExitStack() as st_:
            mcol = sbuf(st_, [128, nq])
            mrow = sbuf(st_, [nq, 128])
            base = dict(b.c)

            def mmt(e):
                b.w(e, "VD")
                for q in range(nq):
                    if q >= 4:
                        e.wait_ge(b.sem["VD"], base["VD"] + q - 3)
                    b.inc(e.transpose(pt[q % 4][:, 0:128],
                                      mac[:, 128 * q:128 * (q + 1)], eyeb[:]), "TD")
                e.wait_ge(b.sem["VD"], base["VD"] + nq)
                b.inc(e.transpose(pt[4][0:nq, 0:128], mcol[:], eyeb[:]), "TD")
            blk.tensor(mmt)

            def vv(e):
                for q in range(nq):
                    e.wait_ge(b.sem["TD"], base["TD"] + q + 1)
                    b.inc(e.reduce_max(mcol[:, q:q + 1], pt[q % 4][:, 0:128],
                                       axis=AX), "VD")
            blk.vector(vv)

            def sc(e):
                e.wait_ge(b.sem["TD"], base["TD"] + nq + 1)
                b.inc(e.copy(mrow[:], pt[4][0:nq, 0:128]), "SD")
            blk.scalar(sc)

            def gst(e):
                b.w(e, "SD")
                b.inc(e.dma_start(
                    out=row_ap(rownm, t0, t0 + tl).rearrange("p (q j) -> p q j",
                                                             j=128),
                    in_=mrow[:]), "SP", 16)
            blk.gpsimd(gst)
            b.barrier(blk)

    def quant_rows(hfsrc, mac, rowb, f1b, ffdst, t0, swnm, fnm, ffnm, sqpt):
        def sv(e):
            b.w(e, "TD")
            for n in range(NS2):
                vfence(e, e.tensor_scalar(rowb[0:1, 512 * n:512 * (n + 1)],
                                          pt[sqpt + n][0:1, :], 1.0 / hfsrc, EPS,
                                          ALU.mult, ALU.add))
            vfence(e, e.reciprocal(rowb[0:1, TL:2 * TL], rowb[0:1, 0:TL]))
        blk.vector(sv)

        def sc1(e):
            b.w(e, "VD")
            b.inc(e.activation(rowb[0:1, 0:TL], rowb[0:1, TL:2 * TL], AF.Sqrt),
                  "SD")
        blk.scalar(sc1)
        b.barrier(blk, ("vector", "tensor", "gpsimd"))
        pmax_to_row(mac, "m0", t0, TL)
        with contextlib.ExitStack() as st_:
            m0b = sbuf(st_, [1, TL])
            swb = sbuf(st_, [1, 2])

            def ld2(e):
                b.inc(e.dma_start(out=m0b[:], in_=row_ap("m0", t0, t0 + TL)),
                      "LD", 16)
                b.inc(e.dma_start(out=swb[:], in_=row_ap(swnm, 0, 2)), "LD", 16)
            blk.sync(ld2)

            def v5(e):
                b.w(e, "LD")
                rA = rowb[0:1, 0:TL]
                rB = rowb[0:1, TL:2 * TL]
                vfence(e, e.tensor_tensor(rB, rA, m0b[:], ALU.mult))
                vfence(e, e.tensor_scalar(rB, rB, 1e-5, None, ALU.max))
                vfence(e, e.reciprocal(rB, rB))
                vfence(e, e.tensor_scalar(rB, rB, 127.0, None, ALU.mult))
                vfence(e, e.tensor_tensor(rA, rA, rB, ALU.mult))
                vfence(e, e.reciprocal(rB, rB))
                vfence(e, e.tensor_scalar(rB, rB, swb[0:1, 1:2], None,
                                          ALU.mult))
            blk.vector(v5)

            def gst2(e):
                b.w(e, "VD")
                b.inc(e.dma_start(out=row_ap(fnm, t0, t0 + TL),
                                  in_=rowb[0:1, 0:TL]), "SP", 16)
                b.inc(e.dma_start(out=row_ap(ffnm, t0, t0 + TL),
                                  in_=rowb[0:1, TL:2 * TL]), "SP", 16)
            blk.gpsimd(gst2)

            def ld3(e):
                b.w(e, "SP")
                b.inc(e.dma_start(out=f1b[:], in_=row_ap(fnm, t0, t0 + TL)
                                  .broadcast_to([128, TL])), "LD", 16)
                b.inc(e.dma_start(out=ffdst, in_=row_ap(ffnm, t0, t0 + TL)
                                  .broadcast_to([128, TL])), "LD", 16)
            blk.sync(ld3)
            b.barrier(blk, ("vector", "scalar", "tensor"))

    def s2():
        if upto < 2:
            return
        ln_stats(lambda m: scr["x1"][128 * m:128 * (m + 1), :], KD, "mu2", "rstd2")
        wmean_pass("w1T", KD, NI, "sw1")
        wmean_pass("w2T", KI, D, "sw2")

        s2st = contextlib.ExitStack()
        xq1 = s2st.enter_context(nc.sbuf_tensor("xq1", [128, KD * T], BF16))
        ff1b = s2st.enter_context(nc.sbuf_tensor("ff1b", [128, T], F32))

        for half in range(2):
            t0 = TL * half
            with contextlib.ExitStack() as st_:
                hf = sbuf(st_, [128, KD * TL])
                mub = sbuf(st_, [128, TL])
                rsb = sbuf(st_, [128, TL])
                xb = [sbuf(st_, [128, TL]) for _ in range(2)]
                hs = [sbuf(st_, [128, TL]) for _ in range(2)]
                sq = [sbuf(st_, [128, TL]) for _ in range(2)]
                mac = sbuf(st_, [128, TL])
                rowb = sbuf(st_, [1, 2 * TL])
                f1b = sbuf(st_, [128, TL])
                base = dict(b.c)

                def ld(e):
                    b.inc(e.dma_start(out=mub[:], in_=row_ap("mu2", t0, t0 + TL)
                                      .broadcast_to([128, TL])), "LD", 16)
                    b.inc(e.dma_start(out=rsb[:], in_=row_ap("rstd2", t0, t0 + TL)
                                      .broadcast_to([128, TL])), "LD", 16)
                    for m in range(KD):
                        if m >= 2:
                            e.wait_ge(b.sem["VD"], base["VD"] + 3 * (m - 1))
                        b.inc(e.dma_start(out=xb[m % 2][:],
                                          in_=scr["x1"][128 * m:128 * (m + 1),
                                                        t0:t0 + TL]), "LD", 16)
                blk.sync(ld)

                def vv(e):
                    for m in range(KD):
                        e.wait_ge(b.sem["LD"], base["LD"] + 16 * (m + 3))
                        if m >= 2:
                            e.wait_ge(b.sem["SD"], base["SD"] + 2 * (m - 1))
                        e.tensor_tensor(hs[m % 2][:], xb[m % 2][:], mub[:],
                                        ALU.subtract)
                        b.inc(e.tensor_tensor(hs[m % 2][:], hs[m % 2][:], rsb[:],
                                              ALU.mult), "VD")
                        e.wait_ge(b.sem["SD"], base["SD"] + 2 * m + 1)
                        hfm = hf[:, TL * m:TL * (m + 1)]
                        if m == 0:
                            b.inc(e.tensor_scalar(mac[:], hfm, -1.0, None, ALU.mult),
                                  "VD")
                            b.inc(e.tensor_tensor(mac[:], mac[:], hfm, ALU.max),
                                  "VD")
                        else:
                            b.inc(e.scalar_tensor_tensor(mac[:], hfm, -1.0, mac[:],
                                                         ALU.mult, ALU.max), "VD")
                            b.inc(e.tensor_tensor(mac[:], mac[:], hfm, ALU.max),
                                  "VD")
                blk.vector(vv)

                def ss(e):
                    for m in range(KD):
                        e.wait_ge(b.sem["VD"], base["VD"] + 3 * m + 1)
                        if m >= 2:
                            e.wait_ge(b.sem["TD"], base["TD"] + m - 1)
                        b.inc(e.activation(hf[:, TL * m:TL * (m + 1)], hs[m % 2][:],
                                           AF.Identity, bias=DVC("sb0", m),
                                           scale=DVC("c1pcb0", m)), "SD")
                        b.inc(e.activation(sq[m % 2][:], hf[:, TL * m:TL * (m + 1)],
                                           AF.Square), "SD")
                blk.scalar(ss)

                def mm(e):
                    for m in range(KD):
                        e.wait_ge(b.sem["SD"], base["SD"] + 2 * m + 2)
                        for n in range(NS2):
                            i = e.matmul(pt[4 + n][0:1, :], onescol,
                                         sq[m % 2][:, 512 * n:512 * (n + 1)],
                                         start=(m == 0), stop=(m == KD - 1))
                        b.inc(i, "TD")
                blk.tensor(mm)
                b.barrier(blk)

                quant_rows(D, mac, rowb, f1b, ff1b[:, t0:t0 + TL], t0,
                           "sw1", "f1", "ff1", 4)

                def v6(e):
                    b.w(e, "LD")
                    for m in range(KD):
                        hfm = hf[:, TL * m:TL * (m + 1)]
                        dst = xq1[:, T * m + t0:T * m + t0 + TL]
                        e.tensor_tensor(hs[0][:], hfm, f1b[:], ALU.mult)
                        e.tensor_scalar(hs[1][:], hs[0][:], RC, None, ALU.add)
                        b.inc(e.tensor_scalar(dst, hs[1][:], RC, None,
                                              ALU.subtract), "VD")
                blk.vector(v6)
                b.barrier(blk)

        with contextlib.ExitStack() as st_:
            wbl = [sbuf(st_, [128, KD * 128]) for _ in range(2)]
            wq = [sbuf(st_, [128, KD * 128], BF16) for _ in range(2)]
            tqa = sbuf(st_, [128, KD * 128])
            hv = [sbuf(st_, [128, T]) for _ in range(2)]
            hg = [sbuf(st_, [128, T]) for _ in range(2)]
            swc = sbuf(st_, [128, 1])
            base = dict(b.c)

            def ld(e):
                b.inc(e.dma_start(out=swc[:],
                                  in_=row_ap("sw1", 0, 1).broadcast_to([128, 1])),
                      "LD", 16)
                for m in range(KI):
                    if m >= 2:
                        e.wait_ge(b.sem["VD"], base["VD"] + 2 * (m - 2) + 1)
                    b.inc(e.dma_start(out=wbl[m % 2][:], in_=wblk_ap("w1T", m, D)),
                          "LD", 16)
            blk.sync(ld)

            def vv(e):
                for m in range(KI + 1):
                    if m >= 1:
                        jj = m - 1
                        e.wait_ge(b.sem["TD"], base["TD"] + m)
                        if jj >= 2:
                            e.wait_ge(b.sem["SD"], base["SD"] + 8 * (jj - 1))
                            e.wait_ge(b.sem["SP"], base["SP"] + 16 * (jj - 1))
                        po = 4 * (jj % 2)
                        for n in range(NSEG):
                            i = e.tensor_tensor(hv[jj % 2][:, 512 * n:512 * (n + 1)],
                                                pt[po + n][:], ff1b[:, 512 * n:512 * (n + 1)],
                                                ALU.mult)
                        b.inc(i, "VD")
                    if m < KI:
                        e.wait_ge(b.sem["LD"], base["LD"] + 16 * (m + 2))
                        e.tensor_scalar(tqa[:], wbl[m % 2][:], swc[:, 0:1], RC,
                                        ALU.mult, ALU.add)
                        e.tensor_scalar(tqa[:], tqa[:], RC, None, ALU.subtract)
                        e.tensor_scalar(tqa[:], tqa[:], -1.0, 1.0, ALU.max, ALU.min)
                        b.inc(e.tensor_copy(wq[m % 2][:], tqa[:]), "VD")
            blk.vector(vv)

            def mm(e):
                for m in range(KI):
                    e.wait_ge(b.sem["VD"], base["VD"] + 2 * m + 1)
                    po = 4 * (m % 2)
                    for n in range(NSEG):
                        for k in range(KD):
                            i = e.matmul(pt[po + n][:],
                                         wq[m % 2][:, 128 * k:128 * (k + 1)],
                                         xq1[:, T * k + 512 * n:T * k + 512 * (n + 1)],
                                         start=(k == 0), stop=(k == KD - 1))
                    b.inc(i, "TD")
            blk.tensor(mm)

            def ss(e):
                for jj in range(KI):
                    e.wait_ge(b.sem["VD"], base["VD"] + 2 * jj + 2)
                    if jj >= 2:
                        e.wait_ge(b.sem["SP"], base["SP"] + 16 * (jj - 1))
                    for n in range(NSEG):
                        b.inc(e.activation(hg[jj % 2][:, 512 * n:512 * (n + 1)],
                                           hv[jj % 2][:, 512 * n:512 * (n + 1)],
                                           AF.Silu, bias=CV("b1", jj)), "SD")
                    for n in range(NSEG):
                        b.inc(e.activation(hv[jj % 2][:, 512 * n:512 * (n + 1)],
                                           hg[jj % 2][:, 512 * n:512 * (n + 1)],
                                           AF.Gelu), "SD")
            blk.scalar(ss)

            def gs(e):
                for jj in range(KI):
                    e.wait_ge(b.sem["SD"], base["SD"] + 8 * jj + 8)
                    b.inc(e.dma_start(out=scr["h1"][128 * jj:128 * (jj + 1), :],
                                      in_=hv[jj % 2][:]), "SP", 16)
            blk.gpsimd(gs)
            b.barrier(blk)
        s2st.close()

        ln_stats(lambda m: scr["h1"][128 * m:128 * (m + 1), :], KI, "muh", "rstdh")

        with contextlib.ExitStack() as st_:
            wbl = [sbuf(st_, [128, KI * 128]) for _ in range(2)]
            tqa = sbuf(st_, [128, KI * 128])
            wq = [sbuf(st_, [128, KI * 128], BF16) for _ in range(2)]
            swc = sbuf(st_, [128, 1])
            base = dict(b.c)

            def ld(e):
                b.inc(e.dma_start(out=swc[:],
                                  in_=row_ap("sw2", 0, 1).broadcast_to([128, 1])),
                      "LD", 16)
                for m in range(KD):
                    if m >= 2:
                        e.wait_ge(b.sem["VD"], base["VD"] + m - 1)
                    b.inc(e.dma_start(out=wbl[m % 2][:], in_=inp["w2Tb"][m]),
                          "LD", 16)
            blk.sync(ld)

            def vv(e):
                for m in range(KD):
                    e.wait_ge(b.sem["LD"], base["LD"] + 16 * (m + 2))
                    if m >= 2:
                        e.wait_ge(b.sem["SP"], base["SP"] + 16 * (m - 1))
                    e.tensor_scalar(tqa[:], wbl[m % 2][:], swc[:, 0:1], RC,
                                    ALU.mult, ALU.add)
                    e.tensor_scalar(tqa[:], tqa[:], RC, None, ALU.subtract)
                    e.tensor_scalar(tqa[:], tqa[:], -1.0, 1.0, ALU.max, ALU.min)
                    b.inc(e.tensor_copy(wq[m % 2][:], tqa[:]), "VD")
            blk.vector(vv)

            def gs(e):
                for m in range(KD):
                    e.wait_ge(b.sem["VD"], base["VD"] + m + 1)
                    b.inc(e.dma_start(out=scr["w2q"][m], in_=wq[m % 2][:]),
                          "SP", 16)
            blk.gpsimd(gs)
            b.barrier(blk)

        s2st2 = contextlib.ExitStack()
        ff2b = s2st2.enter_context(nc.sbuf_tensor("ff2b", [128, T], F32))
        for half in range(2):
            t0 = TL * half
            with contextlib.ExitStack() as st_:
                mub = sbuf(st_, [128, TL])
                rsb = sbuf(st_, [128, TL])
                hb = [sbuf(st_, [128, TL]) for _ in range(2)]
                hs = [sbuf(st_, [128, TL]) for _ in range(2)]
                sq = [sbuf(st_, [128, TL]) for _ in range(2)]
                mac = sbuf(st_, [128, TL])
                rowb = sbuf(st_, [1, 2 * TL])
                f2b = sbuf(st_, [128, TL])
                base = dict(b.c)

                def ld(e):
                    b.inc(e.dma_start(out=mub[:], in_=row_ap("muh", t0, t0 + TL)
                                      .broadcast_to([128, TL])), "LD", 16)
                    b.inc(e.dma_start(out=rsb[:], in_=row_ap("rstdh", t0, t0 + TL)
                                      .broadcast_to([128, TL])), "LD", 16)
                    for m in range(KI):
                        if m >= 2:
                            e.wait_ge(b.sem["VD"], base["VD"] + 3 * (m - 1))
                        b.inc(e.dma_start(out=hb[m % 2][:],
                                          in_=scr["h1"][128 * m:128 * (m + 1),
                                                        t0:t0 + TL]), "LD", 16)
                blk.sync(ld)

                def vv(e):
                    for m in range(KI):
                        e.wait_ge(b.sem["LD"], base["LD"] + 16 * (m + 3))
                        if m >= 2:
                            e.wait_ge(b.sem["SD"], base["SD"] + 2 * (m - 1))
                        e.tensor_tensor(hs[m % 2][:], hb[m % 2][:], mub[:],
                                        ALU.subtract)
                        b.inc(e.tensor_tensor(hs[m % 2][:], hs[m % 2][:], rsb[:],
                                              ALU.mult), "VD")
                        e.wait_ge(b.sem["SD"], base["SD"] + 2 * m + 1)
                        hfm = hb[m % 2][:]
                        if m == 0:
                            b.inc(e.tensor_scalar(mac[:], hfm, -1.0, None, ALU.mult),
                                  "VD")
                            b.inc(e.tensor_tensor(mac[:], mac[:], hfm, ALU.max),
                                  "VD")
                        else:
                            b.inc(e.scalar_tensor_tensor(mac[:], hfm, -1.0, mac[:],
                                                         ALU.mult, ALU.max), "VD")
                            b.inc(e.tensor_tensor(mac[:], mac[:], hfm, ALU.max),
                                  "VD")
                blk.vector(vv)

                def ss(e):
                    for m in range(KI):
                        e.wait_ge(b.sem["VD"], base["VD"] + 3 * m + 1)
                        if m >= 2:
                            e.wait_ge(b.sem["TD"], base["TD"] + m - 1)
                        b.inc(e.activation(hb[m % 2][:], hs[m % 2][:],
                                           AF.Identity, bias=CV("lnb", m),
                                           scale=CV("lng", m)), "SD")
                        b.inc(e.activation(sq[m % 2][:], hb[m % 2][:], AF.Square),
                              "SD")
                blk.scalar(ss)

                def mm(e):
                    for m in range(KI):
                        e.wait_ge(b.sem["SD"], base["SD"] + 2 * m + 2)
                        for n in range(NS2):
                            i = e.matmul(pt[4 + n][0:1, :], onescol,
                                         sq[m % 2][:, 512 * n:512 * (n + 1)],
                                         start=(m == 0), stop=(m == KI - 1))
                        b.inc(i, "TD")
                blk.tensor(mm)
                b.barrier(blk)
                quant_rows(NI, mac, rowb, f2b, ff2b[:, t0:t0 + TL], t0,
                           "sw2", "f2", "ff2", 4)

            for qtr in range(2):
                q0 = t0 + (TL // 2) * qtr
                QL = TL // 2
                with contextlib.ExitStack() as st_:
                    xq2 = sbuf(st_, [128, KI * QL], BF16)
                    mub = sbuf(st_, [128, QL])
                    rsb = sbuf(st_, [128, QL])
                    f2q = sbuf(st_, [128, QL])
                    ffq = sbuf(st_, [128, QL])
                    hb = [sbuf(st_, [128, QL]) for _ in range(2)]
                    hs = [sbuf(st_, [128, QL]) for _ in range(2)]
                    base = dict(b.c)

                    def ld(e):
                        b.inc(e.dma_start(out=mub[:], in_=row_ap("muh", q0, q0 + QL)
                                          .broadcast_to([128, QL])), "LD", 16)
                        b.inc(e.dma_start(out=rsb[:], in_=row_ap("rstdh", q0, q0 + QL)
                                          .broadcast_to([128, QL])), "LD", 16)
                        b.inc(e.dma_start(out=f2q[:], in_=row_ap("f2", q0, q0 + QL)
                                          .broadcast_to([128, QL])), "LD", 16)
                        b.inc(e.dma_start(out=ffq[:], in_=row_ap("ff2", q0, q0 + QL)
                                          .broadcast_to([128, QL])), "LD", 16)
                        for m in range(KI):
                            if m >= 2:
                                e.wait_ge(b.sem["VD"], base["VD"] + 4 * (m - 1))
                            b.inc(e.dma_start(out=hb[m % 2][:],
                                              in_=scr["h1"][128 * m:128 * (m + 1),
                                                            q0:q0 + QL]), "LD", 16)
                    blk.sync(ld)

                    def vv(e):
                        for m in range(KI):
                            e.wait_ge(b.sem["LD"], base["LD"] + 16 * (m + 5))
                            if m >= 2:
                                e.wait_ge(b.sem["SD"], base["SD"] + m - 1)
                            e.tensor_tensor(hs[m % 2][:], hb[m % 2][:], mub[:],
                                            ALU.subtract)
                            b.inc(e.tensor_tensor(hs[m % 2][:], hs[m % 2][:],
                                                  rsb[:], ALU.mult), "VD")
                            e.wait_ge(b.sem["SD"], base["SD"] + m + 1)
                            b.inc(e.tensor_tensor(hs[m % 2][:], hb[m % 2][:],
                                                  f2q[:], ALU.mult), "VD")
                            b.inc(e.tensor_scalar(hs[m % 2][:], hs[m % 2][:], RC,
                                                  None, ALU.add), "VD")
                            b.inc(e.tensor_scalar(xq2[:, QL * m:QL * (m + 1)],
                                                  hs[m % 2][:], RC, None,
                                                  ALU.subtract), "VD")
                    blk.vector(vv)

                    def ss(e):
                        for m in range(KI):
                            e.wait_ge(b.sem["VD"], base["VD"] + 4 * m + 1)
                            b.inc(e.activation(hb[m % 2][:], hs[m % 2][:],
                                               AF.Identity, bias=CV("lnb", m),
                                               scale=CV("lng", m)), "SD")
                    blk.scalar(ss)
                    b.barrier(blk)

                    base = dict(b.c)
                    with contextlib.ExitStack() as s4_:
                        w2b = [sbuf(s4_, [128, KI * 128], BF16) for _ in range(2)]
                        xb = [sbuf(s4_, [128, QL]) for _ in range(2)]
                        tb = [sbuf(s4_, [128, QL]) for _ in range(2)]
                        x2b = [sbuf(s4_, [128, QL]) for _ in range(2)]

                        def ld4(e):
                            for m in range(KD):
                                if m >= 2:
                                    e.wait_ge(b.sem["TD"], base["TD"] + m - 1)
                                    e.wait_ge(b.sem["VD"], base["VD"] + 2 * (m - 1))
                                b.inc(e.dma_start(out=w2b[m % 2][:],
                                                  in_=scr["w2q"][m]), "LD", 16)
                                b.inc(e.dma_start(out=xb[m % 2][:],
                                                  in_=scr["x1"][128 * m:128 * (m + 1),
                                                                q0:q0 + QL]), "LD", 16)
                        blk.sync(ld4)

                        def mm4(e):
                            for m in range(KD):
                                e.wait_ge(b.sem["LD"], base["LD"] + 32 * m + 16)
                                if m >= 1:
                                    e.wait_ge(b.sem["VD"], base["VD"] + 2 * m)
                                for k in range(KI):
                                    i = e.matmul(pt[0][:, 0:QL],
                                                 w2b[m % 2][:, 128 * k:128 * (k + 1)],
                                                 xq2[:, QL * k:QL * (k + 1)],
                                                 start=(k == 0), stop=(k == KI - 1))
                                b.inc(i, "TD")
                        blk.tensor(mm4)

                        def vev(e):
                            for m in range(KD):
                                e.wait_ge(b.sem["TD"], base["TD"] + m + 1)
                                e.wait_ge(b.sem["SD"], base["SD"] + m)
                                b.inc(e.tensor_tensor(tb[m % 2][:], pt[0][:, 0:QL],
                                                      ffq[:], ALU.mult), "VD")
                                e.wait_ge(b.sem["SD"], base["SD"] + m + 1)
                                e.wait_ge(b.sem["LD"], base["LD"] + 32 * (m + 1))
                                if m >= 2:
                                    e.wait_ge(b.sem["SP"], base["SP"] + 16 * (m - 1))
                                b.inc(e.tensor_tensor(x2b[m % 2][:], xb[m % 2][:],
                                                      tb[m % 2][:], ALU.add), "VD")
                        blk.vector(vev)

                        def sev(e):
                            for m in range(KD):
                                e.wait_ge(b.sem["VD"], base["VD"] + 2 * m + 1)
                                b.inc(e.activation(tb[m % 2][:], tb[m % 2][:],
                                                   AF.Identity,
                                                   bias=DVC("gbb2", m),
                                                   scale=DVC("gb0", m)), "SD")
                        blk.scalar(sev)

                        def g4(e):
                            for m in range(KD):
                                e.wait_ge(b.sem["VD"], base["VD"] + 2 * m + 2)
                                b.inc(e.dma_start(
                                    out=scr["x2"][128 * m:128 * (m + 1), q0:q0 + QL],
                                    in_=x2b[m % 2][:]), "SP", 16)
                        blk.gpsimd(g4)
                        b.barrier(blk)
        s2st2.close()
    s2()

    hbP = es.enter_context(nc.sbuf_tensor("hbP", [128, 96], F32))
    hbN = es.enter_context(nc.sbuf_tensor("hbN", [128, 96], F32))

    def s3():
        if upto < 3:
            return
        ln_stats(lambda m: scr["x2"][128 * m:128 * (m + 1), :], KD, "mu3", "rstd3")
        hinb6 = es.enter_context(nc.sbuf_tensor("hinb6", [128, 96], F32))

        for half in range(2):
            t0 = TL * half
            with contextlib.ExitStack() as st_:
                h3 = sbuf(st_, [128, KD * TL])
                mub = sbuf(st_, [128, TL])
                rsb = sbuf(st_, [128, TL])
                xb = [sbuf(st_, [128, TL]) for _ in range(2)]
                hs = [sbuf(st_, [128, TL]) for _ in range(2)]
                wib = [sbuf(st_, [128, KD * 128]) for _ in range(2)]
                hob = [sbuf(st_, [128, TL]) for _ in range(2)]
                base = dict(b.c)

                def ld(e):
                    b.inc(e.dma_start(out=mub[:], in_=row_ap("mu3", t0, t0 + TL)
                                      .broadcast_to([128, TL])), "LD", 16)
                    b.inc(e.dma_start(out=rsb[:], in_=row_ap("rstd3", t0, t0 + TL)
                                      .broadcast_to([128, TL])), "LD", 16)
                    for m in range(KD):
                        if m >= 2:
                            e.wait_ge(b.sem["VD"], base["VD"] + m - 1)
                        b.inc(e.dma_start(out=xb[m % 2][:],
                                          in_=scr["x2"][128 * m:128 * (m + 1),
                                                        t0:t0 + TL]), "LD", 16)
                blk.sync(ld)

                def vv(e):
                    for m in range(KD):
                        e.wait_ge(b.sem["LD"], base["LD"] + 16 * (m + 3))
                        if m >= 2:
                            e.wait_ge(b.sem["SD"], base["SD"] + m - 1)
                        e.tensor_tensor(hs[m % 2][:], xb[m % 2][:], mub[:],
                                        ALU.subtract)
                        b.inc(e.tensor_tensor(hs[m % 2][:], hs[m % 2][:], rsb[:],
                                              ALU.mult), "VD")
                blk.vector(vv)

                def ss(e):
                    for m in range(KD):
                        e.wait_ge(b.sem["VD"], base["VD"] + m + 1)
                        b.inc(e.activation(h3[:, TL * m:TL * (m + 1)], hs[m % 2][:],
                                           AF.Identity, bias=DVC("sa1", m),
                                           scale=DVC("c1pca1", m)), "SD")
                blk.scalar(ss)
                b.barrier(blk)

                base = dict(b.c)

                def ld2(e):
                    for m in range(KD):
                        if m >= 2:
                            e.wait_ge(b.sem["TD"], base["TD"] + m - 1)
                        b.inc(e.dma_start(out=wib[m % 2][:],
                                          in_=wblk_ap("win1T", m, D)), "LD", 16)
                blk.sync(ld2)

                def mm(e):
                    for m in range(KD):
                        e.wait_ge(b.sem["LD"], base["LD"] + 16 * (m + 1))
                        if m >= 1:
                            e.wait_ge(b.sem["SD"], base["SD"] + NS2 * m + (m if half == 0 else 2 * m))
                        for n in range(NS2):
                            for k in range(KD):
                                i = e.matmul(pt[n][:],
                                             wib[m % 2][:, 128 * k:128 * (k + 1)],
                                             h3[:, TL * k + 512 * n:TL * k + 512 * (n + 1)],
                                             start=(k == 0), stop=(k == KD - 1))
                        b.inc(i, "TD")
                blk.tensor(mm)

                nsd = NS2 + (1 if half == 0 else 2)

                def ev(e):
                    for m in range(KD):
                        e.wait_ge(b.sem["TD"], base["TD"] + m + 1)
                        if m >= 2:
                            e.wait_ge(b.sem["SP"], base["SP"] + 16 * (m - 1))
                        for n in range(NS2):
                            b.inc(e.activation(hob[m % 2][:, 512 * n:512 * (n + 1)],
                                               pt[n][:], AF.Gelu,
                                               bias=CV("bin1", m)), "SD")
                        if half == 0:
                            b.inc(e.copy(hinb6[:, 6 * m:6 * m + 3],
                                         hob[m % 2][:, 0:3]), "SD")
                        else:
                            b.inc(e.copy(hinb6[:, 6 * m + 3:6 * m + 6],
                                         hob[m % 2][:, TL - 3:TL]), "SD")
                            b.inc(e.copy(hinb6[:, 6 * m:6 * m + 1],
                                         hinb6[:, 6 * m:6 * m + 1]), "SD")
                blk.scalar(ev)

                def gs(e):
                    for m in range(KD):
                        e.wait_ge(b.sem["SD"], base["SD"] + nsd * m + NS2)
                        b.inc(e.dma_start(out=scr["hin"][128 * m:128 * (m + 1),
                                                         t0:t0 + TL],
                                          in_=hob[m % 2][:]), "SP", 16)
                blk.gpsimd(gs)
                b.barrier(blk)

        def gx(e):
            b.inc(e.dma_start(out=scr["halo_i"], in_=hinb6[:]), "SP", 16)
            b.w(e, "SP")
            b.inc(e.collective_compute(
                "AllGather", ALU.bypass, replica_groups=[list(range(NCORE))],
                ins=[scr["halo_i"]], outs=[scr["halo_o"]]), "CC")
            b.w(e, "CC")
        blk.gpsimd(gx)

        with contextlib.ExitStack() as st_:
            slab = [sbuf(st_, [128, 96]) for _ in range(2)]
            base = dict(b.c)

            def lx(e):
                b.w(e, "CC")
                for j in range(8):
                    if j >= 2:
                        e.wait_ge(b.sem["VD"], base["VD"] + 2 * (j - 1))
                    b.inc(e.dma_start(out=slab[j % 2][:], in_=scr["halo_o"][j]),
                          "LD", 16)
            blk.sync(lx)

            def vx(e):
                vfence(e, e.memset(hbP[:], 0.0))
                vfence(e, e.memset(hbN[:], 0.0))
                for j in range(8):
                    e.wait_ge(b.sem["LD"], base["LD"] + 16 * (j + 1))
                    vfence(e, e.tensor_scalar(slab[j % 2][:], slab[j % 2][:],
                                              selb[:, j:j + 1], None, ALU.mult))
                    vfence(e, e.tensor_tensor(hbP[:], hbP[:], slab[j % 2][:],
                                              ALU.add))
            blk.vector(vx)
            b.barrier(blk)

        with contextlib.ExitStack() as st_:
            slab = [sbuf(st_, [128, 96]) for _ in range(2)]
            base = dict(b.c)

            def lx2(e):
                for j in range(8):
                    if j >= 2:
                        e.wait_ge(b.sem["VD"], base["VD"] + 2 * (j - 1))
                    b.inc(e.dma_start(out=slab[j % 2][:], in_=scr["halo_o"][j]),
                          "LD", 16)
            blk.sync(lx2)

            def vx2(e):
                for j in range(8):
                    e.wait_ge(b.sem["LD"], base["LD"] + 16 * (j + 1))
                    vfence(e, e.tensor_scalar(slab[j % 2][:], slab[j % 2][:],
                                              selb[:, 8 + j:8 + j + 1], None,
                                              ALU.mult))
                    vfence(e, e.tensor_tensor(hbN[:], hbN[:], slab[j % 2][:],
                                              ALU.add))
            blk.vector(vx2)
            b.barrier(blk)
    s3()

    def s4():
        if upto < 4:
            return
        for dr in ("f", "b"):
            wnm = "wfmT" if dr == "f" else "wbmT"
            xinm, zsnm, dtnm, dxnm = ("xi" + dr, "zs" + dr, "dt" + dr, "dx" + dr)
            with contextlib.ExitStack() as dst_:
                xpA = sbuf(dst_, [128, 32 * 128])
                xpB = sbuf(dst_, [128, 32 * 32])
                dtw = sbuf(dst_, [128, KDI * 128])
                xzb = sbuf(dst_, [128, 32 * 3])

                def ld0(e):
                    b.inc(e.dma_start(out=xpA[:], in_=inp["xproj" + dr + "A"]),
                          "LD", 16)
                    b.inc(e.dma_start(out=xpB[:], in_=inp["xproj" + dr + "B"]),
                          "LD", 16)
                    b.inc(e.dma_start(out=dtw[:], in_=inp["dtw" + dr + "T"]),
                          "LD", 16)
                blk.sync(ld0)
                b.barrier(blk, ("tensor", "vector", "scalar"))

                halves = (0, 1) if dr == "f" else (1, 0)
                for hi, half in enumerate(halves):
                    t0 = TL * half
                    outerb = (half == 0 and dr == "f") or (half == 1 and dr == "b")
                    NTD = [(3 if outerb else 2) if m < 32 else 1 for m in range(64)]
                    tdsum = [0]
                    for m in range(64):
                        tdsum.append(tdsum[-1] + NTD[m])
                    NSD = [(1 if 1 <= m <= 32 else 0) + (3 if m < 32 else 2)
                           for m in range(64)]
                    sdsum = [0]
                    for m in range(64):
                        sdsum.append(sdsum[-1] + NSD[m])
                    sd_evac_done = [sdsum[m] + NSD[m] for m in range(64)]
                    sd_silu_done = [sdsum[m + 1] + 1 if m < 31 else
                                    (sdsum[32] + 1 if m == 31 else 0)
                                    for m in range(32)]
                    vdsum = lambda m: 5 * m

                    with contextlib.ExitStack() as st_:
                        hin = sbuf(st_, [128, KD * TL])
                        hb = [sbuf(st_, [128, TL]) for _ in range(2)]
                        wib = [sbuf(st_, [128, KD * 128]) for _ in range(2)]
                        xzp = [sbuf(st_, [128, TL + 6]) for _ in range(2)]
                        cac = [sbuf(st_, [128, TL]) for _ in range(2)]
                        xib = [sbuf(st_, [128, TL]) for _ in range(2)]
                        zb = [sbuf(st_, [128, TL]) for _ in range(2)]
                        dtrb = sbuf(st_, [128, TL])
                        bcb = sbuf(st_, [32, TL])
                        st2_ = contextlib.ExitStack()
                        base = dict(b.c)

                        def ldh(e):
                            for m in range(KD):
                                if m >= 2:
                                    e.wait_ge(b.sem["SD"], base["SD"] + m - 1)
                                b.inc(e.dma_start(
                                    out=hb[m % 2][:],
                                    in_=scr["hin"][128 * m:128 * (m + 1),
                                                   t0:t0 + TL]), "LD", 16)
                        blk.sync(ldh)

                        def sch(e):
                            for m in range(KD):
                                e.wait_ge(b.sem["LD"], base["LD"] + 16 * (m + 1))
                                b.inc(e.copy(hin[:, TL * m:TL * (m + 1)],
                                             hb[m % 2][:]), "SD")
                        blk.scalar(sch)
                        b.barrier(blk)

                        base = dict(b.c)

                        def ld(e):
                            for m in range(64):
                                if m >= 2:
                                    e.wait_ge(b.sem["TD"], base["TD"] + tdsum[m - 1])
                                b.inc(e.dma_start(out=wib[m % 2][:],
                                                  in_=wblk_ap(wnm, m, D)), "LD", 16)
                        blk.sync(ld)

                        def mm(e):
                            for m in range(64):
                                e.wait_ge(b.sem["LD"], base["LD"] + 16 * (m + 1))
                                if m >= 1:
                                    e.wait_ge(b.sem["SD"],
                                              base["SD"] + sd_evac_done[m - 1])
                                for n in range(NS2):
                                    for k in range(KD):
                                        i = e.matmul(
                                            pt[n][:],
                                            wib[m % 2][:, 128 * k:128 * (k + 1)],
                                            hin[:, TL * k + 512 * n:TL * k + 512 * (n + 1)],
                                            start=(k == 0), stop=(k == KD - 1))
                                b.inc(i, "TD")
                                if m < 32 and outerb:
                                    hsrc = hbP if dr == "f" else hbN
                                    off = 3 if dr == "f" else 0
                                    for k in range(KD):
                                        i = e.matmul(
                                            pt[2][:, 0:3],
                                            wib[m % 2][:, 128 * k:128 * (k + 1)],
                                            hsrc[:, 6 * k + off:6 * k + off + 3],
                                            start=(k == 0), stop=(k == KD - 1))
                                    b.inc(i, "TD")
                                if m < 32:
                                    e.wait_ge(b.sem["SD"],
                                              base["SD"] + sd_silu_done[m])
                                    for n in range(NS2):
                                        e.matmul(pt[4 + n][:],
                                                 xpA[:, 128 * m:128 * (m + 1)],
                                                 xib[m % 2][:, 512 * n:512 * (n + 1)],
                                                 start=(m == 0), stop=(m == 31))
                                        i = e.matmul(pt[6 + n][0:32, :],
                                                     xpB[:, 32 * m:32 * (m + 1)],
                                                     xib[m % 2][:, 512 * n:512 * (n + 1)],
                                                     start=(m == 0), stop=(m == 31))
                                    b.inc(i, "TD")
                        blk.tensor(mm)

                        def ev(e):
                            for m in range(64):
                                if 1 <= m <= 32:
                                    mm1 = m - 1
                                    e.wait_ge(b.sem["VD"], base["VD"] + 5 * mm1 + 4)
                                    if mm1 >= 2:
                                        e.wait_ge(b.sem["TD"],
                                                  base["TD"] + tdsum[mm1 - 1])
                                        e.wait_ge(b.sem["SP"],
                                                  base["SP"] + 16 * (mm1 - 1))
                                    b.inc(e.activation(xib[mm1 % 2][:],
                                                       cac[mm1 % 2][:], AF.Silu,
                                                       bias=CV("cb" + dr, mm1)),
                                          "SD")
                                if m < 32:
                                    e.wait_ge(b.sem["TD"], base["TD"] + tdsum[m] +
                                              (2 if outerb else 1))
                                    if m >= 2:
                                        e.wait_ge(b.sem["VD"], base["VD"] + 5 * (m - 1))
                                    for n in range(NS2):
                                        b.inc(e.copy(
                                            xzp[m % 2][:, 3 + 512 * n:3 + 512 * (n + 1)],
                                            pt[n][:]), "SD")
                                    if outerb:
                                        pd0 = 0 if dr == "f" else 3 + TL
                                        b.inc(e.copy(xzp[m % 2][:, pd0:pd0 + 3],
                                                     pt[2][:, 0:3]), "SD")
                                    else:
                                        pd0 = 0 if dr == "f" else 3 + TL
                                        b.inc(e.copy(xzp[m % 2][:, pd0:pd0 + 3],
                                                     xzb[:, 3 * m:3 * (m + 1)]), "SD")
                                else:
                                    e.wait_ge(b.sem["TD"], base["TD"] + tdsum[m] + 1)
                                    if m >= 34:
                                        e.wait_ge(b.sem["SP"],
                                                  base["SP"] + 16 * (m - 1))
                                    for n in range(NS2):
                                        b.inc(e.activation(
                                            zb[m % 2][:, 512 * n:512 * (n + 1)],
                                            pt[n][:], AF.Silu), "SD")
                        blk.scalar(ev)

                        cw = ["cw" + dr + str(k) for k in range(4)]

                        def vv(e):
                            for m in range(32):
                                e.wait_ge(b.sem["SD"], base["SD"] + sd_evac_done[m])
                                x_ = xzp[m % 2]
                                offs = [0, 1, 2, 3] if dr == "f" else [6, 5, 4, 3]
                                b.inc(e.tensor_scalar(cac[m % 2][:],
                                                      x_[:, offs[0]:offs[0] + TL],
                                                      CV(cw[0], m), None, ALU.mult),
                                      "VD")
                                for k in range(1, 4):
                                    b.inc(e.scalar_tensor_tensor(
                                        cac[m % 2][:], x_[:, offs[k]:offs[k] + TL],
                                        CV(cw[k], m), cac[m % 2][:],
                                        ALU.mult, ALU.add), "VD")
                                if hi == 0:
                                    csrc = (x_[:, TL:TL + 3] if dr == "f"
                                            else x_[:, 3:6])
                                    b.inc(e.tensor_copy(xzb[:, 3 * m:3 * (m + 1)],
                                                        csrc), "VD")
                                else:
                                    b.inc(e.tensor_copy(cac[m % 2][:, 0:1],
                                                        cac[m % 2][:, 0:1]), "VD")
                        blk.vector(vv)

                        def gs(e):
                            for m in range(64):
                                if m < 32:
                                    e.wait_ge(b.sem["SD"],
                                              base["SD"] + sd_silu_done[m])
                                    b.inc(e.dma_start(
                                        out=scr[xinm][128 * m:128 * (m + 1),
                                                      t0:t0 + TL],
                                        in_=xib[m % 2][:]), "SP", 16)
                                else:
                                    e.wait_ge(b.sem["SD"],
                                              base["SD"] + sd_evac_done[m])
                                    b.inc(e.dma_start(
                                        out=scr[zsnm][128 * (m - 32):128 * (m - 31),
                                                      t0:t0 + TL],
                                        in_=zb[m % 2][:]), "SP", 16)
                        blk.gpsimd(gs)
                        b.barrier(blk)

                        base = dict(b.c)

                        def evx(e):
                            for n in range(NS2):
                                b.inc(e.copy(dtrb[:, 512 * n:512 * (n + 1)],
                                             pt[4 + n][:]), "SD")
                                b.inc(e.copy(bcb[0:32, 512 * n:512 * (n + 1)],
                                             pt[6 + n][0:32, :]), "SD")
                        blk.scalar(evx)

                        def gbc(e):
                            b.w(e, "SD")
                            b.inc(e.dma_start(
                                out=scr["bc_i"][0 if dr == "f" else 1][:, t0:t0 + TL],
                                in_=bcb[:]), "SP", 16)
                        blk.gpsimd(gbc)
                        b.barrier(blk, ("tensor", "vector", "sync"))

                        base = dict(b.c)
                        dttmp = [sbuf(st2_, [128, TL]) for _ in range(2)]
                        dtob = [sbuf(st2_, [128, TL]) for _ in range(2)]
                        xirb = [sbuf(st2_, [128, TL]) for _ in range(2)]
                        dxob = [sbuf(st2_, [128, TL]) for _ in range(2)]

                        def ldxi(e):
                            for m2 in range(KDI):
                                if m2 >= 2:
                                    e.wait_ge(b.sem["VD"], base["VD"] + m2 - 1)
                                b.inc(e.dma_start(
                                    out=xirb[m2 % 2][:],
                                    in_=scr[xinm][128 * m2:128 * (m2 + 1),
                                                  t0:t0 + TL]), "LD", 16)
                        blk.sync(ldxi)

                        def mmd(e):
                            for m2 in range(KDI):
                                if m2 >= 2:
                                    e.wait_ge(b.sem["SD"], base["SD"] + 4 * (m2 - 1))
                                po = 2 * (m2 % 2)
                                for n in range(NS2):
                                    i = e.matmul(pt[po + n][:],
                                                 dtw[:, 128 * m2:128 * (m2 + 1)],
                                                 dtrb[:, 512 * n:512 * (n + 1)],
                                                 start=True, stop=True)
                                b.inc(i, "TD")
                        blk.tensor(mmd)

                        def evd(e):
                            for m2 in range(KDI):
                                e.wait_ge(b.sem["TD"], base["TD"] + m2 + 1)
                                if m2 >= 2:
                                    e.wait_ge(b.sem["VD"], base["VD"] + m2 - 1)
                                    e.wait_ge(b.sem["SP"], base["SP"] + 32 * (m2 - 1))
                                po = 2 * (m2 % 2)
                                for n in range(NS2):
                                    b.inc(e.activation(
                                        dttmp[m2 % 2][:, 512 * n:512 * (n + 1)],
                                        pt[po + n][:], AF.Exp,
                                        bias=CV("dtb" + dr, m2)), "SD")
                                for n in range(NS2):
                                    b.inc(e.activation(
                                        dtob[m2 % 2][:, 512 * n:512 * (n + 1)],
                                        dttmp[m2 % 2][:, 512 * n:512 * (n + 1)],
                                        AF.Ln, bias=1.0), "SD")
                        blk.scalar(evd)

                        def vvd(e):
                            for m2 in range(KDI):
                                e.wait_ge(b.sem["SD"], base["SD"] + 4 * m2 + 4)
                                e.wait_ge(b.sem["LD"], base["LD"] + 16 * (m2 + 1))
                                b.inc(e.tensor_tensor(dxob[m2 % 2][:],
                                                      dtob[m2 % 2][:],
                                                      xirb[m2 % 2][:], ALU.mult),
                                      "VD")
                        blk.vector(vvd)

                        def gsd(e):
                            for m2 in range(KDI):
                                e.wait_ge(b.sem["VD"], base["VD"] + m2 + 1)
                                b.inc(e.dma_start(
                                    out=scr[dtnm][128 * m2:128 * (m2 + 1),
                                                  t0:t0 + TL],
                                    in_=dtob[m2 % 2][:]), "SP", 16)
                                b.inc(e.dma_start(
                                    out=scr[dxnm][128 * m2:128 * (m2 + 1),
                                                  t0:t0 + TL],
                                    in_=dxob[m2 % 2][:]), "SP", 16)
                        blk.gpsimd(gsd)
                        b.barrier(blk)
                        st2_.close()
    s4()

    def s567():
        if upto < 5:
            return
        def gx(e):
            for dr in ("f", "b"):
                for snm, onm in (("dt" + dr, "dt_sh" + dr), ("dx" + dr, "dx_sh" + dr)):
                    b.inc(e.collective_compute(
                        "AllToAll", ALU.bypass, replica_groups=[list(range(NCORE))],
                        ins=[scr[snm].rearrange("(j d) t -> j d t", j=8)],
                        outs=[scr[onm]]), "CC")
            b.inc(e.collective_compute(
                "AllGather", ALU.bypass, replica_groups=[list(range(NCORE))],
                ins=[scr["bc_i"]], outs=[scr["bc_o"]]), "CC")
            b.w(e, "CC")
        blk.gpsimd(gx)
        b.barrier(blk)

        for dr in ("f", "b"):
            rev = (dr == "b")
            CL = 1024
            chunks = list(range(16)) if not rev else list(range(15, -1, -1))
            with contextlib.ExitStack() as st_:
                dtt = [sbuf(st_, [128, CL]) for _ in range(4)]
                dxt = [sbuf(st_, [128, CL]) for _ in range(4)]
                Bb = sbuf(st_, [128, CL])
                Cb = sbuf(st_, [128, CL])
                dA = [sbuf(st_, [128, CL]) for _ in range(2)]
                bt = sbuf(st_, [128, CL])
                ht = sbuf(st_, [128, CL])
                yt = sbuf(st_, [128, CL])
                yac = [sbuf(st_, [128, CL]) for _ in range(4)]
                yrv = sbuf(st_, [128, CL])
                stc = [sbuf(st_, [128, DS]) for _ in range(4)]
                base = dict(b.c)
                VPI = 6
                VDC = VPI * 64
                V0 = base["VD"] + 4
                GD0 = base["GD"]

                def vz(e):
                    for dt_ in range(4):
                        vfence(e, e.memset(stc[dt_][:], 0.0))
                blk.vector(vz)

                for ci, c2 in enumerate(chunks):
                    c = c2 // 2
                    o0 = (c2 % 2) * CL
                    def ld(e, ci=ci, c=c, o0=o0):
                        if ci >= 1:
                            e.wait_ge(b.sem["VD"], V0 + VDC * ci)
                            e.wait_ge(b.sem["SD"], base["SD"] + 64 * ci)
                        for dt_ in range(4):
                            b.inc(e.dma_start(
                                out=dtt[dt_][:],
                                in_=scr["dt_sh" + dr][c, 128 * dt_:128 * (dt_ + 1),
                                                      o0:o0 + CL]), "LD", 16)
                            b.inc(e.dma_start(
                                out=dxt[dt_][:],
                                in_=scr["dx_sh" + dr][c, 128 * dt_:128 * (dt_ + 1),
                                                      o0:o0 + CL]), "LD", 16)
                        for s in range(DS):
                            if s >= 1:
                                e.wait_ge(b.sem["VD"], V0 + VDC * ci + 24 * s)
                            b.inc(e.dma_start(
                                out=Bb[:],
                                in_=scr["bc_o"][c, 0 if dr == "f" else 1, s:s + 1,
                                                o0:o0 + CL]
                                .broadcast_to([128, CL])), "LD", 16)
                            b.inc(e.dma_start(
                                out=Cb[:],
                                in_=scr["bc_o"][c, 0 if dr == "f" else 1,
                                                16 + s:17 + s, o0:o0 + CL]
                                .broadcast_to([128, CL])), "LD", 16)
                    blk.sync(ld)

                    def sc(e, ci=ci):
                        for s in range(DS):
                            for dt_ in range(4):
                                g = 64 * ci + 4 * s + dt_
                                e.wait_ge(b.sem["LD"], base["LD"] +
                                          16 * (40 * ci + 8 + 2 * (s + 1)))
                                if g >= 2:
                                    e.wait_ge(b.sem["VD"], V0 + VPI * (g - 2) + 3)
                                src_ap = dtt[dt_][:]
                                if rev:
                                    src_ap = dtt[dt_][:, ::-1]
                                acol = DV["A" + dr] + 16 * dt_ + s
                                b.inc(e.activation(dA[g % 2][:], src_ap, AF.Exp,
                                                   scale=dv[:, acol:acol + 1]), "SD")
                    blk.scalar(sc)

                    def vv(e, ci=ci):
                        for s in range(DS):
                            for dt_ in range(4):
                                g = 64 * ci + 4 * s + dt_
                                e.wait_ge(b.sem["SD"], base["SD"] + g + 1)
                                if s == 0 and dt_ == 0 and ci >= 1:
                                    e.wait_ge(b.sem["SP"], base["SP"] + 64 * ci)
                                dxs = dxt[dt_][:]
                                Bs = Bb[:]
                                Cs = Cb[:]
                                if rev:
                                    dxs = dxt[dt_][:, ::-1]
                                    Bs = Bb[:, ::-1]
                                    Cs = Cb[:, ::-1]
                                b.inc(e.tensor_tensor(bt[:], dxs, Bs, ALU.mult),
                                      "VD")
                                b.inc(e.tensor_tensor_scan(
                                    ht[:], dA[g % 2][:], bt[:],
                                    stc[dt_][:, s:s + 1], ALU.mult, ALU.add), "VD")
                                e.wait_ge(b.sem["VD"], b.c["VD"])
                                b.inc(e.tensor_copy(stc[dt_][:, s:s + 1],
                                                    ht[:, CL - 1:CL]), "VD")
                                b.inc(e.tensor_tensor(yt[:], ht[:], Cs, ALU.mult),
                                      "VD")
                                if s == 0:
                                    b.inc(e.tensor_copy(yac[dt_][:], yt[:]), "VD")
                                else:
                                    b.inc(e.tensor_tensor(yac[dt_][:], yac[dt_][:],
                                                          yt[:], ALU.add), "VD")
                                b.inc(e.tensor_copy(bt[:, 0:1], bt[:, 0:1]), "VD")
                        if rev:
                            for dt_ in range(4):
                                b.inc(e.tensor_copy(yrv[:], yac[dt_][:, ::-1]), "GD")
                                e.wait_ge(b.sem["GD"], b.c["GD"])
                                b.inc(e.tensor_copy(yac[dt_][:], yrv[:]), "GD")
                    blk.vector(vv)

                    def gy(e, ci=ci, c=c, o0=o0):
                        e.wait_ge(b.sem["VD"], V0 + VDC * (ci + 1))
                        if rev:
                            e.wait_ge(b.sem["GD"], GD0 + 8 * (ci + 1))
                        for dt_ in range(4):
                            b.inc(e.dma_start(
                                out=scr["ysh_" + dr][c, 128 * dt_:128 * (dt_ + 1),
                                                     o0:o0 + CL],
                                in_=yac[dt_][:]), "SP", 16)
                    blk.gpsimd(gy)
                b.barrier(blk)

        def gx2(e):
            for dr in ("f", "b"):
                b.inc(e.collective_compute(
                    "AllToAll", ALU.bypass, replica_groups=[list(range(NCORE))],
                    ins=[scr["ysh_" + dr]], outs=[scr["yo_" + dr]]), "CC")
            b.w(e, "CC")
        blk.gpsimd(gx2)
        b.barrier(blk)

        for dr in ("f", "b"):
            with contextlib.ExitStack() as st_:
                yb = [sbuf(st_, [128, T]) for _ in range(2)]
                xb = [sbuf(st_, [128, T]) for _ in range(2)]
                zb = [sbuf(st_, [128, T]) for _ in range(2)]
                y3 = [sbuf(st_, [128, T]) for _ in range(2)]
                base = dict(b.c)

                def ld(e):
                    for m in range(KDI):
                        if m >= 2:
                            e.wait_ge(b.sem["VD"], base["VD"] + 2 * (m - 1))
                        src = m // 4
                        dl = 128 * (m % 4)
                        b.inc(e.dma_start(out=yb[m % 2][:],
                                          in_=scr["yo_" + dr][src, dl:dl + 128, :]),
                              "LD", 16)
                        b.inc(e.dma_start(out=xb[m % 2][:],
                                          in_=scr["xi" + dr][128 * m:128 * (m + 1), :]),
                              "LD", 16)
                        b.inc(e.dma_start(out=zb[m % 2][:],
                                          in_=scr["zs" + dr][128 * m:128 * (m + 1), :]),
                              "LD", 16)
                blk.sync(ld)

                def vv(e):
                    for m in range(KDI):
                        e.wait_ge(b.sem["LD"], base["LD"] + 48 * (m + 1))
                        if m >= 2:
                            e.wait_ge(b.sem["SP"], base["SP"] + 16 * (m - 1))
                        b.inc(e.scalar_tensor_tensor(y3[m % 2][:], xb[m % 2][:],
                                                     CV("D" + dr, m), yb[m % 2][:],
                                                     ALU.mult, ALU.add), "VD")
                        b.inc(e.tensor_tensor(y3[m % 2][:], y3[m % 2][:],
                                              zb[m % 2][:], ALU.mult), "VD")
                blk.vector(vv)

                def gs(e):
                    for m in range(KDI):
                        e.wait_ge(b.sem["VD"], base["VD"] + 2 * (m + 1))
                        b.inc(e.dma_start(out=scr["y3" + dr][128 * m:128 * (m + 1), :],
                                          in_=y3[m % 2][:]), "SP", 16)
                blk.gpsimd(gs)
                b.barrier(blk)
    s567()

    def s8():
        if upto < 8:
            return
        for dr in ("f", "b"):
            onm = "fout" if dr == "f" else "bout"
            for qtr in range(4):
                q0 = 512 * qtr
                QL = 512
                with contextlib.ExitStack() as st_:
                    y3q = sbuf(st_, [128, KDI * QL])
                    yb = [sbuf(st_, [128, QL]) for _ in range(2)]
                    wob = [sbuf(st_, [128, KDI * 128]) for _ in range(2)]
                    fb = [sbuf(st_, [128, QL]) for _ in range(2)]
                    base = dict(b.c)

                    def ldy(e):
                        for m in range(KDI):
                            if m >= 2:
                                e.wait_ge(b.sem["SD"], base["SD"] + m - 1)
                            b.inc(e.dma_start(
                                out=yb[m % 2][:],
                                in_=scr["y3" + dr][128 * m:128 * (m + 1),
                                                   q0:q0 + QL]), "LD", 16)
                    blk.sync(ldy)

                    def scy(e):
                        for m in range(KDI):
                            e.wait_ge(b.sem["LD"], base["LD"] + 16 * (m + 1))
                            b.inc(e.copy(y3q[:, QL * m:QL * (m + 1)], yb[m % 2][:]),
                                  "SD")
                    blk.scalar(scy)
                    b.barrier(blk)

                    base = dict(b.c)

                    def ld(e):
                        for m in range(KD):
                            if m >= 2:
                                e.wait_ge(b.sem["TD"], base["TD"] + m - 1)
                            b.inc(e.dma_start(out=wob[m % 2][:],
                                              in_=wblk_ap("outw" + dr + "T", m, DI)),
                                  "LD", 16)
                    blk.sync(ld)

                    def mm(e):
                        for m in range(KD):
                            e.wait_ge(b.sem["LD"], base["LD"] + 16 * (m + 1))
                            if m >= 2:
                                e.wait_ge(b.sem["SD"], base["SD"] + m - 1)
                            for k in range(KDI):
                                i = e.matmul(pt[m % 2][:, 0:QL],
                                             wob[m % 2][:, 128 * k:128 * (k + 1)],
                                             y3q[:, QL * k:QL * (k + 1)],
                                             start=(k == 0), stop=(k == KDI - 1))
                            b.inc(i, "TD")
                    blk.tensor(mm)

                    def ev(e):
                        for m in range(KD):
                            e.wait_ge(b.sem["TD"], base["TD"] + m + 1)
                            if m >= 2:
                                e.wait_ge(b.sem["SP"], base["SP"] + 16 * (m - 1))
                            b.inc(e.copy(fb[m % 2][:], pt[m % 2][:, 0:QL]), "SD")
                    blk.scalar(ev)

                    def gs(e):
                        for m in range(KD):
                            e.wait_ge(b.sem["SD"], base["SD"] + m + 1)
                            b.inc(e.dma_start(
                                out=scr[onm][128 * m:128 * (m + 1), q0:q0 + QL],
                                in_=fb[m % 2][:]), "SP", 16)
                    blk.gpsimd(gs)
                    b.barrier(blk)
    s8()

    def s9():
        if upto < 9 or upto in (98,):
            return
        with contextlib.ExitStack() as st_:
            fbv = [sbuf(st_, [128, T]) for _ in range(2)]
            bbv = [sbuf(st_, [128, T]) for _ in range(2)]
            hbv = [sbuf(st_, [128, T]) for _ in range(2)]
            rb = [sbuf(st_, [128, T]) for _ in range(2)]
            sq = [sbuf(st_, [128, T]) for _ in range(2)]
            base = dict(b.c)

            def ld(e):
                for m in range(KD):
                    if m >= 2:
                        e.wait_ge(b.sem["VD"], base["VD"] + 2 * (m - 1))
                    for snm, bb in (("fout", fbv), ("bout", bbv), ("hin", hbv)):
                        b.inc(e.dma_start(out=bb[m % 2][:],
                                          in_=scr[snm][128 * m:128 * (m + 1), :]),
                              "LD", 16)
            blk.sync(ld)

            def vv(e):
                for m in range(KD):
                    e.wait_ge(b.sem["LD"], base["LD"] + 48 * (m + 1))
                    if m >= 2:
                        e.wait_ge(b.sem["SP"], base["SP"] + 16 * (m - 1))
                        e.wait_ge(b.sem["SD"], base["SD"] + m - 1)
                        e.wait_ge(b.sem["TD"], base["TD"] + m - 1)
                    b.inc(e.tensor_tensor(rb[m % 2][:], fbv[m % 2][:], bbv[m % 2][:],
                                          ALU.add), "VD")
                    b.inc(e.tensor_tensor(rb[m % 2][:], rb[m % 2][:], hbv[m % 2][:],
                                          ALU.add), "VD")
            blk.vector(vv)

            def ss(e):
                for m in range(KD):
                    e.wait_ge(b.sem["VD"], base["VD"] + 2 * (m + 1))
                    if m >= 2:
                        e.wait_ge(b.sem["TD"], base["TD"] + m - 1)
                    b.inc(e.activation(sq[m % 2][:], rb[m % 2][:], AF.Square), "SD")
            blk.scalar(ss)

            def mm(e):
                for m in range(KD):
                    e.wait_ge(b.sem["SD"], base["SD"] + m + 1)
                    for n in range(NSEG):
                        e.matmul(pt[n][0:1, :], onescol,
                                 rb[m % 2][:, 512 * n:512 * (n + 1)],
                                 start=(m == 0), stop=(m == KD - 1))
                        i = e.matmul(pt[4 + n][0:1, :], onescol,
                                     sq[m % 2][:, 512 * n:512 * (n + 1)],
                                     start=(m == 0), stop=(m == KD - 1))
                    b.inc(i, "TD")
            blk.tensor(mm)

            def gs(e):
                for m in range(KD):
                    e.wait_ge(b.sem["VD"], base["VD"] + 2 * (m + 1))
                    b.inc(e.dma_start(out=scr["x3"][128 * m:128 * (m + 1), :],
                                      in_=rb[m % 2][:]), "SP", 16)
            blk.gpsimd(gs)
            b.barrier(blk)

            rr = sbuf(st_, [1, 3 * T])

            def evs(e):
                for n in range(NSEG):
                    e.mul(rr[0:1, 512 * n:512 * (n + 1)], pt[n][0:1, :], 1.0 / D)
                    e.mul(rr[0:1, T + 512 * n:T + 512 * (n + 1)], pt[4 + n][0:1, :],
                          1.0 / D)
                b.inc(e.activation(rr[0:1, 2 * T:3 * T], rr[0:1, 0:T], AF.Square),
                      "SD")
            blk.scalar(evs)

            def evv(e):
                b.w(e, "SD")
                vfence(e, e.tensor_tensor(rr[0:1, T:2 * T], rr[0:1, T:2 * T],
                                          rr[0:1, 2 * T:3 * T], ALU.subtract))
                vfence(e, e.tensor_scalar(rr[0:1, T:2 * T], rr[0:1, T:2 * T], EPS,
                                          None, ALU.add))
                vfence(e, e.reciprocal(rr[0:1, 2 * T:3 * T], rr[0:1, T:2 * T]))
            blk.vector(evv)

            def evs2(e):
                b.w(e, "VD")
                b.inc(e.activation(rr[0:1, T:2 * T], rr[0:1, 2 * T:3 * T], AF.Sqrt),
                      "SD")
            blk.scalar(evs2)

            def gr(e):
                b.w(e, "SD")
                b.inc(e.dma_start(out=row_ap("mur"), in_=rr[0:1, 0:T]), "SP", 16)
                b.inc(e.dma_start(out=row_ap("rstdr"), in_=rr[0:1, T:2 * T]),
                      "SP", 16)
            blk.gpsimd(gr)
            b.barrier(blk)
        if upto == 9:
            return

        for half in range(2):
            t0 = TL * half
            with contextlib.ExitStack() as st_:
                rl = sbuf(st_, [128, KD * TL])
                mub = sbuf(st_, [128, TL])
                rsb = sbuf(st_, [128, TL])
                xb = [sbuf(st_, [128, TL]) for _ in range(2)]
                hs = [sbuf(st_, [128, TL]) for _ in range(2)]
                wib = [sbuf(st_, [128, KD * 128]) for _ in range(2)]
                tb = [sbuf(st_, [128, TL]) for _ in range(2)]
                x3b = [sbuf(st_, [128, TL]) for _ in range(2)]
                base = dict(b.c)

                def ld(e):
                    b.inc(e.dma_start(out=mub[:], in_=row_ap("mur", t0, t0 + TL)
                                      .broadcast_to([128, TL])), "LD", 16)
                    b.inc(e.dma_start(out=rsb[:], in_=row_ap("rstdr", t0, t0 + TL)
                                      .broadcast_to([128, TL])), "LD", 16)
                    for m in range(KD):
                        if m >= 2:
                            e.wait_ge(b.sem["VD"], base["VD"] + m - 1)
                        b.inc(e.dma_start(out=xb[m % 2][:],
                                          in_=scr["x3"][128 * m:128 * (m + 1),
                                                        t0:t0 + TL]), "LD", 16)
                blk.sync(ld)

                def vv(e):
                    for m in range(KD):
                        e.wait_ge(b.sem["LD"], base["LD"] + 16 * (m + 3))
                        if m >= 2:
                            e.wait_ge(b.sem["SD"], base["SD"] + m - 1)
                        e.tensor_tensor(hs[m % 2][:], xb[m % 2][:], mub[:],
                                        ALU.subtract)
                        b.inc(e.tensor_tensor(hs[m % 2][:], hs[m % 2][:], rsb[:],
                                              ALU.mult), "VD")
                blk.vector(vv)

                def ss(e):
                    for m in range(KD):
                        e.wait_ge(b.sem["VD"], base["VD"] + m + 1)
                        b.inc(e.activation(rl[:, TL * m:TL * (m + 1)], hs[m % 2][:],
                                           AF.Identity, bias=CV("lnbr", m),
                                           scale=CV("lngr", m)), "SD")
                blk.scalar(ss)
                b.barrier(blk)

                base = dict(b.c)

                def ld2(e):
                    for m in range(KD):
                        if m >= 2:
                            e.wait_ge(b.sem["TD"], base["TD"] + m - 1)
                            e.wait_ge(b.sem["VD"], base["VD"] + m - 1)
                        b.inc(e.dma_start(out=wib[m % 2][:],
                                          in_=wblk_ap("wout1T", m, D)), "LD", 16)
                        b.inc(e.dma_start(out=xb[m % 2][:],
                                          in_=scr["x2"][128 * m:128 * (m + 1),
                                                        t0:t0 + TL]), "LD", 16)
                blk.sync(ld2)

                def mm2(e):
                    for m in range(KD):
                        e.wait_ge(b.sem["LD"], base["LD"] + 32 * m + 16)
                        if m >= 1:
                            e.wait_ge(b.sem["SD"], base["SD"] + NS2 * m)
                        for n in range(NS2):
                            for k in range(KD):
                                i = e.matmul(pt[n][:],
                                             wib[m % 2][:, 128 * k:128 * (k + 1)],
                                             rl[:, TL * k + 512 * n:TL * k + 512 * (n + 1)],
                                             start=(k == 0), stop=(k == KD - 1))
                        b.inc(i, "TD")
                blk.tensor(mm2)

                def ev2(e):
                    for m in range(KD):
                        e.wait_ge(b.sem["TD"], base["TD"] + m + 1)
                        if m >= 2:
                            e.wait_ge(b.sem["VD"], base["VD"] + m - 1)
                        for n in range(NS2):
                            b.inc(e.activation(tb[m % 2][:, 512 * n:512 * (n + 1)],
                                               pt[n][:], AF.Identity,
                                               bias=DVC("gabout1", m),
                                               scale=DVC("ga1", m)), "SD")
                blk.scalar(ev2)

                def v2(e):
                    for m in range(KD):
                        e.wait_ge(b.sem["SD"], base["SD"] + NS2 * (m + 1))
                        e.wait_ge(b.sem["LD"], base["LD"] + 32 * (m + 1))
                        if m >= 2:
                            e.wait_ge(b.sem["SP"], base["SP"] + 16 * (m - 1))
                        b.inc(e.tensor_tensor(x3b[m % 2][:], xb[m % 2][:],
                                              tb[m % 2][:], ALU.add), "VD")
                blk.vector(v2)

                def g2(e):
                    for m in range(KD):
                        e.wait_ge(b.sem["VD"], base["VD"] + m + 1)
                        b.inc(e.dma_start(out=scr["x1"][128 * m:128 * (m + 1),
                                                        t0:t0 + TL],
                                          in_=x3b[m % 2][:]), "SP", 16)
                blk.gpsimd(g2)
                b.barrier(blk)

        if upto == 10:
            return
        ln_stats(lambda m: scr["x1"][128 * m:128 * (m + 1), :], KD, "mu4", "rstd4")

        acol = es.enter_context(nc.sbuf_tensor("acol", [128, KD], F32))
        galf = es.enter_context(nc.sbuf_tensor("galf", [128, KD], F32))
        with contextlib.ExitStack() as st_:
            wb = [sbuf(st_, [128, D]) for _ in range(2)]
            base = dict(b.c)

            def ld(e):
                for m in range(KD):
                    if m >= 2:
                        e.wait_ge(b.sem["VD"], base["VD"] + m - 1)
                    b.inc(e.dma_start(out=wb[m % 2][:],
                                      in_=inp["wffnN"][128 * m:128 * (m + 1), :]),
                          "LD", 16)
            blk.sync(ld)

            def vv(e):
                for m in range(KD):
                    e.wait_ge(b.sem["LD"], base["LD"] + 16 * (m + 1))
                    b.inc(e.reduce_sum(acol[:, m:m + 1], wb[m % 2][:], axis=AX,
                                       apply_absolute_value=True), "VD")
                e.wait_ge(b.sem["VD"], b.c["VD"])
                vfence(e, e.tensor_scalar(acol[:], acol[:], 1.0 / D, None, ALU.mult))
                vfence(e, e.tensor_tensor(galf[:], acol[:],
                                          dv[:, DV["gb1"]:DV["gb1"] + KD], ALU.mult))
            blk.vector(vv)
            b.barrier(blk)

        if upto == 11:
            return
        for half in range(2):
            t0 = TL * half
            with contextlib.ExitStack() as st_:
                h4 = sbuf(st_, [128, KD * TL])
                mub = sbuf(st_, [128, TL])
                rsb = sbuf(st_, [128, TL])
                xb = [sbuf(st_, [128, TL]) for _ in range(2)]
                hs = [sbuf(st_, [128, TL]) for _ in range(2)]
                wib = [sbuf(st_, [128, KD * 128]) for _ in range(2)]
                sgn = [sbuf(st_, [128, KD * 128]) for _ in range(2)]
                tb = [sbuf(st_, [128, TL]) for _ in range(2)]
                x4b = [sbuf(st_, [128, TL]) for _ in range(2)]
                base = dict(b.c)

                def ld(e):
                    b.inc(e.dma_start(out=mub[:], in_=row_ap("mu4", t0, t0 + TL)
                                      .broadcast_to([128, TL])), "LD", 16)
                    b.inc(e.dma_start(out=rsb[:], in_=row_ap("rstd4", t0, t0 + TL)
                                      .broadcast_to([128, TL])), "LD", 16)
                    for m in range(KD):
                        if m >= 2:
                            e.wait_ge(b.sem["VD"], base["VD"] + m - 1)
                        b.inc(e.dma_start(out=xb[m % 2][:],
                                          in_=scr["x1"][128 * m:128 * (m + 1),
                                                        t0:t0 + TL]), "LD", 16)
                blk.sync(ld)

                def vv(e):
                    for m in range(KD):
                        e.wait_ge(b.sem["LD"], base["LD"] + 16 * (m + 3))
                        if m >= 2:
                            e.wait_ge(b.sem["SD"], base["SD"] + m - 1)
                        e.tensor_tensor(hs[m % 2][:], xb[m % 2][:], mub[:],
                                        ALU.subtract)
                        b.inc(e.tensor_tensor(hs[m % 2][:], hs[m % 2][:], rsb[:],
                                              ALU.mult), "VD")
                blk.vector(vv)

                def ss(e):
                    for m in range(KD):
                        e.wait_ge(b.sem["VD"], base["VD"] + m + 1)
                        b.inc(e.activation(h4[:, TL * m:TL * (m + 1)], hs[m % 2][:],
                                           AF.Identity, bias=DVC("sb1", m),
                                           scale=DVC("c1pcb1", m)), "SD")
                blk.scalar(ss)
                b.barrier(blk)

                for m in range(KD):
                    base = dict(b.c)

                    def ld2(e, m=m):
                        if m >= 1:
                            b.w(e, "VD")
                        b.inc(e.dma_start(out=wib[m % 2][:],
                                          in_=wblk_ap("wffnT", m, D)), "LD", 16)
                        b.inc(e.dma_start(out=xb[m % 2][:],
                                          in_=scr["x1"][128 * m:128 * (m + 1),
                                                        t0:t0 + TL]), "LD", 16)
                    blk.sync(ld2)

                    def sg(e, m=m):
                        b.w(e, "LD")
                        b.inc(e.sign(sgn[m % 2][:], wib[m % 2][:]), "SD")
                    blk.scalar(sg)

                    def mm2(e, m=m):
                        b.w(e, "SD")
                        b.w(e, "VD")
                        for n in range(NS2):
                            for k in range(KD):
                                i = e.matmul(pt[n][:],
                                             sgn[m % 2][:, 128 * k:128 * (k + 1)],
                                             h4[:, TL * k + 512 * n:TL * k + 512 * (n + 1)],
                                             start=(k == 0), stop=(k == KD - 1))
                        b.inc(i, "TD")
                    blk.tensor(mm2)

                    def ev2(e, m=m):
                        b.w(e, "TD")
                        for n in range(NS2):
                            b.inc(e.activation(tb[m % 2][:, 512 * n:512 * (n + 1)],
                                               pt[n][:], AF.Identity,
                                               bias=DVC("gbbffn", m),
                                               scale=galf[:, m:m + 1]), "SD")
                    blk.scalar(ev2)

                    def v2(e, m=m):
                        b.w(e, "SD")
                        b.w(e, "SP")
                        b.inc(e.tensor_tensor(x4b[m % 2][:], xb[m % 2][:],
                                              tb[m % 2][:], ALU.add), "VD")
                    blk.vector(v2)

                    def g2(e, m=m):
                        b.w(e, "VD")
                        b.inc(e.dma_start(out=out_x[128 * m:128 * (m + 1),
                                                    t0:t0 + TL],
                                          in_=x4b[m % 2][:]), "SP", 16)
                    blk.gpsimd(g2)
                b.barrier(blk)
    s9()


    def fin(dbg_src=None):
        if isinstance(dbg_src, str) and dbg_src.startswith("h1"):
            qq = {"h1a": 0, "h1b": 1, "h1c": 2, "h1d": 3}[dbg_src]

            def gh(e):
                for m in range(KD):
                    b.inc(e.dma_start(
                        out=out_x[128 * m:128 * (m + 1), :],
                        in_=scr["h1"][2048 * qq + 128 * m:2048 * qq + 128 * (m + 1), :]),
                        "SP", 16)
            blk.gpsimd(gh)

            def syh(e):
                b.w(e, "SP")
            blk.sync(syh)
            return
        if dbg_src == "ROWS":
            def gr(e):
                b.inc(e.dma_start(out=out_x[0:48, :], in_=scr["rows"][:, :]),
                      "SP", 16)
            blk.gpsimd(gr)

            def syr(e):
                b.w(e, "SP")
            blk.sync(syr)
            return
        if dbg_src is None:
            def syn(e):
                b.w(e, "SP")
            blk.sync(syn)
            return
        if dbg_src == "DV":
            def gdv(e):
                b.inc(e.dma_start(out=out_x[0:128, 0:512], in_=dv[:]), "SP", 16)
                b.inc(e.dma_start(out=out_x[0:128, 512:512 + NCV], in_=cv[:]),
                      "SP", 16)
            blk.gpsimd(gdv)

            def syf2(e):
                b.w(e, "SP")
            blk.sync(syf2)
            return
        src_t = dbg_src if dbg_src is not None else scr["x1"]

        def gf(e):
            for m in range(KD):
                b.inc(e.dma_start(out=out_x[128 * m:128 * (m + 1), :],
                                  in_=src_t[128 * m:128 * (m + 1), :]), "SP", 16)
        blk.gpsimd(gf)

        def syf(e):
            b.w(e, "SP")
        blk.sync(syf)

    if upto < 99:
        fin(_DBG[0] if _DBG[0] in ("DV", "ROWS", "h1a", "h1b", "h1c", "h1d") else
            (scr[_DBG[0]] if _DBG[0] else
             {1: scr["x1"], 2: scr["x2"], 3: scr["hin"], 9: scr["x3"],
              10: scr["x1"]}.get(upto, scr["x1"])))
    else:
        fin(None)
    return nc


_DBG = [None]


def kernel(**inputs):
    maps, _ = _prep_inputs(inputs)
    with allow_ici_all_to_all():
        nc = _build(_UPTO[0])
        res = run_bass_kernel_spmd(nc, maps, core_ids=list(range(NCORE)))
    outs = [np.asarray(r["out_x"], np.float32).T for r in res.results]
    return np.concatenate(outs, axis=0)[None]


_UPTO = [99]


# version 10 (final)
# speedup vs baseline: 1.0692x; 1.0692x over previous
import math
import contextlib
import numpy as np
import concourse.bass as bass
import concourse.mybir as mybir
from concourse.bass_utils import run_bass_kernel_spmd
from concourse.aot_env import allow_ici_all_to_all

F32 = mybir.dt.float32
BF16 = mybir.dt.bfloat16
AF = mybir.ActivationFunctionType
ALU = mybir.AluOpType
AX = mybir.AxisListType.X

NCORE = 8
T = 2048
TG = 16384
D = 2048
KD = D // 128
NI = 8192
KI = NI // 128
DI = 4096
KDI = DI // 128
DS = 16
DTR = 128
RC = 12582912.0
EPS = 1e-5
NSEG = T // 512

MODE = "f32"


CVTAB = [("bin0", 2048), ("bdec0", 2048), ("dbias0", 2048), ("bgate0", 2048),
         ("bout0", 2048), ("b1", 8192), ("lng", 8192), ("lnb", 8192),
         ("b2", 2048), ("bin1", 2048),
         ("cwf0", 4096), ("cwb0", 4096), ("cwf1", 4096), ("cwb1", 4096),
         ("cwf2", 4096), ("cwb2", 4096), ("cwf3", 4096), ("cwb3", 4096),
         ("cbf", 4096), ("cbb", 4096), ("dtbf", 4096), ("dtbb", 4096),
         ("Df", 4096), ("Db", 4096), ("lngr", 2048), ("lnbr", 2048),
         ("bout1", 2048), ("bffn", 2048), ("ones", 128)]
CM = {}
NCV = 0
for _nm, _n in CVTAB:
    CM[_nm] = NCV
    NCV += _n // 128


def _pp(v, n_ch):
    return np.ascontiguousarray(np.asarray(v, np.float32).reshape(n_ch // 128, 128).T)


def _prep_inputs(I):
    x = np.asarray(I["L0_attn_x"], np.float32)[0]
    tr = lambda a: np.ascontiguousarray(np.asarray(a, np.float32).T)
    cc = np.ascontiguousarray

    com = {}
    com["wl0in"] = cc(np.asarray(I["L0_attn_W_in"], np.float32))
    com["wl0dec"] = cc(np.asarray(I["L0_attn_W_decay"], np.float32))
    com["wl0gate"] = cc(np.asarray(I["L0_attn_W_gate"], np.float32))
    com["wl0out"] = cc(np.asarray(I["L0_attn_W_out"], np.float32))
    com["w1T"] = tr(I["L0_ffn_w1"])
    com["w2T"] = tr(I["L0_ffn_w2"])
    com["wmodT0"] = tr(I["L0_wmod"])
    com["wmodT1"] = tr(I["L1_wmod"])
    com["win1T"] = tr(I["L1_attn_w_in"])
    com["wfmT"] = tr(I["L1_attn_fm_in_w"])
    com["wbmT"] = tr(I["L1_attn_bm_in_w"])
    com["xprojfT"] = tr(I["L1_attn_fm_xproj_w"])
    com["xprojbT"] = tr(I["L1_attn_bm_xproj_w"])
    com["dtwfT"] = tr(I["L1_attn_fm_dt_w"])
    com["dtwbT"] = tr(I["L1_attn_bm_dt_w"])
    com["outwfT"] = tr(I["L1_attn_fm_out_w"])
    com["outwbT"] = tr(I["L1_attn_bm_out_w"])
    com["wout1T"] = tr(I["L1_attn_w_out"])
    com["wffnT"] = tr(I["L1_ffn_weight"])
    com["wffnN"] = cc(np.asarray(I["L1_ffn_weight"], np.float32))

    vals = {
        "bin0": I["L0_attn_b_in"], "bdec0": I["L0_attn_b_decay"],
        "dbias0": I["L0_attn_decay_bias"], "bgate0": I["L0_attn_b_gate"],
        "bout0": I["L0_attn_b_out"], "b1": I["L0_ffn_b1"],
        "lng": I["L0_ffn_ln_g"], "lnb": I["L0_ffn_ln_b"], "b2": I["L0_ffn_b2"],
        "bin1": I["L1_attn_b_in"], "cbf": I["L1_attn_fm_conv_b"],
        "cbb": I["L1_attn_bm_conv_b"], "dtbf": I["L1_attn_fm_dt_b"],
        "dtbb": I["L1_attn_bm_dt_b"], "Df": I["L1_attn_fm_D"],
        "Db": I["L1_attn_bm_D"], "lngr": I["L1_attn_ln_g"],
        "lnbr": I["L1_attn_ln_b"], "bout1": I["L1_attn_b_out"],
        "bffn": I["L1_ffn_bias"], "ones": np.ones(128, np.float32),
    }
    for k in range(4):
        vals[f"cwf{k}"] = np.asarray(I["L1_attn_fm_conv_w"], np.float32)[:, k]
        vals[f"cwb{k}"] = np.asarray(I["L1_attn_bm_conv_w"], np.float32)[:, k]
    cvec = np.concatenate([_pp(vals[nm], n) for nm, n in CVTAB], axis=1)
    colmap = CM

    def blkw(wT, Kt, Mt):
        a = np.asarray(wT, np.float32).reshape(Kt, 128, Mt, 128)
        return np.ascontiguousarray(a.transpose(2, 1, 0, 3).reshape(Mt, 128, Kt * 128))

    com["wl0in_b"] = blkw(com["wl0in"], 16, 16)
    com["wl0dec_b"] = blkw(com["wl0dec"], 16, 16)
    com["wl0gate_b"] = blkw(com["wl0gate"], 16, 16)
    com["wl0out_b"] = blkw(com["wl0out"], 16, 16)
    com["w1Tb"] = blkw(com["w1T"], 16, 64)
    com["w2Tb"] = blkw(com["w2T"], 64, 16)
    com["win1Tb"] = blkw(com["win1T"], 16, 16)
    com["wfmTb"] = blkw(com["wfmT"], 16, 64)
    com["wbmTb"] = blkw(com["wbmT"], 16, 64)
    com["outwfTb"] = blkw(com["outwfT"], 32, 16)
    com["outwbTb"] = blkw(com["outwbT"], 32, 16)
    com["wout1Tb"] = blkw(com["wout1T"], 16, 16)
    com["wffnTb"] = blkw(com["wffnT"], 16, 16)
    for d_, nm in (("f", "xprojfT"), ("b", "xprojbT")):
        xp = com[nm]
        a = xp[:, 0:128].reshape(32, 128, 128)
        com[f"xproj{d_}A"] = np.ascontiguousarray(
            a.transpose(1, 0, 2).reshape(128, 32 * 128))
        bb = xp[:, 128:160].reshape(32, 128, 32)
        com[f"xproj{d_}B"] = np.ascontiguousarray(
            bb.transpose(1, 0, 2).reshape(128, 32 * 32))

    crow = np.stack([np.asarray(I["L0_c"], np.float32)[0],
                     np.asarray(I["L1_c"], np.float32)[0]])
    eye = np.eye(128, dtype=np.float32)

    alog_f = np.asarray(I["L1_attn_fm_A_log"], np.float32)
    alog_b = np.asarray(I["L1_attn_bm_A_log"], np.float32)

    maps = []
    for k in range(NCORE):
        m = dict(com)
        m["cvec"] = cvec
        m["crow"] = crow
        m["eye"] = eye
        m["xT"] = cc(x[T * k:T * (k + 1)].T)
        for nm, al in (("alogf", alog_f), ("alogb", alog_b)):
            sh = al[512 * k:512 * (k + 1)]
            a = np.zeros((128, 64), np.float32)
            for dt_ in range(4):
                a[:, 16 * dt_:16 * (dt_ + 1)] = sh[128 * dt_:128 * (dt_ + 1)]
            m[nm] = a
        sel = np.zeros((128, 24), np.float32)
        if k > 0:
            sel[:, k - 1] = 1.0
        if k < 7:
            sel[:, 8 + (k + 1)] = 1.0
        m["selv"] = sel
        maps.append(m)
    return maps, colmap


class Bld:
    def __init__(self, nc, es):
        self.nc = nc
        self.es = es
        self.sem = {}
        self.c = {}
        for n in ("LD", "SP", "TD", "VD", "SD", "GD", "CC",
                  "WLD", "WFREE", "QD", "QFREE", "EV", "LD2"):
            self.sem[n] = es.enter_context(nc.semaphore(n))
            self.c[n] = 0

    def inc(self, inst, name, k=1):
        inst.then_inc(self.sem[name], k)
        self.c[name] += k
        return inst

    def bump(self, name, k=1):
        self.c[name] += k

    def w(self, eng, name, v=None):
        eng.wait_ge(self.sem[name], self.c[name] if v is None else v)

    def barrier(self, block, engines=("sync", "vector", "scalar", "tensor", "gpsimd")):
        cc = dict(self.c)
        for en in engines:
            def f(e, cc=cc):
                for n, v in cc.items():
                    if v:
                        e.wait_ge(self.sem[n], v)
            getattr(block, en)(f)


def _ceil(a, b):
    return (a + b - 1) // b


def _build(upto=99):
    nc = bass.Bass("TRN2", target_bir_lowering=False)
    nc.container_num_cores = 8
    es = contextlib.ExitStack()
    nc._es = es

    inp = {}
    for nm, shp in (
        ("xT", [D, T]), ("wl0in", [D, D]), ("wl0dec", [D, D]),
        ("wl0gate", [D, D]), ("wl0out", [D, D]), ("w1T", [D, NI]),
        ("w2T", [NI, D]), ("wmodT0", [6 * D, D]), ("wmodT1", [6 * D, D]),
        ("win1T", [D, D]), ("wfmT", [D, 2 * DI]), ("wbmT", [D, 2 * DI]),
        ("xprojfT", [DI, 160]), ("xprojbT", [DI, 160]),
        ("dtwfT", [DTR, DI]), ("dtwbT", [DTR, DI]),
        ("outwfT", [DI, D]), ("outwbT", [DI, D]), ("wout1T", [D, D]),
        ("wffnT", [D, D]), ("wffnN", [D, D]),
        ("cvec", [128, NCV]), ("crow", [2, D]), ("eye", [128, 128]),
        ("alogf", [128, 64]), ("alogb", [128, 64]), ("selv", [128, 24]),
        ("wl0in_b", [16, 128, 2048]), ("wl0dec_b", [16, 128, 2048]),
        ("wl0gate_b", [16, 128, 2048]), ("wl0out_b", [16, 128, 2048]),
        ("w1Tb", [64, 128, 2048]), ("w2Tb", [16, 128, 8192]),
        ("win1Tb", [16, 128, 2048]), ("wfmTb", [64, 128, 2048]),
        ("wbmTb", [64, 128, 2048]), ("outwfTb", [16, 128, 4096]),
        ("outwbTb", [16, 128, 4096]), ("wout1Tb", [16, 128, 2048]),
        ("wffnTb", [16, 128, 2048]),
        ("xprojfA", [128, 32 * 128]), ("xprojfB", [128, 32 * 32]),
        ("xprojbA", [128, 32 * 128]), ("xprojbB", [128, 32 * 32]),
    ):
        inp[nm] = nc.dram_tensor(nm, shp, F32, kind="ExternalInput").ap()
    out_x = nc.dram_tensor("out_x", [D, T], F32, kind="ExternalOutput").ap()

    scr = {}
    for nm, shp, dt_ in (
        ("rows", [48, T], F32),
        ("d0", [D, T], F32), ("u0", [D, T], F32), ("g0", [D, T], F32),
        ("x1", [D, T], F32), ("h1", [NI, T], F32), ("x2", [D, T], F32),
        ("hin", [D, T], F32), ("x3", [D, T], F32),
        ("w2q", [16, 128, 8192], BF16),
        ("xif", [DI, T], F32), ("xib", [DI, T], F32),
        ("zsf", [DI, T], F32), ("zsb", [DI, T], F32),
        ("dtf", [DI, T], F32), ("dtb", [DI, T], F32),
        ("dxf", [DI, T], F32), ("dxb", [DI, T], F32),
        ("y3f", [DI, T], F32), ("y3b", [DI, T], F32),
        ("fout", [D, T], F32), ("bout", [D, T], F32),
        ("carry_i", [2, 128, 16], F32), ("carry_o", [NCORE, 2, 128, 16], F32),
        ("halo_i", [128, 96], F32), ("halo_o", [NCORE, 128, 96], F32),
        ("bc_i", [2, 32, T], F32), ("bc_o", [NCORE, 2, 32, T], F32),
        ("dt_shf", [NCORE, 512, T], F32), ("dx_shf", [NCORE, 512, T], F32),
        ("dt_shb", [NCORE, 512, T], F32), ("dx_shb", [NCORE, 512, T], F32),
        ("ysh_f", [NCORE, 512, T], F32), ("ysh_b", [NCORE, 512, T], F32),
        ("yo_f", [NCORE, 512, T], F32), ("yo_b", [NCORE, 512, T], F32),
    ):
        scr[nm] = nc.dram_tensor(nm, shp, dt_).ap()

    b = Bld(nc, es)
    blk = es.enter_context(nc.Block())
    _sbn = [0]

    def sbuf(stack, shape, dt_=F32):
        _sbn[0] += 1
        return stack.enter_context(nc.sbuf_tensor(f"sb{_sbn[0]}", shape, dt_))

    cv = es.enter_context(nc.sbuf_tensor("cv", [128, NCV], F32))
    dv = es.enter_context(nc.sbuf_tensor("dv", [128, 512], F32))
    eyeb = es.enter_context(nc.sbuf_tensor("eyeb", [128, 128], F32))
    selb = es.enter_context(nc.sbuf_tensor("selb", [128, 24], F32))
    alf = es.enter_context(nc.sbuf_tensor("alf", [128, 64], F32))
    alb = es.enter_context(nc.sbuf_tensor("alb", [128, 64], F32))
    pt = [es.enter_context(nc.psum_tensor(f"pt{i}", [128, 512], F32))
          for i in range(8)]

    DV = {}
    _dvoff = [0]
    for nm, w_ in (("c1pca0", KD), ("sa0", KD), ("ga0", KD), ("c1pcb0", KD),
                   ("sb0", KD), ("gb0", KD), ("c1pca1", KD), ("sa1", KD),
                   ("ga1", KD), ("c1pcb1", KD), ("sb1", KD), ("gb1", KD),
                   ("bdecx", KD), ("gabout0", KD), ("gbb2", KD),
                   ("gabout1", KD), ("gbbffn", KD), ("Af", 64), ("Ab", 64)):
        DV[nm] = _dvoff[0]
        _dvoff[0] += w_
    assert _dvoff[0] <= 512

    def CV(name, i=0):
        return cv[:, CM[name] + i:CM[name] + i + 1]

    def DVC(name, i=0):
        return dv[:, DV[name] + i:DV[name] + i + 1]

    def s0():
        base = dict(b.c)

        def loads(e):
            for nm, dst in (("cvec", cv), ("eye", eyeb), ("selv", selb),
                            ("alogf", alf), ("alogb", alb)):
                b.inc(e.dma_start(out=dst[:], in_=inp[nm]), "LD", 16)
        blk.sync(loads)

        s0st = contextlib.ExitStack()
        scrow = s0st.enter_context(nc.sbuf_tensor("scrow", [1, 2 * D], F32))

        def sc0(e):
            b.w(e, "LD")
            b.inc(e.dma_start(out=scrow[:], in_=inp["crow"].rearrange("a b -> (a b)").unsqueeze(0)), "LD", 16)
            b.w(e, "LD")
            i1 = e.activation(scrow[:], scrow[:], AF.Silu)
            b.inc(i1, "SD")
            for src, off in ((alf, DV["Af"]), (alb, DV["Ab"])):
                i2 = e.activation(dv[:, off:off + 64], src[:], AF.Exp)
                b.inc(i2, "SD")
        blk.scalar(sc0)

        def v0(e):
            b.w(e, "SD")
            for nm in ("Af", "Ab"):
                i = e.tensor_scalar(dv[:, DV[nm]:DV[nm] + 64],
                                    dv[:, DV[nm]:DV[nm] + 64], -1.0, None, ALU.mult)
                b.inc(i, "VD")
        blk.vector(v0)

        def gst(e):
            b.w(e, "SD")
            b.inc(e.dma_start(out=scr["rows"][0:2, :].rearrange("a b -> (a b)").unsqueeze(0), in_=scrow[:]), "SP", 16)
        blk.gpsimd(gst)

        scb = s0st.enter_context(nc.sbuf_tensor("scb", [128, D], F32))
        wmb = [s0st.enter_context(nc.sbuf_tensor(f"wmb{i}", [128, D], F32))
               for i in range(2)]
        ttro = s0st.enter_context(nc.sbuf_tensor("ttro", [128, D], F32))
        modv = [s0st.enter_context(nc.sbuf_tensor(f"modv{i}", [128, 96], F32))
                for i in range(2)]
        base2 = dict(b.c)

        def ld1(e):
            b.w(e, "SP")
            for l in range(2):
                if l == 1:
                    e.wait_ge(b.sem["VD"], base2["VD"] + 96)
                b.inc(e.dma_start(
                    out=scb[:], in_=scr["rows"][l:l + 1, :].broadcast_to([128, D])),
                    "LD", 16)
                for j in range(96):
                    idx = 96 * l + j
                    if idx >= 2:
                        e.wait_ge(b.sem["VD"], base2["VD"] + idx - 1)
                    b.inc(e.dma_start(out=wmb[idx % 2][:],
                                      in_=inp[f"wmodT{l}"][128 * j:128 * (j + 1), :]),
                          "LD", 16)
        blk.sync(ld1)

        def v1(e):
            for l in range(2):
                for j in range(96):
                    idx = 96 * l + j
                    e.wait_ge(b.sem["LD"], base2["LD"] + 16 * (2 + idx + l))
                    e.tensor_tensor(ttro[:], wmb[idx % 2][:], scb[:], ALU.mult)
                    i = e.reduce_sum(modv[l][:, j:j + 1], ttro[:], axis=AX)
                    b.inc(i, "VD")
        blk.vector(v1)

        def v2(e):
            b.w(e, "VD")
            fk = [b.c["VD"]]

            def fence(inst):
                fk[0] += 1
                inst.then_inc(b.sem["VD"], 1)
                b.c["VD"] += 1
                e.wait_ge(b.sem["VD"], fk[0])
            for l in range(2):
                sfx = str(l)
                g = modv[l]
                i = e.tensor_scalar(dv[:, DV["c1pca" + sfx]:DV["c1pca" + sfx] + KD],
                                    g[:, 16:32], 1.0, None, ALU.add)
                i = e.tensor_scalar(dv[:, DV["c1pcb" + sfx]:DV["c1pcb" + sfx] + KD],
                                    g[:, 64:80], 1.0, None, ALU.add)
                i = e.tensor_copy(dv[:, DV["sa" + sfx]:DV["sa" + sfx] + KD], g[:, 0:16])
                i = e.tensor_copy(dv[:, DV["sb" + sfx]:DV["sb" + sfx] + KD], g[:, 48:64])
                for gnm, c0 in (("ga" + sfx, 32), ("gb" + sfx, 80)):
                    gg = g[:, c0:c0 + 16]
                    dd = dv[:, DV[gnm]:DV[gnm] + KD]
                    fence(e.tensor_scalar(ttro[:, 0:16], gg, 0.0, None, ALU.is_ge))
                    fence(e.tensor_scalar(ttro[:, 16:32], ttro[:, 0:16], 2.0, -1.0,
                                          ALU.mult, ALU.add))
                    fence(e.tensor_scalar(ttro[:, 32:48], gg, -1.0, None, ALU.mult))
                    fence(e.tensor_tensor(ttro[:, 48:64], gg, ttro[:, 32:48],
                                          ALU.max))
                    fence(e.tensor_scalar(ttro[:, 64:80], ttro[:, 48:64], 0.25, None,
                                          ALU.max))
                    fence(e.tensor_tensor(dd, ttro[:, 16:32], ttro[:, 64:80],
                                          ALU.mult))
            e.tensor_tensor(dv[:, DV["bdecx"]:DV["bdecx"] + KD],
                            cv[:, CM["bdec0"]:CM["bdec0"] + KD],
                            cv[:, CM["dbias0"]:CM["dbias0"] + KD], ALU.add)
            e.tensor_tensor(dv[:, DV["gabout0"]:DV["gabout0"] + KD],
                            dv[:, DV["ga0"]:DV["ga0"] + KD],
                            cv[:, CM["bout0"]:CM["bout0"] + KD], ALU.mult)
            e.tensor_tensor(dv[:, DV["gbb2"]:DV["gbb2"] + KD],
                            dv[:, DV["gb0"]:DV["gb0"] + KD],
                            cv[:, CM["b2"]:CM["b2"] + KD], ALU.mult)
            e.tensor_tensor(dv[:, DV["gabout1"]:DV["gabout1"] + KD],
                            dv[:, DV["ga1"]:DV["ga1"] + KD],
                            cv[:, CM["bout1"]:CM["bout1"] + KD], ALU.mult)
            i = e.tensor_tensor(dv[:, DV["gbbffn"]:DV["gbbffn"] + KD],
                                dv[:, DV["gb1"]:DV["gb1"] + KD],
                                cv[:, CM["bffn"]:CM["bffn"] + KD], ALU.mult)
            b.inc(i, "VD")
        blk.vector(v2)
        b.barrier(blk)
        s0st.close()
    s0()

    onescol = cv[:, CM["ones"]:CM["ones"] + 1]

    ROWS = {"mu1": 2, "rstd1": 3, "mu2": 4, "rstd2": 5, "r1": 6, "m0": 7,
            "f1": 8, "ff1": 9, "muh": 10, "rstdh": 11, "f2": 12, "ff2": 13,
            "mu3": 14, "rstd3": 15, "mur": 16, "rstdr": 17, "mu4": 18,
            "rstd4": 19, "sw1": 20, "sw2": 21}

    def row_ap(nm, n0=0, n1=T):
        r = ROWS[nm]
        return scr["rows"][r:r + 1, n0:n1]

    def ln_stats(src_ap_fn, nK, mu_nm, rstd_nm, eps=EPS, denom=None):
        dn = denom or (nK * 128)
        base = dict(b.c)
        with contextlib.ExitStack() as st_:
            xb = [sbuf(st_, [128, T], F32)
                  for _ in range(2)]
            xq = [sbuf(st_, [128, T], F32)
                  for _ in range(2)]
            rr = sbuf(st_, [1, 4 * T], F32)

            def ld(e):
                for m in range(nK):
                    if m >= 2:
                        e.wait_ge(b.sem["TD"], base["TD"] + m - 1)
                    b.inc(e.dma_start(out=xb[m % 2][:], in_=src_ap_fn(m)), "LD", 16)
            blk.sync(ld)

            def sq(e):
                for m in range(nK):
                    e.wait_ge(b.sem["LD"], base["LD"] + 16 * (m + 1))
                    if m >= 2:
                        e.wait_ge(b.sem["TD"], base["TD"] + m - 1)
                    b.inc(e.activation(xq[m % 2][:], xb[m % 2][:], AF.Square), "SD")
            blk.scalar(sq)

            def mm(e):
                for m in range(nK):
                    e.wait_ge(b.sem["LD"], base["LD"] + 16 * (m + 1))
                    e.wait_ge(b.sem["SD"], base["SD"] + m + 1)
                    for n in range(NSEG):
                        e.matmul(pt[n][0:1, :], onescol, xb[m % 2][:, 512 * n:512 * (n + 1)],
                                 start=(m == 0), stop=(m == nK - 1))
                        i = e.matmul(pt[4 + n][0:1, :], onescol,
                                     xq[m % 2][:, 512 * n:512 * (n + 1)],
                                     start=(m == 0), stop=(m == nK - 1))
                    b.inc(i, "TD")
            blk.tensor(mm)

            def ev_s(e):
                b.w(e, "TD")
                for n in range(NSEG):
                    e.mul(rr[0:1, 512 * n:512 * (n + 1)], pt[n][0:1, :], 1.0 / dn)
                    e.mul(rr[0:1, T + 512 * n:T + 512 * (n + 1)], pt[4 + n][0:1, :],
                          1.0 / dn)
                b.inc(e.activation(rr[0:1, 2 * T:3 * T], rr[0:1, 0:T], AF.Square),
                      "SD")
            blk.scalar(ev_s)

            def ev_v(e):
                b.w(e, "SD")
                e.tensor_tensor(rr[0:1, T:2 * T], rr[0:1, T:2 * T],
                                rr[0:1, 2 * T:3 * T], ALU.subtract)
                e.tensor_scalar(rr[0:1, T:2 * T], rr[0:1, T:2 * T], eps, None, ALU.add)
                b.inc(e.reciprocal(rr[0:1, 2 * T:3 * T], rr[0:1, T:2 * T]), "VD")
            blk.vector(ev_v)

            def ev_s2(e):
                b.w(e, "VD")
                b.inc(e.activation(rr[0:1, T:2 * T], rr[0:1, 2 * T:3 * T], AF.Sqrt),
                      "SD")
            blk.scalar(ev_s2)

            def gst(e):
                b.w(e, "SD")
                b.inc(e.dma_start(out=row_ap(mu_nm), in_=rr[0:1, 0:T]), "SP", 16)
                b.inc(e.dma_start(out=row_ap(rstd_nm), in_=rr[0:1, T:2 * T]), "SP", 16)
            blk.gpsimd(gst)
            b.barrier(blk, ("sync", "vector", "scalar", "tensor", "gpsimd"))

    def bcast_row(dst, nm, e_sync):
        b.inc(e_sync.dma_start(
            out=dst[:], in_=row_ap(nm).broadcast_to([128, T])), "LD", 16)

    _BLK = {"wl0in": "wl0in_b", "wl0dec": "wl0dec_b", "wl0gate": "wl0gate_b",
            "wl0out": "wl0out_b", "w1T": "w1Tb", "w2T": "w2Tb",
            "win1T": "win1Tb", "wfmT": "wfmTb", "wbmT": "wbmTb",
            "outwfT": "outwfTb", "outwbT": "outwbTb", "wout1T": "wout1Tb",
            "wffnT": "wffnTb"}

    def wblk_ap(w_nm, m, kdim):
        return inp[_BLK[w_nm]][m]

    def s1():
        if upto < 1:
            return
        ln_stats(lambda m: inp["xT"][128 * m:128 * (m + 1), :], KD, "mu1", "rstd1")

        TL = T // 2
        NS2 = TL // 512
        Pl = es.enter_context(nc.sbuf_tensor("Pl", [128, 2 * KD], F32))
        Sb2 = es.enter_context(nc.sbuf_tensor("Sb2", [128, 2 * KD], F32))
        Pb = es.enter_context(nc.sbuf_tensor("Pb", [128, KD], F32))
        Sb = es.enter_context(nc.sbuf_tensor("Sb", [128, KD], F32))
        cmine = es.enter_context(nc.sbuf_tensor("cmine", [128, KD], F32))
        smid = es.enter_context(nc.sbuf_tensor("smid", [128, KD], F32))
        zcol = es.enter_context(nc.sbuf_tensor("zcol", [128, 1], F32))

        def vz(e):
            b.inc(e.memset(zcol[:], 0.0), "VD")
        blk.vector(vz)

        for half in range(2):
            t0 = TL * half
            with contextlib.ExitStack() as st_:
                hL = sbuf(st_, [128, KD * TL])
                mub = sbuf(st_, [128, TL])
                rsb = sbuf(st_, [128, TL])
                xb = [sbuf(st_, [128, TL]) for _ in range(2)]
                hs = [sbuf(st_, [128, TL]) for _ in range(2)]
                base = dict(b.c)

                def ldb(e):
                    b.inc(e.dma_start(out=mub[:], in_=row_ap("mu1", t0, t0 + TL)
                                      .broadcast_to([128, TL])), "LD", 16)
                    b.inc(e.dma_start(out=rsb[:], in_=row_ap("rstd1", t0, t0 + TL)
                                      .broadcast_to([128, TL])), "LD", 16)
                    for m in range(KD):
                        if m >= 2:
                            e.wait_ge(b.sem["VD"], base["VD"] + m - 1)
                        b.inc(e.dma_start(out=xb[m % 2][:],
                                          in_=inp["xT"][128 * m:128 * (m + 1),
                                                        t0:t0 + TL]), "LD", 16)
                blk.sync(ldb)

                def vv(e):
                    for m in range(KD):
                        e.wait_ge(b.sem["LD"], base["LD"] + 16 * (m + 3))
                        if m >= 2:
                            e.wait_ge(b.sem["SD"], base["SD"] + m - 1)
                        e.tensor_tensor(hs[m % 2][:], xb[m % 2][:], mub[:], ALU.subtract)
                        b.inc(e.tensor_tensor(hs[m % 2][:], hs[m % 2][:], rsb[:],
                                              ALU.mult), "VD")
                blk.vector(vv)

                def ss(e):
                    for m in range(KD):
                        e.wait_ge(b.sem["VD"], base["VD"] + m + 1)
                        b.inc(e.activation(hL[:, TL * m:TL * (m + 1)], hs[m % 2][:],
                                           AF.Identity, bias=DVC("sa0", m),
                                           scale=DVC("c1pca0", m)), "SD")
                blk.scalar(ss)
                b.barrier(blk)

                base = dict(b.c)
                with contextlib.ExitStack() as s2_:
                    wib = [sbuf(s2_, [128, KD * 128]) for _ in range(2)]
                    wdb = [sbuf(s2_, [128, KD * 128]) for _ in range(2)]
                    wgb = [sbuf(s2_, [128, KD * 128]) for _ in range(2)]
                    ib = [sbuf(s2_, [128, TL]) for _ in range(2)]
                    db = [sbuf(s2_, [128, TL]) for _ in range(2)]
                    gb = [sbuf(s2_, [128, TL]) for _ in range(2)]
                    ub = [sbuf(s2_, [128, TL]) for _ in range(2)]
                    ts_ = sbuf(s2_, [128, TL])
                    ssc = sbuf(s2_, [128, TL])

                    def ld2(e):
                        for m in range(KD):
                            if m >= 2:
                                e.wait_ge(b.sem["TD"], base["TD"] + 3 * (m - 1))
                            for wn, wb_ in (("wl0in", wib), ("wl0dec", wdb),
                                            ("wl0gate", wgb)):
                                b.inc(e.dma_start(out=wb_[m % 2][:],
                                                  in_=wblk_ap(wn, m, D)), "LD", 16)
                    blk.sync(ld2)

                    def mm2(e):
                        for m in range(KD):
                            e.wait_ge(b.sem["LD"], base["LD"] + 48 * (m + 1))
                            if m >= 1:
                                e.wait_ge(b.sem["SD"], base["SD"] + 7 * m)
                            for w_, b0 in ((wib[m % 2], 0), (wdb[m % 2], NS2)):
                                for n in range(NS2):
                                    for k in range(KD):
                                        i = e.matmul(
                                            pt[b0 + n][:],
                                            w_[:, 128 * k:128 * (k + 1)],
                                            hL[:, TL * k + 512 * n:TL * k + 512 * (n + 1)],
                                            start=(k == 0), stop=(k == KD - 1))
                                b.inc(i, "TD")
                            e.wait_ge(b.sem["SD"], base["SD"] + 7 * m + NS2)
                            for n in range(NS2):
                                for k in range(KD):
                                    i = e.matmul(
                                        pt[n][:],
                                        wgb[m % 2][:, 128 * k:128 * (k + 1)],
                                        hL[:, TL * k + 512 * n:TL * k + 512 * (n + 1)],
                                        start=(k == 0), stop=(k == KD - 1))
                            b.inc(i, "TD")
                    blk.tensor(mm2)

                    def ev2(e):
                        for m in range(KD):
                            if m >= 2:
                                e.wait_ge(b.sem["VD"], base["VD"] + 2 * (m - 1))
                                e.wait_ge(b.sem["SP"], base["SP"] + 48 * (m - 1))
                            e.wait_ge(b.sem["TD"], base["TD"] + 3 * m + 1)
                            for n in range(NS2):
                                b.inc(e.activation(ib[m % 2][:, 512 * n:512 * (n + 1)],
                                                   pt[n][:], AF.Silu,
                                                   bias=CV("bin0", m)), "SD")
                            e.wait_ge(b.sem["TD"], base["TD"] + 3 * m + 2)
                            for n in range(NS2):
                                b.inc(e.activation(db[m % 2][:, 512 * n:512 * (n + 1)],
                                                   pt[NS2 + n][:], AF.Sigmoid,
                                                   bias=DVC("bdecx", m)), "SD")
                            e.wait_ge(b.sem["TD"], base["TD"] + 3 * m + 3)
                            for n in range(NS2):
                                b.inc(e.activation(gb[m % 2][:, 512 * n:512 * (n + 1)],
                                                   pt[n][:], AF.Sigmoid,
                                                   bias=CV("bgate0", m)), "SD")
                            e.wait_ge(b.sem["VD"], base["VD"] + 2 * m)
                            b.inc(e.activation(ssc[:], db[m % 2][:], AF.Ln), "SD")
                    blk.scalar(ev2)

                    def v2(e):
                        for m in range(KD):
                            e.wait_ge(b.sem["SD"], base["SD"] + 7 * m + 2 * NS2)
                            e.tensor_tensor(ts_[:], db[m % 2][:], ib[m % 2][:], ALU.mult)
                            b.inc(e.tensor_tensor(ub[m % 2][:], ib[m % 2][:], ts_[:],
                                                  ALU.subtract), "VD")
                            e.wait_ge(b.sem["SD"], base["SD"] + 7 * m + 7)
                            e.reduce_sum(Pl[:, 2 * m + half:2 * m + half + 1],
                                         ssc[:], axis=AX)
                            e.tensor_tensor_scan(ssc[:], db[m % 2][:], ub[m % 2][:],
                                                 zcol[:], ALU.mult, ALU.add)
                            b.inc(e.tensor_copy(Sb2[:, 2 * m + half:2 * m + half + 1],
                                                ssc[:, TL - 1:TL]), "VD")
                    blk.vector(v2)

                    def g2(e):
                        for m in range(KD):
                            e.wait_ge(b.sem["VD"], base["VD"] + 2 * (m + 1))
                            e.wait_ge(b.sem["SD"], base["SD"] + 7 * m + 7)
                            b.inc(e.dma_start(
                                out=scr["d0"][128 * m:128 * (m + 1), t0:t0 + TL],
                                in_=db[m % 2][:]), "SP", 16)
                            b.inc(e.dma_start(
                                out=scr["u0"][128 * m:128 * (m + 1), t0:t0 + TL],
                                in_=ub[m % 2][:]), "SP", 16)
                            b.inc(e.dma_start(
                                out=scr["g0"][128 * m:128 * (m + 1), t0:t0 + TL],
                                in_=gb[m % 2][:]), "SP", 16)
                    blk.gpsimd(g2)
                    b.barrier(blk)

        def vc(e):
            b.w(e, "SD")
            b.w(e, "VD")
            b.inc(e.tensor_tensor(Pb[:], Pl[:, 0::2], Pl[:, 1::2], ALU.add), "VD")
            e.wait_ge(b.sem["VD"], b.c["VD"])
            b.inc(e.tensor_copy(ssc_carry[:, 0:KD], Pl[:, 1::2]), "VD")
            e.wait_ge(b.sem["VD"], b.c["VD"])
        ssc_carry = es.enter_context(nc.sbuf_tensor("ssc_carry", [128, 2 * KD], F32))
        blk.vector(vc)

        def sc_c(e):
            b.w(e, "VD")
            b.inc(e.activation(Pb[:], Pb[:], AF.Exp), "SD")
            b.inc(e.activation(ssc_carry[:, 0:KD], ssc_carry[:, 0:KD], AF.Exp), "SD")
        blk.scalar(sc_c)

        def vc2(e):
            b.w(e, "SD")
            b.inc(e.tensor_tensor(Sb[:], ssc_carry[:, 0:KD], Sb2[:, 0::2], ALU.mult),
                  "VD")
            e.wait_ge(b.sem["VD"], b.c["VD"])
            b.inc(e.tensor_tensor(Sb[:], Sb[:], Sb2[:, 1::2], ALU.add), "VD")
            e.wait_ge(b.sem["VD"], b.c["VD"])
        blk.vector(vc2)
        b.barrier(blk, ("gpsimd",))

        def gx(e):
            b.inc(e.dma_start(out=scr["carry_i"][0], in_=Pb[:]), "SP", 16)
            b.inc(e.dma_start(out=scr["carry_i"][1], in_=Sb[:]), "SP", 16)
            b.w(e, "SP")
            b.inc(e.collective_compute(
                "AllGather", ALU.bypass, replica_groups=[list(range(NCORE))],
                ins=[scr["carry_i"]], outs=[scr["carry_o"]]), "CC")
            b.w(e, "CC")
        blk.gpsimd(gx)

        psall = es.enter_context(nc.sbuf_tensor("psall", [128, 8 * 2 * KD], F32))

        def lx(e):
            b.w(e, "CC")
            b.inc(e.dma_start(
                out=psall[:],
                in_=scr["carry_o"].rearrange("j two p s -> p j two s")), "LD", 16)
        blk.sync(lx)

        def vx(e):
            b.w(e, "LD")
            fk = [b.c["VD"]]

            def fence(inst):
                fk[0] += 1
                inst.then_inc(b.sem["VD"], 1)
                b.c["VD"] += 1
                e.wait_ge(b.sem["VD"], fk[0])
            fence(e.memset(cmine[:], 0.0))
            fence(e.memset(ssc_carry[:, 0:KD], 0.0))
            for j in range(7):
                Pj = psall[:, (2 * j + 0) * KD:(2 * j + 0) * KD + KD]
                Sj = psall[:, (2 * j + 1) * KD:(2 * j + 1) * KD + KD]
                fence(e.tensor_tensor(ssc_carry[:, KD:2 * KD], Pj,
                                      ssc_carry[:, 0:KD], ALU.mult))
                fence(e.tensor_tensor(ssc_carry[:, 0:KD], Sj,
                                      ssc_carry[:, KD:2 * KD], ALU.add))
                fence(e.tensor_scalar(ssc_carry[:, KD:2 * KD], ssc_carry[:, 0:KD],
                                      selb[:, j:j + 1], None, ALU.mult))
                fence(e.tensor_tensor(cmine[:], cmine[:], ssc_carry[:, KD:2 * KD],
                                      ALU.add))
        blk.vector(vx)
        b.barrier(blk)

        for half in range(2):
            t0 = TL * half
            init_src = cmine if half == 0 else smid
            with contextlib.ExitStack() as s3_:
                yL = sbuf(s3_, [128, KD * TL])
                db = [sbuf(s3_, [128, TL]) for _ in range(2)]
                ub = [sbuf(s3_, [128, TL]) for _ in range(2)]
                gb = [sbuf(s3_, [128, TL]) for _ in range(2)]
                ssc = sbuf(s3_, [128, TL])
                base = dict(b.c)

                def ld3(e):
                    for m in range(KD):
                        if m >= 2:
                            e.wait_ge(b.sem["VD"], base["VD"] + 2 * (m - 1))
                        for snm, bb in (("d0", db), ("u0", ub), ("g0", gb)):
                            b.inc(e.dma_start(
                                out=bb[m % 2][:],
                                in_=scr[snm][128 * m:128 * (m + 1), t0:t0 + TL]),
                                "LD", 16)
                blk.sync(ld3)

                def v3(e):
                    for m in range(KD):
                        e.wait_ge(b.sem["LD"], base["LD"] + 48 * (m + 1))
                        e.tensor_tensor_scan(ssc[:], db[m % 2][:], ub[m % 2][:],
                                             init_src[:, m:m + 1], ALU.mult, ALU.add)
                        e.tensor_copy(smid[:, m:m + 1], ssc[:, TL - 1:TL])
                        b.inc(e.tensor_tensor(yL[:, TL * m:TL * (m + 1)], gb[m % 2][:],
                                              ssc[:], ALU.mult), "VD")
                        b.c["VD"] += 1
                        e.sem_inc(b.sem["VD"], 1)
                blk.vector(v3)
                b.barrier(blk, ("sync", "vector", "tensor", "scalar"))

                base = dict(b.c)
                with contextlib.ExitStack() as s4_:
                    wob = [sbuf(s4_, [128, KD * 128]) for _ in range(2)]
                    xb = [sbuf(s4_, [128, TL]) for _ in range(2)]
                    tb = [sbuf(s4_, [128, TL]) for _ in range(2)]
                    x1b = [sbuf(s4_, [128, TL]) for _ in range(2)]

                    def ld4(e):
                        for m in range(KD):
                            if m >= 2:
                                e.wait_ge(b.sem["TD"], base["TD"] + m - 1)
                                e.wait_ge(b.sem["VD"], base["VD"] + m - 1)
                            b.inc(e.dma_start(out=wob[m % 2][:],
                                              in_=wblk_ap("wl0out", m, D)), "LD", 16)
                            b.inc(e.dma_start(out=xb[m % 2][:],
                                              in_=inp["xT"][128 * m:128 * (m + 1),
                                                            t0:t0 + TL]), "LD", 16)
                    blk.sync(ld4)

                    def mm4(e):
                        for m in range(KD):
                            e.wait_ge(b.sem["LD"], base["LD"] + 32 * m + 16)
                            if m >= 1:
                                e.wait_ge(b.sem["SD"], base["SD"] + NS2 * m)
                            for n in range(NS2):
                                for k in range(KD):
                                    i = e.matmul(
                                        pt[n][:], wob[m % 2][:, 128 * k:128 * (k + 1)],
                                        yL[:, TL * k + 512 * n:TL * k + 512 * (n + 1)],
                                        start=(k == 0), stop=(k == KD - 1))
                            b.inc(i, "TD")
                    blk.tensor(mm4)

                    def ev4(e):
                        for m in range(KD):
                            e.wait_ge(b.sem["TD"], base["TD"] + m + 1)
                            if m >= 2:
                                e.wait_ge(b.sem["VD"], base["VD"] + m - 1)
                            for n in range(NS2):
                                b.inc(e.activation(tb[m % 2][:, 512 * n:512 * (n + 1)],
                                                   pt[n][:], AF.Identity,
                                                   bias=DVC("gabout0", m),
                                                   scale=DVC("ga0", m)), "SD")
                    blk.scalar(ev4)

                    def v4(e):
                        for m in range(KD):
                            e.wait_ge(b.sem["SD"], base["SD"] + NS2 * (m + 1))
                            e.wait_ge(b.sem["LD"], base["LD"] + 32 * (m + 1))
                            if m >= 2:
                                e.wait_ge(b.sem["SP"], base["SP"] + 16 * (m - 1))
                            b.inc(e.tensor_tensor(x1b[m % 2][:], xb[m % 2][:],
                                                  tb[m % 2][:], ALU.add), "VD")
                    blk.vector(v4)

                    def g4(e):
                        for m in range(KD):
                            e.wait_ge(b.sem["VD"], base["VD"] + m + 1)
                            b.inc(e.dma_start(
                                out=scr["x1"][128 * m:128 * (m + 1), t0:t0 + TL],
                                in_=x1b[m % 2][:]), "SP", 16)
                    blk.gpsimd(g4)
                    b.barrier(blk)
    s1()

    def vfence(e, inst):
        b.c["VD"] += 1
        inst.then_inc(b.sem["VD"], 1)
        e.wait_ge(b.sem["VD"], b.c["VD"])

    TL = T // 2
    NS2 = TL // 512

    def wmean_pass(wnm, ntl, wid, rowk):
        with contextlib.ExitStack() as st_:
            wb = [sbuf(st_, [128, wid]) for _ in range(2)]
            wsum = sbuf(st_, [128, ntl + 1])
            rowb = sbuf(st_, [1, 256])
            base = dict(b.c)

            def ld(e):
                for m in range(ntl):
                    if m >= 2:
                        e.wait_ge(b.sem["VD"], base["VD"] + m - 1)
                    b.inc(e.dma_start(out=wb[m % 2][:],
                                      in_=inp[wnm][128 * m:128 * (m + 1), :]),
                          "LD", 16)
            blk.sync(ld)

            def vv(e):
                for m in range(ntl):
                    e.wait_ge(b.sem["LD"], base["LD"] + 16 * (m + 1))
                    b.inc(e.reduce_sum(wsum[:, m:m + 1], wb[m % 2][:], axis=AX,
                                       apply_absolute_value=True), "VD")
                e.wait_ge(b.sem["VD"], b.c["VD"])
                vfence(e, e.reduce_sum(wsum[:, ntl:ntl + 1], wsum[:, 0:ntl],
                                       axis=AX))
            blk.vector(vv)

            def mmx(e):
                b.w(e, "VD")
                b.inc(e.matmul(pt[0][0:1, 0:128], wsum[:, ntl:ntl + 1], eyeb[:],
                               start=True, stop=True), "TD")
            blk.tensor(mmx)

            def vv2(e):
                b.w(e, "TD")
                vfence(e, e.tensor_copy(rowb[0:1, 0:128], pt[0][0:1, 0:128]))
                vfence(e, e.reduce_sum(rowb[0:1, 128:129], rowb[0:1, 0:128], axis=AX))
                vfence(e, e.tensor_scalar(rowb[0:1, 130:131], rowb[0:1, 128:129],
                                          1.0 / (ntl * 128.0 * wid), 1e-5,
                                          ALU.mult, ALU.max))
                vfence(e, e.reciprocal(rowb[0:1, 129:130], rowb[0:1, 130:131]))
            blk.vector(vv2)

            def gst(e):
                b.w(e, "VD")
                b.inc(e.dma_start(out=row_ap(rowk, 0, 2), in_=rowb[0:1, 129:131]),
                      "SP", 16)
            blk.gpsimd(gst)
            b.barrier(blk)

    def pmax_to_row(mac, rownm, t0, tl, base_off=0):
        nq = tl // 128
        with contextlib.ExitStack() as st_:
            mcol = sbuf(st_, [128, nq])
            mrow = sbuf(st_, [nq, 128])
            base = dict(b.c)

            def mmt(e):
                b.w(e, "VD")
                for q in range(nq):
                    if q >= 4:
                        e.wait_ge(b.sem["VD"], base["VD"] + q - 3)
                    b.inc(e.transpose(pt[q % 4][:, 0:128],
                                      mac[:, 128 * q:128 * (q + 1)], eyeb[:]), "TD")
                e.wait_ge(b.sem["VD"], base["VD"] + nq)
                b.inc(e.transpose(pt[4][0:nq, 0:128], mcol[:], eyeb[:]), "TD")
            blk.tensor(mmt)

            def vv(e):
                for q in range(nq):
                    e.wait_ge(b.sem["TD"], base["TD"] + q + 1)
                    b.inc(e.reduce_max(mcol[:, q:q + 1], pt[q % 4][:, 0:128],
                                       axis=AX), "VD")
            blk.vector(vv)

            def sc(e):
                e.wait_ge(b.sem["TD"], base["TD"] + nq + 1)
                b.inc(e.copy(mrow[:], pt[4][0:nq, 0:128]), "SD")
            blk.scalar(sc)

            def gst(e):
                b.w(e, "SD")
                b.inc(e.dma_start(
                    out=row_ap(rownm, t0, t0 + tl).rearrange("p (q j) -> p q j",
                                                             j=128),
                    in_=mrow[:]), "SP", 16)
            blk.gpsimd(gst)
            b.barrier(blk)

    def quant_rows(hfsrc, mac, rowb, f1b, ffdst, t0, swnm, fnm, ffnm, sqpt):
        def sv(e):
            b.w(e, "TD")
            for n in range(NS2):
                vfence(e, e.tensor_scalar(rowb[0:1, 512 * n:512 * (n + 1)],
                                          pt[sqpt + n][0:1, :], 1.0 / hfsrc, EPS,
                                          ALU.mult, ALU.add))
            vfence(e, e.reciprocal(rowb[0:1, TL:2 * TL], rowb[0:1, 0:TL]))
        blk.vector(sv)

        def sc1(e):
            b.w(e, "VD")
            b.inc(e.activation(rowb[0:1, 0:TL], rowb[0:1, TL:2 * TL], AF.Sqrt),
                  "SD")
        blk.scalar(sc1)
        b.barrier(blk, ("vector", "tensor", "gpsimd"))
        pmax_to_row(mac, "m0", t0, TL)
        with contextlib.ExitStack() as st_:
            m0b = sbuf(st_, [1, TL])
            swb = sbuf(st_, [1, 2])

            def ld2(e):
                b.inc(e.dma_start(out=m0b[:], in_=row_ap("m0", t0, t0 + TL)),
                      "LD", 16)
                b.inc(e.dma_start(out=swb[:], in_=row_ap(swnm, 0, 2)), "LD", 16)
            blk.sync(ld2)

            def v5(e):
                b.w(e, "LD")
                rA = rowb[0:1, 0:TL]
                rB = rowb[0:1, TL:2 * TL]
                vfence(e, e.tensor_tensor(rB, rA, m0b[:], ALU.mult))
                vfence(e, e.tensor_scalar(rB, rB, 1e-5, None, ALU.max))
                vfence(e, e.reciprocal(rB, rB))
                vfence(e, e.tensor_scalar(rB, rB, 127.0, None, ALU.mult))
                vfence(e, e.tensor_tensor(rA, rA, rB, ALU.mult))
                vfence(e, e.reciprocal(rB, rB))
                vfence(e, e.tensor_scalar(rB, rB, swb[0:1, 1:2], None,
                                          ALU.mult))
            blk.vector(v5)

            def gst2(e):
                b.w(e, "VD")
                b.inc(e.dma_start(out=row_ap(fnm, t0, t0 + TL),
                                  in_=rowb[0:1, 0:TL]), "SP", 16)
                b.inc(e.dma_start(out=row_ap(ffnm, t0, t0 + TL),
                                  in_=rowb[0:1, TL:2 * TL]), "SP", 16)
            blk.gpsimd(gst2)

            def ld3(e):
                b.w(e, "SP")
                b.inc(e.dma_start(out=f1b[:], in_=row_ap(fnm, t0, t0 + TL)
                                  .broadcast_to([128, TL])), "LD", 16)
                b.inc(e.dma_start(out=ffdst, in_=row_ap(ffnm, t0, t0 + TL)
                                  .broadcast_to([128, TL])), "LD", 16)
            blk.sync(ld3)
            b.barrier(blk, ("vector", "scalar", "tensor"))

    def s2():
        if upto < 2:
            return
        ln_stats(lambda m: scr["x1"][128 * m:128 * (m + 1), :], KD, "mu2", "rstd2")
        wmean_pass("w1T", KD, NI, "sw1")
        wmean_pass("w2T", KI, D, "sw2")

        s2st = contextlib.ExitStack()
        xq1 = s2st.enter_context(nc.sbuf_tensor("xq1", [128, KD * T], BF16))
        ff1b = s2st.enter_context(nc.sbuf_tensor("ff1b", [128, T], F32))

        for half in range(2):
            t0 = TL * half
            with contextlib.ExitStack() as st_:
                hf = sbuf(st_, [128, KD * TL])
                mub = sbuf(st_, [128, TL])
                rsb = sbuf(st_, [128, TL])
                xb = [sbuf(st_, [128, TL]) for _ in range(2)]
                hs = [sbuf(st_, [128, TL]) for _ in range(2)]
                sq = [sbuf(st_, [128, TL]) for _ in range(2)]
                mac = sbuf(st_, [128, TL])
                rowb = sbuf(st_, [1, 2 * TL])
                f1b = sbuf(st_, [128, TL])
                base = dict(b.c)

                def ld(e):
                    b.inc(e.dma_start(out=mub[:], in_=row_ap("mu2", t0, t0 + TL)
                                      .broadcast_to([128, TL])), "LD", 16)
                    b.inc(e.dma_start(out=rsb[:], in_=row_ap("rstd2", t0, t0 + TL)
                                      .broadcast_to([128, TL])), "LD", 16)
                    for m in range(KD):
                        if m >= 2:
                            e.wait_ge(b.sem["VD"], base["VD"] + 3 * (m - 1))
                        b.inc(e.dma_start(out=xb[m % 2][:],
                                          in_=scr["x1"][128 * m:128 * (m + 1),
                                                        t0:t0 + TL]), "LD", 16)
                blk.sync(ld)

                def vv(e):
                    for m in range(KD):
                        e.wait_ge(b.sem["LD"], base["LD"] + 16 * (m + 3))
                        if m >= 2:
                            e.wait_ge(b.sem["SD"], base["SD"] + 2 * (m - 1))
                        e.tensor_tensor(hs[m % 2][:], xb[m % 2][:], mub[:],
                                        ALU.subtract)
                        b.inc(e.tensor_tensor(hs[m % 2][:], hs[m % 2][:], rsb[:],
                                              ALU.mult), "VD")
                        e.wait_ge(b.sem["SD"], base["SD"] + 2 * m + 1)
                        hfm = hf[:, TL * m:TL * (m + 1)]
                        if m == 0:
                            b.inc(e.tensor_scalar(mac[:], hfm, -1.0, None, ALU.mult),
                                  "VD")
                            b.inc(e.tensor_tensor(mac[:], mac[:], hfm, ALU.max),
                                  "VD")
                        else:
                            b.inc(e.scalar_tensor_tensor(mac[:], hfm, -1.0, mac[:],
                                                         ALU.mult, ALU.max), "VD")
                            b.inc(e.tensor_tensor(mac[:], mac[:], hfm, ALU.max),
                                  "VD")
                blk.vector(vv)

                def ss(e):
                    for m in range(KD):
                        e.wait_ge(b.sem["VD"], base["VD"] + 3 * m + 1)
                        if m >= 2:
                            e.wait_ge(b.sem["TD"], base["TD"] + m - 1)
                        b.inc(e.activation(hf[:, TL * m:TL * (m + 1)], hs[m % 2][:],
                                           AF.Identity, bias=DVC("sb0", m),
                                           scale=DVC("c1pcb0", m)), "SD")
                        b.inc(e.activation(sq[m % 2][:], hf[:, TL * m:TL * (m + 1)],
                                           AF.Square), "SD")
                blk.scalar(ss)

                def mm(e):
                    for m in range(KD):
                        e.wait_ge(b.sem["SD"], base["SD"] + 2 * m + 2)
                        for n in range(NS2):
                            i = e.matmul(pt[4 + n][0:1, :], onescol,
                                         sq[m % 2][:, 512 * n:512 * (n + 1)],
                                         start=(m == 0), stop=(m == KD - 1))
                        b.inc(i, "TD")
                blk.tensor(mm)
                b.barrier(blk)

                quant_rows(D, mac, rowb, f1b, ff1b[:, t0:t0 + TL], t0,
                           "sw1", "f1", "ff1", 4)

                def v6(e):
                    b.w(e, "LD")
                    for m in range(KD):
                        hfm = hf[:, TL * m:TL * (m + 1)]
                        dst = xq1[:, T * m + t0:T * m + t0 + TL]
                        e.tensor_tensor(hs[0][:], hfm, f1b[:], ALU.mult)
                        e.tensor_scalar(hs[1][:], hs[0][:], RC, None, ALU.add)
                        b.inc(e.tensor_scalar(dst, hs[1][:], RC, None,
                                              ALU.subtract), "VD")
                blk.vector(v6)
                b.barrier(blk)

        with contextlib.ExitStack() as st_:
            wbl = [sbuf(st_, [128, KD * 128]) for _ in range(2)]
            wq = [sbuf(st_, [128, KD * 128], BF16) for _ in range(2)]
            tqa = sbuf(st_, [128, KD * 128])
            hv = [sbuf(st_, [128, T]) for _ in range(2)]
            hg = [sbuf(st_, [128, T]) for _ in range(2)]
            swc = sbuf(st_, [128, 1])
            base = dict(b.c)

            def ld(e):
                b.inc(e.dma_start(out=swc[:],
                                  in_=row_ap("sw1", 0, 1).broadcast_to([128, 1])),
                      "LD", 16)
                for m in range(KI):
                    if m >= 2:
                        e.wait_ge(b.sem["VD"], base["VD"] + 2 * (m - 2) + 1)
                    b.inc(e.dma_start(out=wbl[m % 2][:], in_=wblk_ap("w1T", m, D)),
                          "LD", 16)
            blk.sync(ld)

            def vv(e):
                for m in range(KI + 1):
                    if m >= 1:
                        jj = m - 1
                        e.wait_ge(b.sem["TD"], base["TD"] + m)
                        if jj >= 2:
                            e.wait_ge(b.sem["SD"], base["SD"] + 8 * (jj - 1))
                            e.wait_ge(b.sem["SP"], base["SP"] + 16 * (jj - 1))
                        po = 4 * (jj % 2)
                        for n in range(NSEG):
                            i = e.tensor_tensor(hv[jj % 2][:, 512 * n:512 * (n + 1)],
                                                pt[po + n][:], ff1b[:, 512 * n:512 * (n + 1)],
                                                ALU.mult)
                        b.inc(i, "VD")
                    if m < KI:
                        e.wait_ge(b.sem["LD"], base["LD"] + 16 * (m + 2))
                        e.tensor_scalar(tqa[:], wbl[m % 2][:], swc[:, 0:1], RC,
                                        ALU.mult, ALU.add)
                        e.tensor_scalar(tqa[:], tqa[:], RC, None, ALU.subtract)
                        e.tensor_scalar(tqa[:], tqa[:], -1.0, 1.0, ALU.max, ALU.min)
                        b.inc(e.tensor_copy(wq[m % 2][:], tqa[:]), "VD")
            blk.vector(vv)

            def mm(e):
                for m in range(KI):
                    e.wait_ge(b.sem["VD"], base["VD"] + 2 * m + 1)
                    po = 4 * (m % 2)
                    for n in range(NSEG):
                        for k in range(KD):
                            i = e.matmul(pt[po + n][:],
                                         wq[m % 2][:, 128 * k:128 * (k + 1)],
                                         xq1[:, T * k + 512 * n:T * k + 512 * (n + 1)],
                                         start=(k == 0), stop=(k == KD - 1))
                    b.inc(i, "TD")
            blk.tensor(mm)

            def ss(e):
                for jj in range(KI):
                    e.wait_ge(b.sem["VD"], base["VD"] + 2 * jj + 2)
                    if jj >= 2:
                        e.wait_ge(b.sem["SP"], base["SP"] + 16 * (jj - 1))
                    for n in range(NSEG):
                        b.inc(e.activation(hg[jj % 2][:, 512 * n:512 * (n + 1)],
                                           hv[jj % 2][:, 512 * n:512 * (n + 1)],
                                           AF.Silu, bias=CV("b1", jj)), "SD")
                    for n in range(NSEG):
                        b.inc(e.activation(hv[jj % 2][:, 512 * n:512 * (n + 1)],
                                           hg[jj % 2][:, 512 * n:512 * (n + 1)],
                                           AF.Gelu), "SD")
            blk.scalar(ss)

            def gs(e):
                for jj in range(KI):
                    e.wait_ge(b.sem["SD"], base["SD"] + 8 * jj + 8)
                    b.inc(e.dma_start(out=scr["h1"][128 * jj:128 * (jj + 1), :],
                                      in_=hv[jj % 2][:]), "SP", 16)
            blk.gpsimd(gs)
            b.barrier(blk)
        s2st.close()

        ln_stats(lambda m: scr["h1"][128 * m:128 * (m + 1), :], KI, "muh", "rstdh")

        with contextlib.ExitStack() as st_:
            wbl = [sbuf(st_, [128, KI * 128]) for _ in range(2)]
            tqa = sbuf(st_, [128, KI * 128])
            wq = [sbuf(st_, [128, KI * 128], BF16) for _ in range(2)]
            swc = sbuf(st_, [128, 1])
            base = dict(b.c)

            def ld(e):
                b.inc(e.dma_start(out=swc[:],
                                  in_=row_ap("sw2", 0, 1).broadcast_to([128, 1])),
                      "LD", 16)
                for m in range(KD):
                    if m >= 2:
                        e.wait_ge(b.sem["VD"], base["VD"] + m - 1)
                    b.inc(e.dma_start(out=wbl[m % 2][:], in_=inp["w2Tb"][m]),
                          "LD", 16)
            blk.sync(ld)

            def vv(e):
                for m in range(KD):
                    e.wait_ge(b.sem["LD"], base["LD"] + 16 * (m + 2))
                    if m >= 2:
                        e.wait_ge(b.sem["SP"], base["SP"] + 16 * (m - 1))
                    e.tensor_scalar(tqa[:], wbl[m % 2][:], swc[:, 0:1], RC,
                                    ALU.mult, ALU.add)
                    e.tensor_scalar(tqa[:], tqa[:], RC, None, ALU.subtract)
                    e.tensor_scalar(tqa[:], tqa[:], -1.0, 1.0, ALU.max, ALU.min)
                    b.inc(e.tensor_copy(wq[m % 2][:], tqa[:]), "VD")
            blk.vector(vv)

            def gs(e):
                for m in range(KD):
                    e.wait_ge(b.sem["VD"], base["VD"] + m + 1)
                    b.inc(e.dma_start(out=scr["w2q"][m], in_=wq[m % 2][:]),
                          "SP", 16)
            blk.gpsimd(gs)
            b.barrier(blk)

        s2st2 = contextlib.ExitStack()
        ff2b = s2st2.enter_context(nc.sbuf_tensor("ff2b", [128, T], F32))
        for half in range(2):
            t0 = TL * half
            with contextlib.ExitStack() as st_:
                mub = sbuf(st_, [128, TL])
                rsb = sbuf(st_, [128, TL])
                hb = [sbuf(st_, [128, TL]) for _ in range(2)]
                hs = [sbuf(st_, [128, TL]) for _ in range(2)]
                sq = [sbuf(st_, [128, TL]) for _ in range(2)]
                mac = sbuf(st_, [128, TL])
                rowb = sbuf(st_, [1, 2 * TL])
                f2b = sbuf(st_, [128, TL])
                base = dict(b.c)

                def ld(e):
                    b.inc(e.dma_start(out=mub[:], in_=row_ap("muh", t0, t0 + TL)
                                      .broadcast_to([128, TL])), "LD", 16)
                    b.inc(e.dma_start(out=rsb[:], in_=row_ap("rstdh", t0, t0 + TL)
                                      .broadcast_to([128, TL])), "LD", 16)
                    for m in range(KI):
                        if m >= 2:
                            e.wait_ge(b.sem["VD"], base["VD"] + 3 * (m - 1))
                        b.inc(e.dma_start(out=hb[m % 2][:],
                                          in_=scr["h1"][128 * m:128 * (m + 1),
                                                        t0:t0 + TL]), "LD", 16)
                blk.sync(ld)

                def vv(e):
                    for m in range(KI):
                        e.wait_ge(b.sem["LD"], base["LD"] + 16 * (m + 3))
                        if m >= 2:
                            e.wait_ge(b.sem["SD"], base["SD"] + 2 * (m - 1))
                        e.tensor_tensor(hs[m % 2][:], hb[m % 2][:], mub[:],
                                        ALU.subtract)
                        b.inc(e.tensor_tensor(hs[m % 2][:], hs[m % 2][:], rsb[:],
                                              ALU.mult), "VD")
                        e.wait_ge(b.sem["SD"], base["SD"] + 2 * m + 1)
                        hfm = hb[m % 2][:]
                        if m == 0:
                            b.inc(e.tensor_scalar(mac[:], hfm, -1.0, None, ALU.mult),
                                  "VD")
                            b.inc(e.tensor_tensor(mac[:], mac[:], hfm, ALU.max),
                                  "VD")
                        else:
                            b.inc(e.scalar_tensor_tensor(mac[:], hfm, -1.0, mac[:],
                                                         ALU.mult, ALU.max), "VD")
                            b.inc(e.tensor_tensor(mac[:], mac[:], hfm, ALU.max),
                                  "VD")
                blk.vector(vv)

                def ss(e):
                    for m in range(KI):
                        e.wait_ge(b.sem["VD"], base["VD"] + 3 * m + 1)
                        if m >= 2:
                            e.wait_ge(b.sem["TD"], base["TD"] + m - 1)
                        b.inc(e.activation(hb[m % 2][:], hs[m % 2][:],
                                           AF.Identity, bias=CV("lnb", m),
                                           scale=CV("lng", m)), "SD")
                        b.inc(e.activation(sq[m % 2][:], hb[m % 2][:], AF.Square),
                              "SD")
                blk.scalar(ss)

                def mm(e):
                    for m in range(KI):
                        e.wait_ge(b.sem["SD"], base["SD"] + 2 * m + 2)
                        for n in range(NS2):
                            i = e.matmul(pt[4 + n][0:1, :], onescol,
                                         sq[m % 2][:, 512 * n:512 * (n + 1)],
                                         start=(m == 0), stop=(m == KI - 1))
                        b.inc(i, "TD")
                blk.tensor(mm)
                b.barrier(blk)
                quant_rows(NI, mac, rowb, f2b, ff2b[:, t0:t0 + TL], t0,
                           "sw2", "f2", "ff2", 4)

            for qtr in range(2):
                q0 = t0 + (TL // 2) * qtr
                QL = TL // 2
                with contextlib.ExitStack() as st_:
                    xq2 = sbuf(st_, [128, KI * QL], BF16)
                    mub = sbuf(st_, [128, QL])
                    rsb = sbuf(st_, [128, QL])
                    f2q = sbuf(st_, [128, QL])
                    ffq = sbuf(st_, [128, QL])
                    hb = [sbuf(st_, [128, QL]) for _ in range(2)]
                    hs = [sbuf(st_, [128, QL]) for _ in range(2)]
                    base = dict(b.c)

                    def ld(e):
                        b.inc(e.dma_start(out=mub[:], in_=row_ap("muh", q0, q0 + QL)
                                          .broadcast_to([128, QL])), "LD", 16)
                        b.inc(e.dma_start(out=rsb[:], in_=row_ap("rstdh", q0, q0 + QL)
                                          .broadcast_to([128, QL])), "LD", 16)
                        b.inc(e.dma_start(out=f2q[:], in_=row_ap("f2", q0, q0 + QL)
                                          .broadcast_to([128, QL])), "LD", 16)
                        b.inc(e.dma_start(out=ffq[:], in_=row_ap("ff2", q0, q0 + QL)
                                          .broadcast_to([128, QL])), "LD", 16)
                        for m in range(KI):
                            if m >= 2:
                                e.wait_ge(b.sem["VD"], base["VD"] + 4 * (m - 1))
                            b.inc(e.dma_start(out=hb[m % 2][:],
                                              in_=scr["h1"][128 * m:128 * (m + 1),
                                                            q0:q0 + QL]), "LD", 16)
                    blk.sync(ld)

                    def vv(e):
                        for m in range(KI):
                            e.wait_ge(b.sem["LD"], base["LD"] + 16 * (m + 5))
                            if m >= 2:
                                e.wait_ge(b.sem["SD"], base["SD"] + m - 1)
                            e.tensor_tensor(hs[m % 2][:], hb[m % 2][:], mub[:],
                                            ALU.subtract)
                            b.inc(e.tensor_tensor(hs[m % 2][:], hs[m % 2][:],
                                                  rsb[:], ALU.mult), "VD")
                            e.wait_ge(b.sem["SD"], base["SD"] + m + 1)
                            b.inc(e.tensor_tensor(hs[m % 2][:], hb[m % 2][:],
                                                  f2q[:], ALU.mult), "VD")
                            b.inc(e.tensor_scalar(hs[m % 2][:], hs[m % 2][:], RC,
                                                  None, ALU.add), "VD")
                            b.inc(e.tensor_scalar(xq2[:, QL * m:QL * (m + 1)],
                                                  hs[m % 2][:], RC, None,
                                                  ALU.subtract), "VD")
                    blk.vector(vv)

                    def ss(e):
                        for m in range(KI):
                            e.wait_ge(b.sem["VD"], base["VD"] + 4 * m + 1)
                            b.inc(e.activation(hb[m % 2][:], hs[m % 2][:],
                                               AF.Identity, bias=CV("lnb", m),
                                               scale=CV("lng", m)), "SD")
                    blk.scalar(ss)
                    b.barrier(blk)

                    base = dict(b.c)
                    with contextlib.ExitStack() as s4_:
                        w2b = [sbuf(s4_, [128, KI * 128], BF16) for _ in range(2)]
                        xb = [sbuf(s4_, [128, QL]) for _ in range(2)]
                        tb = [sbuf(s4_, [128, QL]) for _ in range(2)]
                        x2b = [sbuf(s4_, [128, QL]) for _ in range(2)]

                        def ld4(e):
                            for m in range(KD):
                                if m >= 2:
                                    e.wait_ge(b.sem["TD"], base["TD"] + m - 1)
                                    e.wait_ge(b.sem["VD"], base["VD"] + 2 * (m - 1))
                                b.inc(e.dma_start(out=w2b[m % 2][:],
                                                  in_=scr["w2q"][m]), "LD", 16)
                                b.inc(e.dma_start(out=xb[m % 2][:],
                                                  in_=scr["x1"][128 * m:128 * (m + 1),
                                                                q0:q0 + QL]), "LD", 16)
                        blk.sync(ld4)

                        def mm4(e):
                            for m in range(KD):
                                e.wait_ge(b.sem["LD"], base["LD"] + 32 * m + 16)
                                if m >= 1:
                                    e.wait_ge(b.sem["VD"], base["VD"] + 2 * m)
                                for k in range(KI):
                                    i = e.matmul(pt[0][:, 0:QL],
                                                 w2b[m % 2][:, 128 * k:128 * (k + 1)],
                                                 xq2[:, QL * k:QL * (k + 1)],
                                                 start=(k == 0), stop=(k == KI - 1))
                                b.inc(i, "TD")
                        blk.tensor(mm4)

                        def vev(e):
                            for m in range(KD):
                                e.wait_ge(b.sem["TD"], base["TD"] + m + 1)
                                e.wait_ge(b.sem["SD"], base["SD"] + m)
                                b.inc(e.tensor_tensor(tb[m % 2][:], pt[0][:, 0:QL],
                                                      ffq[:], ALU.mult), "VD")
                                e.wait_ge(b.sem["SD"], base["SD"] + m + 1)
                                e.wait_ge(b.sem["LD"], base["LD"] + 32 * (m + 1))
                                if m >= 2:
                                    e.wait_ge(b.sem["SP"], base["SP"] + 16 * (m - 1))
                                b.inc(e.tensor_tensor(x2b[m % 2][:], xb[m % 2][:],
                                                      tb[m % 2][:], ALU.add), "VD")
                        blk.vector(vev)

                        def sev(e):
                            for m in range(KD):
                                e.wait_ge(b.sem["VD"], base["VD"] + 2 * m + 1)
                                b.inc(e.activation(tb[m % 2][:], tb[m % 2][:],
                                                   AF.Identity,
                                                   bias=DVC("gbb2", m),
                                                   scale=DVC("gb0", m)), "SD")
                        blk.scalar(sev)

                        def g4(e):
                            for m in range(KD):
                                e.wait_ge(b.sem["VD"], base["VD"] + 2 * m + 2)
                                b.inc(e.dma_start(
                                    out=scr["x2"][128 * m:128 * (m + 1), q0:q0 + QL],
                                    in_=x2b[m % 2][:]), "SP", 16)
                        blk.gpsimd(g4)
                        b.barrier(blk)
        s2st2.close()
    s2()

    hbP = es.enter_context(nc.sbuf_tensor("hbP", [128, 96], F32))
    hbN = es.enter_context(nc.sbuf_tensor("hbN", [128, 96], F32))

    def s3():
        if upto < 3:
            return
        ln_stats(lambda m: scr["x2"][128 * m:128 * (m + 1), :], KD, "mu3", "rstd3")
        hinb6 = es.enter_context(nc.sbuf_tensor("hinb6", [128, 96], F32))

        for half in range(2):
            t0 = TL * half
            with contextlib.ExitStack() as st_:
                h3 = sbuf(st_, [128, KD * TL])
                mub = sbuf(st_, [128, TL])
                rsb = sbuf(st_, [128, TL])
                xb = [sbuf(st_, [128, TL]) for _ in range(2)]
                hs = [sbuf(st_, [128, TL]) for _ in range(2)]
                wib = [sbuf(st_, [128, KD * 128]) for _ in range(2)]
                hob = [sbuf(st_, [128, TL]) for _ in range(2)]
                base = dict(b.c)

                def ld(e):
                    b.inc(e.dma_start(out=mub[:], in_=row_ap("mu3", t0, t0 + TL)
                                      .broadcast_to([128, TL])), "LD", 16)
                    b.inc(e.dma_start(out=rsb[:], in_=row_ap("rstd3", t0, t0 + TL)
                                      .broadcast_to([128, TL])), "LD", 16)
                    for m in range(KD):
                        if m >= 2:
                            e.wait_ge(b.sem["VD"], base["VD"] + m - 1)
                        b.inc(e.dma_start(out=xb[m % 2][:],
                                          in_=scr["x2"][128 * m:128 * (m + 1),
                                                        t0:t0 + TL]), "LD", 16)
                blk.sync(ld)

                def vv(e):
                    for m in range(KD):
                        e.wait_ge(b.sem["LD"], base["LD"] + 16 * (m + 3))
                        if m >= 2:
                            e.wait_ge(b.sem["SD"], base["SD"] + m - 1)
                        e.tensor_tensor(hs[m % 2][:], xb[m % 2][:], mub[:],
                                        ALU.subtract)
                        b.inc(e.tensor_tensor(hs[m % 2][:], hs[m % 2][:], rsb[:],
                                              ALU.mult), "VD")
                blk.vector(vv)

                def ss(e):
                    for m in range(KD):
                        e.wait_ge(b.sem["VD"], base["VD"] + m + 1)
                        b.inc(e.activation(h3[:, TL * m:TL * (m + 1)], hs[m % 2][:],
                                           AF.Identity, bias=DVC("sa1", m),
                                           scale=DVC("c1pca1", m)), "SD")
                blk.scalar(ss)
                b.barrier(blk)

                base = dict(b.c)

                def ld2(e):
                    for m in range(KD):
                        if m >= 2:
                            e.wait_ge(b.sem["TD"], base["TD"] + m - 1)
                        b.inc(e.dma_start(out=wib[m % 2][:],
                                          in_=wblk_ap("win1T", m, D)), "LD", 16)
                blk.sync(ld2)

                def mm(e):
                    for m in range(KD):
                        e.wait_ge(b.sem["LD"], base["LD"] + 16 * (m + 1))
                        if m >= 1:
                            e.wait_ge(b.sem["SD"], base["SD"] + NS2 * m + (m if half == 0 else 2 * m))
                        for n in range(NS2):
                            for k in range(KD):
                                i = e.matmul(pt[n][:],
                                             wib[m % 2][:, 128 * k:128 * (k + 1)],
                                             h3[:, TL * k + 512 * n:TL * k + 512 * (n + 1)],
                                             start=(k == 0), stop=(k == KD - 1))
                        b.inc(i, "TD")
                blk.tensor(mm)

                nsd = NS2 + (1 if half == 0 else 2)

                def ev(e):
                    for m in range(KD):
                        e.wait_ge(b.sem["TD"], base["TD"] + m + 1)
                        if m >= 2:
                            e.wait_ge(b.sem["SP"], base["SP"] + 16 * (m - 1))
                        for n in range(NS2):
                            b.inc(e.activation(hob[m % 2][:, 512 * n:512 * (n + 1)],
                                               pt[n][:], AF.Gelu,
                                               bias=CV("bin1", m)), "SD")
                        if half == 0:
                            b.inc(e.copy(hinb6[:, 6 * m:6 * m + 3],
                                         hob[m % 2][:, 0:3]), "SD")
                        else:
                            b.inc(e.copy(hinb6[:, 6 * m + 3:6 * m + 6],
                                         hob[m % 2][:, TL - 3:TL]), "SD")
                            b.inc(e.copy(hinb6[:, 6 * m:6 * m + 1],
                                         hinb6[:, 6 * m:6 * m + 1]), "SD")
                blk.scalar(ev)

                def gs(e):
                    for m in range(KD):
                        e.wait_ge(b.sem["SD"], base["SD"] + nsd * m + NS2)
                        b.inc(e.dma_start(out=scr["hin"][128 * m:128 * (m + 1),
                                                         t0:t0 + TL],
                                          in_=hob[m % 2][:]), "SP", 16)
                blk.gpsimd(gs)
                b.barrier(blk)

        def gx(e):
            b.inc(e.dma_start(out=scr["halo_i"], in_=hinb6[:]), "SP", 16)
            b.w(e, "SP")
            b.inc(e.collective_compute(
                "AllGather", ALU.bypass, replica_groups=[list(range(NCORE))],
                ins=[scr["halo_i"]], outs=[scr["halo_o"]]), "CC")
            b.w(e, "CC")
        blk.gpsimd(gx)

        with contextlib.ExitStack() as st_:
            slab = [sbuf(st_, [128, 96]) for _ in range(2)]
            base = dict(b.c)

            def lx(e):
                b.w(e, "CC")
                for j in range(8):
                    if j >= 2:
                        e.wait_ge(b.sem["VD"], base["VD"] + 2 * (j - 1))
                    b.inc(e.dma_start(out=slab[j % 2][:], in_=scr["halo_o"][j]),
                          "LD", 16)
            blk.sync(lx)

            def vx(e):
                vfence(e, e.memset(hbP[:], 0.0))
                vfence(e, e.memset(hbN[:], 0.0))
                for j in range(8):
                    e.wait_ge(b.sem["LD"], base["LD"] + 16 * (j + 1))
                    vfence(e, e.tensor_scalar(slab[j % 2][:], slab[j % 2][:],
                                              selb[:, j:j + 1], None, ALU.mult))
                    vfence(e, e.tensor_tensor(hbP[:], hbP[:], slab[j % 2][:],
                                              ALU.add))
            blk.vector(vx)
            b.barrier(blk)

        with contextlib.ExitStack() as st_:
            slab = [sbuf(st_, [128, 96]) for _ in range(2)]
            base = dict(b.c)

            def lx2(e):
                for j in range(8):
                    if j >= 2:
                        e.wait_ge(b.sem["VD"], base["VD"] + 2 * (j - 1))
                    b.inc(e.dma_start(out=slab[j % 2][:], in_=scr["halo_o"][j]),
                          "LD", 16)
            blk.sync(lx2)

            def vx2(e):
                for j in range(8):
                    e.wait_ge(b.sem["LD"], base["LD"] + 16 * (j + 1))
                    vfence(e, e.tensor_scalar(slab[j % 2][:], slab[j % 2][:],
                                              selb[:, 8 + j:8 + j + 1], None,
                                              ALU.mult))
                    vfence(e, e.tensor_tensor(hbN[:], hbN[:], slab[j % 2][:],
                                              ALU.add))
            blk.vector(vx2)
            b.barrier(blk)
    s3()

    def s4():
        if upto < 4:
            return
        for dr in ("f", "b"):
            wnm = "wfmT" if dr == "f" else "wbmT"
            xinm, zsnm, dtnm, dxnm = ("xi" + dr, "zs" + dr, "dt" + dr, "dx" + dr)
            with contextlib.ExitStack() as dst_:
                xpA = sbuf(dst_, [128, 32 * 128])
                xpB = sbuf(dst_, [128, 32 * 32])
                dtw = sbuf(dst_, [128, KDI * 128])
                xzb = sbuf(dst_, [128, 32 * 3])

                def ld0(e):
                    b.inc(e.dma_start(out=xpA[:], in_=inp["xproj" + dr + "A"]),
                          "LD", 16)
                    b.inc(e.dma_start(out=xpB[:], in_=inp["xproj" + dr + "B"]),
                          "LD", 16)
                    b.inc(e.dma_start(out=dtw[:], in_=inp["dtw" + dr + "T"]),
                          "LD", 16)
                blk.sync(ld0)
                b.barrier(blk, ("tensor", "vector", "scalar"))

                halves = (0, 1) if dr == "f" else (1, 0)
                for hi, half in enumerate(halves):
                    t0 = TL * half
                    outerb = (half == 0 and dr == "f") or (half == 1 and dr == "b")
                    NTD = [(3 if outerb else 2) if m < 32 else 1 for m in range(64)]
                    tdsum = [0]
                    for m in range(64):
                        tdsum.append(tdsum[-1] + NTD[m])
                    NSD = [(1 if 1 <= m <= 32 else 0) + (3 if m < 32 else 2)
                           for m in range(64)]
                    sdsum = [0]
                    for m in range(64):
                        sdsum.append(sdsum[-1] + NSD[m])
                    sd_evac_done = [sdsum[m] + NSD[m] for m in range(64)]
                    sd_silu_done = [sdsum[m + 1] + 1 if m < 31 else
                                    (sdsum[32] + 1 if m == 31 else 0)
                                    for m in range(32)]
                    vdsum = lambda m: 5 * m

                    with contextlib.ExitStack() as st_:
                        hin = sbuf(st_, [128, KD * TL])
                        hb = [sbuf(st_, [128, TL]) for _ in range(2)]
                        wib = [sbuf(st_, [128, KD * 128]) for _ in range(2)]
                        xzp = [sbuf(st_, [128, TL + 6]) for _ in range(2)]
                        cac = [sbuf(st_, [128, TL]) for _ in range(2)]
                        xib = [sbuf(st_, [128, TL]) for _ in range(2)]
                        zb = [sbuf(st_, [128, TL]) for _ in range(2)]
                        dtrb = sbuf(st_, [128, TL])
                        bcb = sbuf(st_, [32, TL])
                        st2_ = contextlib.ExitStack()
                        base = dict(b.c)

                        def ldh(e):
                            for m in range(KD):
                                if m >= 2:
                                    e.wait_ge(b.sem["SD"], base["SD"] + m - 1)
                                b.inc(e.dma_start(
                                    out=hb[m % 2][:],
                                    in_=scr["hin"][128 * m:128 * (m + 1),
                                                   t0:t0 + TL]), "LD", 16)
                        blk.sync(ldh)

                        def sch(e):
                            for m in range(KD):
                                e.wait_ge(b.sem["LD"], base["LD"] + 16 * (m + 1))
                                b.inc(e.copy(hin[:, TL * m:TL * (m + 1)],
                                             hb[m % 2][:]), "SD")
                        blk.scalar(sch)
                        b.barrier(blk)

                        base = dict(b.c)

                        def ld(e):
                            for m in range(64):
                                if m >= 2:
                                    e.wait_ge(b.sem["TD"], base["TD"] + tdsum[m - 1])
                                b.inc(e.dma_start(out=wib[m % 2][:],
                                                  in_=wblk_ap(wnm, m, D)), "LD", 16)
                        blk.sync(ld)

                        def mm(e):
                            for m in range(64):
                                e.wait_ge(b.sem["LD"], base["LD"] + 16 * (m + 1))
                                if m >= 1:
                                    e.wait_ge(b.sem["SD"],
                                              base["SD"] + sd_evac_done[m - 1])
                                for n in range(NS2):
                                    for k in range(KD):
                                        i = e.matmul(
                                            pt[n][:],
                                            wib[m % 2][:, 128 * k:128 * (k + 1)],
                                            hin[:, TL * k + 512 * n:TL * k + 512 * (n + 1)],
                                            start=(k == 0), stop=(k == KD - 1))
                                b.inc(i, "TD")
                                if m < 32 and outerb:
                                    hsrc = hbP if dr == "f" else hbN
                                    off = 3 if dr == "f" else 0
                                    for k in range(KD):
                                        i = e.matmul(
                                            pt[2][:, 0:3],
                                            wib[m % 2][:, 128 * k:128 * (k + 1)],
                                            hsrc[:, 6 * k + off:6 * k + off + 3],
                                            start=(k == 0), stop=(k == KD - 1))
                                    b.inc(i, "TD")
                                if m < 32:
                                    e.wait_ge(b.sem["SD"],
                                              base["SD"] + sd_silu_done[m])
                                    for n in range(NS2):
                                        e.matmul(pt[4 + n][:],
                                                 xpA[:, 128 * m:128 * (m + 1)],
                                                 xib[m % 2][:, 512 * n:512 * (n + 1)],
                                                 start=(m == 0), stop=(m == 31))
                                        i = e.matmul(pt[6 + n][0:32, :],
                                                     xpB[:, 32 * m:32 * (m + 1)],
                                                     xib[m % 2][:, 512 * n:512 * (n + 1)],
                                                     start=(m == 0), stop=(m == 31))
                                    b.inc(i, "TD")
                        blk.tensor(mm)

                        def ev(e):
                            for m in range(64):
                                if 1 <= m <= 32:
                                    mm1 = m - 1
                                    e.wait_ge(b.sem["VD"], base["VD"] + 5 * mm1 + 4)
                                    if mm1 >= 2:
                                        e.wait_ge(b.sem["TD"],
                                                  base["TD"] + tdsum[mm1 - 1])
                                        e.wait_ge(b.sem["SP"],
                                                  base["SP"] + 16 * (mm1 - 1))
                                    b.inc(e.activation(xib[mm1 % 2][:],
                                                       cac[mm1 % 2][:], AF.Silu,
                                                       bias=CV("cb" + dr, mm1)),
                                          "SD")
                                if m < 32:
                                    e.wait_ge(b.sem["TD"], base["TD"] + tdsum[m] +
                                              (2 if outerb else 1))
                                    if m >= 2:
                                        e.wait_ge(b.sem["VD"], base["VD"] + 5 * (m - 1))
                                    for n in range(NS2):
                                        b.inc(e.copy(
                                            xzp[m % 2][:, 3 + 512 * n:3 + 512 * (n + 1)],
                                            pt[n][:]), "SD")
                                    if outerb:
                                        pd0 = 0 if dr == "f" else 3 + TL
                                        b.inc(e.copy(xzp[m % 2][:, pd0:pd0 + 3],
                                                     pt[2][:, 0:3]), "SD")
                                    else:
                                        pd0 = 0 if dr == "f" else 3 + TL
                                        b.inc(e.copy(xzp[m % 2][:, pd0:pd0 + 3],
                                                     xzb[:, 3 * m:3 * (m + 1)]), "SD")
                                else:
                                    e.wait_ge(b.sem["TD"], base["TD"] + tdsum[m] + 1)
                                    if m >= 34:
                                        e.wait_ge(b.sem["SP"],
                                                  base["SP"] + 16 * (m - 1))
                                    for n in range(NS2):
                                        b.inc(e.activation(
                                            zb[m % 2][:, 512 * n:512 * (n + 1)],
                                            pt[n][:], AF.Silu), "SD")
                        blk.scalar(ev)

                        cw = ["cw" + dr + str(k) for k in range(4)]

                        def vv(e):
                            for m in range(32):
                                e.wait_ge(b.sem["SD"], base["SD"] + sd_evac_done[m])
                                x_ = xzp[m % 2]
                                offs = [0, 1, 2, 3] if dr == "f" else [6, 5, 4, 3]
                                b.inc(e.tensor_scalar(cac[m % 2][:],
                                                      x_[:, offs[0]:offs[0] + TL],
                                                      CV(cw[0], m), None, ALU.mult),
                                      "VD")
                                for k in range(1, 4):
                                    b.inc(e.scalar_tensor_tensor(
                                        cac[m % 2][:], x_[:, offs[k]:offs[k] + TL],
                                        CV(cw[k], m), cac[m % 2][:],
                                        ALU.mult, ALU.add), "VD")
                                if hi == 0:
                                    csrc = (x_[:, TL:TL + 3] if dr == "f"
                                            else x_[:, 3:6])
                                    b.inc(e.tensor_copy(xzb[:, 3 * m:3 * (m + 1)],
                                                        csrc), "VD")
                                else:
                                    b.inc(e.tensor_copy(cac[m % 2][:, 0:1],
                                                        cac[m % 2][:, 0:1]), "VD")
                        blk.vector(vv)

                        def gs(e):
                            for m in range(64):
                                if m < 32:
                                    e.wait_ge(b.sem["SD"],
                                              base["SD"] + sd_silu_done[m])
                                    b.inc(e.dma_start(
                                        out=scr[xinm][128 * m:128 * (m + 1),
                                                      t0:t0 + TL],
                                        in_=xib[m % 2][:]), "SP", 16)
                                else:
                                    e.wait_ge(b.sem["SD"],
                                              base["SD"] + sd_evac_done[m])
                                    b.inc(e.dma_start(
                                        out=scr[zsnm][128 * (m - 32):128 * (m - 31),
                                                      t0:t0 + TL],
                                        in_=zb[m % 2][:]), "SP", 16)
                        blk.gpsimd(gs)
                        b.barrier(blk)

                        base = dict(b.c)

                        def evx(e):
                            for n in range(NS2):
                                b.inc(e.copy(dtrb[:, 512 * n:512 * (n + 1)],
                                             pt[4 + n][:]), "SD")
                                b.inc(e.copy(bcb[0:32, 512 * n:512 * (n + 1)],
                                             pt[6 + n][0:32, :]), "SD")
                        blk.scalar(evx)

                        def gbc(e):
                            b.w(e, "SD")
                            b.inc(e.dma_start(
                                out=scr["bc_i"][0 if dr == "f" else 1][:, t0:t0 + TL],
                                in_=bcb[:]), "SP", 16)
                        blk.gpsimd(gbc)
                        b.barrier(blk, ("tensor", "vector", "sync"))

                        base = dict(b.c)
                        dttmp = [sbuf(st2_, [128, TL]) for _ in range(2)]
                        dtob = [sbuf(st2_, [128, TL]) for _ in range(2)]
                        xirb = [sbuf(st2_, [128, TL]) for _ in range(2)]
                        dxob = [sbuf(st2_, [128, TL]) for _ in range(2)]

                        def ldxi(e):
                            for m2 in range(KDI):
                                if m2 >= 2:
                                    e.wait_ge(b.sem["VD"], base["VD"] + m2 - 1)
                                b.inc(e.dma_start(
                                    out=xirb[m2 % 2][:],
                                    in_=scr[xinm][128 * m2:128 * (m2 + 1),
                                                  t0:t0 + TL]), "LD", 16)
                        blk.sync(ldxi)

                        def mmd(e):
                            for m2 in range(KDI):
                                if m2 >= 2:
                                    e.wait_ge(b.sem["SD"], base["SD"] + 4 * (m2 - 1))
                                po = 2 * (m2 % 2)
                                for n in range(NS2):
                                    i = e.matmul(pt[po + n][:],
                                                 dtw[:, 128 * m2:128 * (m2 + 1)],
                                                 dtrb[:, 512 * n:512 * (n + 1)],
                                                 start=True, stop=True)
                                b.inc(i, "TD")
                        blk.tensor(mmd)

                        def evd(e):
                            for m2 in range(KDI):
                                e.wait_ge(b.sem["TD"], base["TD"] + m2 + 1)
                                if m2 >= 2:
                                    e.wait_ge(b.sem["VD"], base["VD"] + m2 - 1)
                                    e.wait_ge(b.sem["SP"], base["SP"] + 32 * (m2 - 1))
                                po = 2 * (m2 % 2)
                                for n in range(NS2):
                                    b.inc(e.activation(
                                        dttmp[m2 % 2][:, 512 * n:512 * (n + 1)],
                                        pt[po + n][:], AF.Exp,
                                        bias=CV("dtb" + dr, m2)), "SD")
                                for n in range(NS2):
                                    b.inc(e.activation(
                                        dtob[m2 % 2][:, 512 * n:512 * (n + 1)],
                                        dttmp[m2 % 2][:, 512 * n:512 * (n + 1)],
                                        AF.Ln, bias=1.0), "SD")
                        blk.scalar(evd)

                        def vvd(e):
                            for m2 in range(KDI):
                                e.wait_ge(b.sem["SD"], base["SD"] + 4 * m2 + 4)
                                e.wait_ge(b.sem["LD"], base["LD"] + 16 * (m2 + 1))
                                b.inc(e.tensor_tensor(dxob[m2 % 2][:],
                                                      dtob[m2 % 2][:],
                                                      xirb[m2 % 2][:], ALU.mult),
                                      "VD")
                        blk.vector(vvd)

                        def gsd(e):
                            for m2 in range(KDI):
                                e.wait_ge(b.sem["VD"], base["VD"] + m2 + 1)
                                b.inc(e.dma_start(
                                    out=scr[dtnm][128 * m2:128 * (m2 + 1),
                                                  t0:t0 + TL],
                                    in_=dtob[m2 % 2][:]), "SP", 16)
                                b.inc(e.dma_start(
                                    out=scr[dxnm][128 * m2:128 * (m2 + 1),
                                                  t0:t0 + TL],
                                    in_=dxob[m2 % 2][:]), "SP", 16)
                        blk.gpsimd(gsd)
                        b.barrier(blk)
                        st2_.close()
    s4()

    def s567():
        if upto < 5:
            return
        def gx(e):
            for dr in ("f", "b"):
                for snm, onm in (("dt" + dr, "dt_sh" + dr), ("dx" + dr, "dx_sh" + dr)):
                    b.inc(e.collective_compute(
                        "AllToAll", ALU.bypass, replica_groups=[list(range(NCORE))],
                        ins=[scr[snm].rearrange("(j d) t -> j d t", j=8)],
                        outs=[scr[onm]]), "CC")
            b.inc(e.collective_compute(
                "AllGather", ALU.bypass, replica_groups=[list(range(NCORE))],
                ins=[scr["bc_i"]], outs=[scr["bc_o"]]), "CC")
            b.w(e, "CC")
        blk.gpsimd(gx)
        b.barrier(blk)

        for dr in ("f", "b"):
            rev = (dr == "b")
            CL = 1024
            chunks = list(range(16)) if not rev else list(range(15, -1, -1))
            with contextlib.ExitStack() as st_:
                dtt = [sbuf(st_, [128, CL]) for _ in range(4)]
                dxt = [sbuf(st_, [128, CL]) for _ in range(4)]
                Bb = [sbuf(st_, [128, CL]) for _ in range(2)]
                Cb = [sbuf(st_, [128, CL]) for _ in range(2)]
                dA = [sbuf(st_, [128, CL]) for _ in range(2)]
                bt = sbuf(st_, [128, CL])
                ht = sbuf(st_, [128, CL])
                yt = sbuf(st_, [128, CL])
                yac = [sbuf(st_, [128, CL]) for _ in range(4)]
                yrv = sbuf(st_, [128, CL])
                stc = [sbuf(st_, [128, DS]) for _ in range(4)]
                base = dict(b.c)
                VPI = 6
                VDC = VPI * 64
                V0 = base["VD"] + 4
                GD0 = base["GD"]

                def vz(e):
                    for dt_ in range(4):
                        vfence(e, e.memset(stc[dt_][:], 0.0))
                blk.vector(vz)

                for ci, c2 in enumerate(chunks):
                    c = c2 // 2
                    o0 = (c2 % 2) * CL
                    def ld(e, ci=ci, c=c, o0=o0):
                        if ci >= 1:
                            e.wait_ge(b.sem["VD"], V0 + VDC * ci)
                            e.wait_ge(b.sem["SD"], base["SD"] + 64 * ci)
                        for dt_ in range(4):
                            b.inc(e.dma_start(
                                out=dtt[dt_][:],
                                in_=scr["dt_sh" + dr][c, 128 * dt_:128 * (dt_ + 1),
                                                      o0:o0 + CL]), "LD", 16)
                            b.inc(e.dma_start(
                                out=dxt[dt_][:],
                                in_=scr["dx_sh" + dr][c, 128 * dt_:128 * (dt_ + 1),
                                                      o0:o0 + CL]), "LD", 16)
                        for s in range(DS):
                            if s >= 2:
                                e.wait_ge(b.sem["VD"], V0 + VDC * ci + 24 * (s - 1))
                            b.inc(e.dma_start(
                                out=Bb[s % 2][:],
                                in_=scr["bc_o"][c, 0 if dr == "f" else 1, s:s + 1,
                                                o0:o0 + CL]
                                .broadcast_to([128, CL])), "LD", 16)
                            b.inc(e.dma_start(
                                out=Cb[s % 2][:],
                                in_=scr["bc_o"][c, 0 if dr == "f" else 1,
                                                16 + s:17 + s, o0:o0 + CL]
                                .broadcast_to([128, CL])), "LD", 16)
                    blk.sync(ld)

                    def sc(e, ci=ci):
                        for s in range(DS):
                            for dt_ in range(4):
                                g = 64 * ci + 4 * s + dt_
                                e.wait_ge(b.sem["LD"], base["LD"] +
                                          16 * (40 * ci + 8 + 2 * (s + 1)))
                                if g >= 2:
                                    e.wait_ge(b.sem["VD"], V0 + VPI * (g - 2) + 3)
                                src_ap = dtt[dt_][:]
                                if rev:
                                    src_ap = dtt[dt_][:, ::-1]
                                acol = DV["A" + dr] + 16 * dt_ + s
                                b.inc(e.activation(dA[g % 2][:], src_ap, AF.Exp,
                                                   scale=dv[:, acol:acol + 1]), "SD")
                    blk.scalar(sc)

                    def vv(e, ci=ci):
                        for s in range(DS):
                            for dt_ in range(4):
                                g = 64 * ci + 4 * s + dt_
                                if s == 0 and dt_ == 0 and ci >= 1:
                                    e.wait_ge(b.sem["SP"], base["SP"] + 64 * ci)
                                if dt_ == 0:
                                    e.wait_ge(b.sem["LD"], base["LD"] +
                                              16 * (40 * ci + 8 + 2 * (s + 1)))
                                dxs = dxt[dt_][:]
                                Bs = Bb[s % 2][:]
                                Cs = Cb[s % 2][:]
                                if rev:
                                    dxs = dxt[dt_][:, ::-1]
                                    Bs = Bb[s % 2][:, ::-1]
                                    Cs = Cb[s % 2][:, ::-1]
                                b.inc(e.tensor_tensor(bt[:], dxs, Bs, ALU.mult),
                                      "VD")
                                e.wait_ge(b.sem["SD"], base["SD"] + g + 1)
                                b.inc(e.tensor_tensor_scan(
                                    ht[:], dA[g % 2][:], bt[:],
                                    stc[dt_][:, s:s + 1], ALU.mult, ALU.add), "VD")
                                e.wait_ge(b.sem["VD"], b.c["VD"])
                                b.inc(e.tensor_copy(stc[dt_][:, s:s + 1],
                                                    ht[:, CL - 1:CL]), "VD")
                                b.inc(e.tensor_tensor(yt[:], ht[:], Cs, ALU.mult),
                                      "VD")
                                if s == 0:
                                    b.inc(e.tensor_copy(yac[dt_][:], yt[:]), "VD")
                                else:
                                    b.inc(e.tensor_tensor(yac[dt_][:], yac[dt_][:],
                                                          yt[:], ALU.add), "VD")
                                b.inc(e.tensor_copy(bt[:, 0:1], bt[:, 0:1]), "VD")
                        if rev:
                            for dt_ in range(4):
                                b.inc(e.tensor_copy(yrv[:], yac[dt_][:, ::-1]), "GD")
                                e.wait_ge(b.sem["GD"], b.c["GD"])
                                b.inc(e.tensor_copy(yac[dt_][:], yrv[:]), "GD")
                    blk.vector(vv)

                    def gy(e, ci=ci, c=c, o0=o0):
                        e.wait_ge(b.sem["VD"], V0 + VDC * (ci + 1))
                        if rev:
                            e.wait_ge(b.sem["GD"], GD0 + 8 * (ci + 1))
                        for dt_ in range(4):
                            b.inc(e.dma_start(
                                out=scr["ysh_" + dr][c, 128 * dt_:128 * (dt_ + 1),
                                                     o0:o0 + CL],
                                in_=yac[dt_][:]), "SP", 16)
                    blk.gpsimd(gy)
                b.barrier(blk)

        def gx2(e):
            for dr in ("f", "b"):
                b.inc(e.collective_compute(
                    "AllToAll", ALU.bypass, replica_groups=[list(range(NCORE))],
                    ins=[scr["ysh_" + dr]], outs=[scr["yo_" + dr]]), "CC")
            b.w(e, "CC")
        blk.gpsimd(gx2)
        b.barrier(blk)

        for dr in ("f", "b"):
            with contextlib.ExitStack() as st_:
                yb = [sbuf(st_, [128, T]) for _ in range(2)]
                xb = [sbuf(st_, [128, T]) for _ in range(2)]
                zb = [sbuf(st_, [128, T]) for _ in range(2)]
                y3 = [sbuf(st_, [128, T]) for _ in range(2)]
                base = dict(b.c)

                def ld(e):
                    for m in range(KDI):
                        if m >= 2:
                            e.wait_ge(b.sem["VD"], base["VD"] + 2 * (m - 1))
                        src = m // 4
                        dl = 128 * (m % 4)
                        b.inc(e.dma_start(out=yb[m % 2][:],
                                          in_=scr["yo_" + dr][src, dl:dl + 128, :]),
                              "LD", 16)
                        b.inc(e.dma_start(out=xb[m % 2][:],
                                          in_=scr["xi" + dr][128 * m:128 * (m + 1), :]),
                              "LD", 16)
                        b.inc(e.dma_start(out=zb[m % 2][:],
                                          in_=scr["zs" + dr][128 * m:128 * (m + 1), :]),
                              "LD", 16)
                blk.sync(ld)

                def vv(e):
                    for m in range(KDI):
                        e.wait_ge(b.sem["LD"], base["LD"] + 48 * (m + 1))
                        if m >= 2:
                            e.wait_ge(b.sem["SP"], base["SP"] + 16 * (m - 1))
                        b.inc(e.scalar_tensor_tensor(y3[m % 2][:], xb[m % 2][:],
                                                     CV("D" + dr, m), yb[m % 2][:],
                                                     ALU.mult, ALU.add), "VD")
                        b.inc(e.tensor_tensor(y3[m % 2][:], y3[m % 2][:],
                                              zb[m % 2][:], ALU.mult), "VD")
                blk.vector(vv)

                def gs(e):
                    for m in range(KDI):
                        e.wait_ge(b.sem["VD"], base["VD"] + 2 * (m + 1))
                        b.inc(e.dma_start(out=scr["y3" + dr][128 * m:128 * (m + 1), :],
                                          in_=y3[m % 2][:]), "SP", 16)
                blk.gpsimd(gs)
                b.barrier(blk)
    s567()

    def s8():
        if upto < 8:
            return
        for dr in ("f", "b"):
            onm = "fout" if dr == "f" else "bout"
            for qtr in range(4):
                q0 = 512 * qtr
                QL = 512
                with contextlib.ExitStack() as st_:
                    y3q = sbuf(st_, [128, KDI * QL])
                    yb = [sbuf(st_, [128, QL]) for _ in range(2)]
                    wob = [sbuf(st_, [128, KDI * 128]) for _ in range(2)]
                    fb = [sbuf(st_, [128, QL]) for _ in range(2)]
                    base = dict(b.c)

                    def ldy(e):
                        for m in range(KDI):
                            if m >= 2:
                                e.wait_ge(b.sem["SD"], base["SD"] + m - 1)
                            b.inc(e.dma_start(
                                out=yb[m % 2][:],
                                in_=scr["y3" + dr][128 * m:128 * (m + 1),
                                                   q0:q0 + QL]), "LD", 16)
                    blk.sync(ldy)

                    def scy(e):
                        for m in range(KDI):
                            e.wait_ge(b.sem["LD"], base["LD"] + 16 * (m + 1))
                            b.inc(e.copy(y3q[:, QL * m:QL * (m + 1)], yb[m % 2][:]),
                                  "SD")
                    blk.scalar(scy)
                    b.barrier(blk)

                    base = dict(b.c)

                    def ld(e):
                        for m in range(KD):
                            if m >= 2:
                                e.wait_ge(b.sem["TD"], base["TD"] + m - 1)
                            b.inc(e.dma_start(out=wob[m % 2][:],
                                              in_=wblk_ap("outw" + dr + "T", m, DI)),
                                  "LD", 16)
                    blk.sync(ld)

                    def mm(e):
                        for m in range(KD):
                            e.wait_ge(b.sem["LD"], base["LD"] + 16 * (m + 1))
                            if m >= 2:
                                e.wait_ge(b.sem["SD"], base["SD"] + m - 1)
                            for k in range(KDI):
                                i = e.matmul(pt[m % 2][:, 0:QL],
                                             wob[m % 2][:, 128 * k:128 * (k + 1)],
                                             y3q[:, QL * k:QL * (k + 1)],
                                             start=(k == 0), stop=(k == KDI - 1))
                            b.inc(i, "TD")
                    blk.tensor(mm)

                    def ev(e):
                        for m in range(KD):
                            e.wait_ge(b.sem["TD"], base["TD"] + m + 1)
                            if m >= 2:
                                e.wait_ge(b.sem["SP"], base["SP"] + 16 * (m - 1))
                            b.inc(e.copy(fb[m % 2][:], pt[m % 2][:, 0:QL]), "SD")
                    blk.scalar(ev)

                    def gs(e):
                        for m in range(KD):
                            e.wait_ge(b.sem["SD"], base["SD"] + m + 1)
                            b.inc(e.dma_start(
                                out=scr[onm][128 * m:128 * (m + 1), q0:q0 + QL],
                                in_=fb[m % 2][:]), "SP", 16)
                    blk.gpsimd(gs)
                    b.barrier(blk)
    s8()

    def s9():
        if upto < 9 or upto in (98,):
            return
        with contextlib.ExitStack() as st_:
            fbv = [sbuf(st_, [128, T]) for _ in range(2)]
            bbv = [sbuf(st_, [128, T]) for _ in range(2)]
            hbv = [sbuf(st_, [128, T]) for _ in range(2)]
            rb = [sbuf(st_, [128, T]) for _ in range(2)]
            sq = [sbuf(st_, [128, T]) for _ in range(2)]
            base = dict(b.c)

            def ld(e):
                for m in range(KD):
                    if m >= 2:
                        e.wait_ge(b.sem["VD"], base["VD"] + 2 * (m - 1))
                    for snm, bb in (("fout", fbv), ("bout", bbv), ("hin", hbv)):
                        b.inc(e.dma_start(out=bb[m % 2][:],
                                          in_=scr[snm][128 * m:128 * (m + 1), :]),
                              "LD", 16)
            blk.sync(ld)

            def vv(e):
                for m in range(KD):
                    e.wait_ge(b.sem["LD"], base["LD"] + 48 * (m + 1))
                    if m >= 2:
                        e.wait_ge(b.sem["SP"], base["SP"] + 16 * (m - 1))
                        e.wait_ge(b.sem["SD"], base["SD"] + m - 1)
                        e.wait_ge(b.sem["TD"], base["TD"] + m - 1)
                    b.inc(e.tensor_tensor(rb[m % 2][:], fbv[m % 2][:], bbv[m % 2][:],
                                          ALU.add), "VD")
                    b.inc(e.tensor_tensor(rb[m % 2][:], rb[m % 2][:], hbv[m % 2][:],
                                          ALU.add), "VD")
            blk.vector(vv)

            def ss(e):
                for m in range(KD):
                    e.wait_ge(b.sem["VD"], base["VD"] + 2 * (m + 1))
                    if m >= 2:
                        e.wait_ge(b.sem["TD"], base["TD"] + m - 1)
                    b.inc(e.activation(sq[m % 2][:], rb[m % 2][:], AF.Square), "SD")
            blk.scalar(ss)

            def mm(e):
                for m in range(KD):
                    e.wait_ge(b.sem["SD"], base["SD"] + m + 1)
                    for n in range(NSEG):
                        e.matmul(pt[n][0:1, :], onescol,
                                 rb[m % 2][:, 512 * n:512 * (n + 1)],
                                 start=(m == 0), stop=(m == KD - 1))
                        i = e.matmul(pt[4 + n][0:1, :], onescol,
                                     sq[m % 2][:, 512 * n:512 * (n + 1)],
                                     start=(m == 0), stop=(m == KD - 1))
                    b.inc(i, "TD")
            blk.tensor(mm)

            def gs(e):
                for m in range(KD):
                    e.wait_ge(b.sem["VD"], base["VD"] + 2 * (m + 1))
                    b.inc(e.dma_start(out=scr["x3"][128 * m:128 * (m + 1), :],
                                      in_=rb[m % 2][:]), "SP", 16)
            blk.gpsimd(gs)
            b.barrier(blk)

            rr = sbuf(st_, [1, 3 * T])

            def evs(e):
                for n in range(NSEG):
                    e.mul(rr[0:1, 512 * n:512 * (n + 1)], pt[n][0:1, :], 1.0 / D)
                    e.mul(rr[0:1, T + 512 * n:T + 512 * (n + 1)], pt[4 + n][0:1, :],
                          1.0 / D)
                b.inc(e.activation(rr[0:1, 2 * T:3 * T], rr[0:1, 0:T], AF.Square),
                      "SD")
            blk.scalar(evs)

            def evv(e):
                b.w(e, "SD")
                vfence(e, e.tensor_tensor(rr[0:1, T:2 * T], rr[0:1, T:2 * T],
                                          rr[0:1, 2 * T:3 * T], ALU.subtract))
                vfence(e, e.tensor_scalar(rr[0:1, T:2 * T], rr[0:1, T:2 * T], EPS,
                                          None, ALU.add))
                vfence(e, e.reciprocal(rr[0:1, 2 * T:3 * T], rr[0:1, T:2 * T]))
            blk.vector(evv)

            def evs2(e):
                b.w(e, "VD")
                b.inc(e.activation(rr[0:1, T:2 * T], rr[0:1, 2 * T:3 * T], AF.Sqrt),
                      "SD")
            blk.scalar(evs2)

            def gr(e):
                b.w(e, "SD")
                b.inc(e.dma_start(out=row_ap("mur"), in_=rr[0:1, 0:T]), "SP", 16)
                b.inc(e.dma_start(out=row_ap("rstdr"), in_=rr[0:1, T:2 * T]),
                      "SP", 16)
            blk.gpsimd(gr)
            b.barrier(blk)
        if upto == 9:
            return

        for half in range(2):
            t0 = TL * half
            with contextlib.ExitStack() as st_:
                rl = sbuf(st_, [128, KD * TL])
                mub = sbuf(st_, [128, TL])
                rsb = sbuf(st_, [128, TL])
                xb = [sbuf(st_, [128, TL]) for _ in range(2)]
                hs = [sbuf(st_, [128, TL]) for _ in range(2)]
                wib = [sbuf(st_, [128, KD * 128]) for _ in range(2)]
                tb = [sbuf(st_, [128, TL]) for _ in range(2)]
                x3b = [sbuf(st_, [128, TL]) for _ in range(2)]
                base = dict(b.c)

                def ld(e):
                    b.inc(e.dma_start(out=mub[:], in_=row_ap("mur", t0, t0 + TL)
                                      .broadcast_to([128, TL])), "LD", 16)
                    b.inc(e.dma_start(out=rsb[:], in_=row_ap("rstdr", t0, t0 + TL)
                                      .broadcast_to([128, TL])), "LD", 16)
                    for m in range(KD):
                        if m >= 2:
                            e.wait_ge(b.sem["VD"], base["VD"] + m - 1)
                        b.inc(e.dma_start(out=xb[m % 2][:],
                                          in_=scr["x3"][128 * m:128 * (m + 1),
                                                        t0:t0 + TL]), "LD", 16)
                blk.sync(ld)

                def vv(e):
                    for m in range(KD):
                        e.wait_ge(b.sem["LD"], base["LD"] + 16 * (m + 3))
                        if m >= 2:
                            e.wait_ge(b.sem["SD"], base["SD"] + m - 1)
                        e.tensor_tensor(hs[m % 2][:], xb[m % 2][:], mub[:],
                                        ALU.subtract)
                        b.inc(e.tensor_tensor(hs[m % 2][:], hs[m % 2][:], rsb[:],
                                              ALU.mult), "VD")
                blk.vector(vv)

                def ss(e):
                    for m in range(KD):
                        e.wait_ge(b.sem["VD"], base["VD"] + m + 1)
                        b.inc(e.activation(rl[:, TL * m:TL * (m + 1)], hs[m % 2][:],
                                           AF.Identity, bias=CV("lnbr", m),
                                           scale=CV("lngr", m)), "SD")
                blk.scalar(ss)
                b.barrier(blk)

                base = dict(b.c)

                def ld2(e):
                    for m in range(KD):
                        if m >= 2:
                            e.wait_ge(b.sem["TD"], base["TD"] + m - 1)
                            e.wait_ge(b.sem["VD"], base["VD"] + m - 1)
                        b.inc(e.dma_start(out=wib[m % 2][:],
                                          in_=wblk_ap("wout1T", m, D)), "LD", 16)
                        b.inc(e.dma_start(out=xb[m % 2][:],
                                          in_=scr["x2"][128 * m:128 * (m + 1),
                                                        t0:t0 + TL]), "LD", 16)
                blk.sync(ld2)

                def mm2(e):
                    for m in range(KD):
                        e.wait_ge(b.sem["LD"], base["LD"] + 32 * m + 16)
                        if m >= 1:
                            e.wait_ge(b.sem["SD"], base["SD"] + NS2 * m)
                        for n in range(NS2):
                            for k in range(KD):
                                i = e.matmul(pt[n][:],
                                             wib[m % 2][:, 128 * k:128 * (k + 1)],
                                             rl[:, TL * k + 512 * n:TL * k + 512 * (n + 1)],
                                             start=(k == 0), stop=(k == KD - 1))
                        b.inc(i, "TD")
                blk.tensor(mm2)

                def ev2(e):
                    for m in range(KD):
                        e.wait_ge(b.sem["TD"], base["TD"] + m + 1)
                        if m >= 2:
                            e.wait_ge(b.sem["VD"], base["VD"] + m - 1)
                        for n in range(NS2):
                            b.inc(e.activation(tb[m % 2][:, 512 * n:512 * (n + 1)],
                                               pt[n][:], AF.Identity,
                                               bias=DVC("gabout1", m),
                                               scale=DVC("ga1", m)), "SD")
                blk.scalar(ev2)

                def v2(e):
                    for m in range(KD):
                        e.wait_ge(b.sem["SD"], base["SD"] + NS2 * (m + 1))
                        e.wait_ge(b.sem["LD"], base["LD"] + 32 * (m + 1))
                        if m >= 2:
                            e.wait_ge(b.sem["SP"], base["SP"] + 16 * (m - 1))
                        b.inc(e.tensor_tensor(x3b[m % 2][:], xb[m % 2][:],
                                              tb[m % 2][:], ALU.add), "VD")
                blk.vector(v2)

                def g2(e):
                    for m in range(KD):
                        e.wait_ge(b.sem["VD"], base["VD"] + m + 1)
                        b.inc(e.dma_start(out=scr["x1"][128 * m:128 * (m + 1),
                                                        t0:t0 + TL],
                                          in_=x3b[m % 2][:]), "SP", 16)
                blk.gpsimd(g2)
                b.barrier(blk)

        if upto == 10:
            return
        ln_stats(lambda m: scr["x1"][128 * m:128 * (m + 1), :], KD, "mu4", "rstd4")

        acol = es.enter_context(nc.sbuf_tensor("acol", [128, KD], F32))
        galf = es.enter_context(nc.sbuf_tensor("galf", [128, KD], F32))
        with contextlib.ExitStack() as st_:
            wb = [sbuf(st_, [128, D]) for _ in range(2)]
            base = dict(b.c)

            def ld(e):
                for m in range(KD):
                    if m >= 2:
                        e.wait_ge(b.sem["VD"], base["VD"] + m - 1)
                    b.inc(e.dma_start(out=wb[m % 2][:],
                                      in_=inp["wffnN"][128 * m:128 * (m + 1), :]),
                          "LD", 16)
            blk.sync(ld)

            def vv(e):
                for m in range(KD):
                    e.wait_ge(b.sem["LD"], base["LD"] + 16 * (m + 1))
                    b.inc(e.reduce_sum(acol[:, m:m + 1], wb[m % 2][:], axis=AX,
                                       apply_absolute_value=True), "VD")
                e.wait_ge(b.sem["VD"], b.c["VD"])
                vfence(e, e.tensor_scalar(acol[:], acol[:], 1.0 / D, None, ALU.mult))
                vfence(e, e.tensor_tensor(galf[:], acol[:],
                                          dv[:, DV["gb1"]:DV["gb1"] + KD], ALU.mult))
            blk.vector(vv)
            b.barrier(blk)

        if upto == 11:
            return
        for half in range(2):
            t0 = TL * half
            with contextlib.ExitStack() as st_:
                h4 = sbuf(st_, [128, KD * TL])
                mub = sbuf(st_, [128, TL])
                rsb = sbuf(st_, [128, TL])
                xb = [sbuf(st_, [128, TL]) for _ in range(2)]
                hs = [sbuf(st_, [128, TL]) for _ in range(2)]
                wib = [sbuf(st_, [128, KD * 128]) for _ in range(2)]
                sgn = [sbuf(st_, [128, KD * 128]) for _ in range(2)]
                tb = [sbuf(st_, [128, TL]) for _ in range(2)]
                x4b = [sbuf(st_, [128, TL]) for _ in range(2)]
                base = dict(b.c)

                def ld(e):
                    b.inc(e.dma_start(out=mub[:], in_=row_ap("mu4", t0, t0 + TL)
                                      .broadcast_to([128, TL])), "LD", 16)
                    b.inc(e.dma_start(out=rsb[:], in_=row_ap("rstd4", t0, t0 + TL)
                                      .broadcast_to([128, TL])), "LD", 16)
                    for m in range(KD):
                        if m >= 2:
                            e.wait_ge(b.sem["VD"], base["VD"] + m - 1)
                        b.inc(e.dma_start(out=xb[m % 2][:],
                                          in_=scr["x1"][128 * m:128 * (m + 1),
                                                        t0:t0 + TL]), "LD", 16)
                blk.sync(ld)

                def vv(e):
                    for m in range(KD):
                        e.wait_ge(b.sem["LD"], base["LD"] + 16 * (m + 3))
                        if m >= 2:
                            e.wait_ge(b.sem["SD"], base["SD"] + m - 1)
                        e.tensor_tensor(hs[m % 2][:], xb[m % 2][:], mub[:],
                                        ALU.subtract)
                        b.inc(e.tensor_tensor(hs[m % 2][:], hs[m % 2][:], rsb[:],
                                              ALU.mult), "VD")
                blk.vector(vv)

                def ss(e):
                    for m in range(KD):
                        e.wait_ge(b.sem["VD"], base["VD"] + m + 1)
                        b.inc(e.activation(h4[:, TL * m:TL * (m + 1)], hs[m % 2][:],
                                           AF.Identity, bias=DVC("sb1", m),
                                           scale=DVC("c1pcb1", m)), "SD")
                blk.scalar(ss)
                b.barrier(blk)

                for m in range(KD):
                    base = dict(b.c)

                    def ld2(e, m=m):
                        if m >= 1:
                            b.w(e, "VD")
                        b.inc(e.dma_start(out=wib[m % 2][:],
                                          in_=wblk_ap("wffnT", m, D)), "LD", 16)
                        b.inc(e.dma_start(out=xb[m % 2][:],
                                          in_=scr["x1"][128 * m:128 * (m + 1),
                                                        t0:t0 + TL]), "LD", 16)
                    blk.sync(ld2)

                    def sg(e, m=m):
                        b.w(e, "LD")
                        b.inc(e.sign(sgn[m % 2][:], wib[m % 2][:]), "SD")
                    blk.scalar(sg)

                    def mm2(e, m=m):
                        b.w(e, "SD")
                        b.w(e, "VD")
                        for n in range(NS2):
                            for k in range(KD):
                                i = e.matmul(pt[n][:],
                                             sgn[m % 2][:, 128 * k:128 * (k + 1)],
                                             h4[:, TL * k + 512 * n:TL * k + 512 * (n + 1)],
                                             start=(k == 0), stop=(k == KD - 1))
                        b.inc(i, "TD")
                    blk.tensor(mm2)

                    def ev2(e, m=m):
                        b.w(e, "TD")
                        for n in range(NS2):
                            b.inc(e.activation(tb[m % 2][:, 512 * n:512 * (n + 1)],
                                               pt[n][:], AF.Identity,
                                               bias=DVC("gbbffn", m),
                                               scale=galf[:, m:m + 1]), "SD")
                    blk.scalar(ev2)

                    def v2(e, m=m):
                        b.w(e, "SD")
                        b.w(e, "SP")
                        b.inc(e.tensor_tensor(x4b[m % 2][:], xb[m % 2][:],
                                              tb[m % 2][:], ALU.add), "VD")
                    blk.vector(v2)

                    def g2(e, m=m):
                        b.w(e, "VD")
                        b.inc(e.dma_start(out=out_x[128 * m:128 * (m + 1),
                                                    t0:t0 + TL],
                                          in_=x4b[m % 2][:]), "SP", 16)
                    blk.gpsimd(g2)
                b.barrier(blk)
    s9()


    def fin(dbg_src=None):
        if isinstance(dbg_src, str) and dbg_src.startswith("h1"):
            qq = {"h1a": 0, "h1b": 1, "h1c": 2, "h1d": 3}[dbg_src]

            def gh(e):
                for m in range(KD):
                    b.inc(e.dma_start(
                        out=out_x[128 * m:128 * (m + 1), :],
                        in_=scr["h1"][2048 * qq + 128 * m:2048 * qq + 128 * (m + 1), :]),
                        "SP", 16)
            blk.gpsimd(gh)

            def syh(e):
                b.w(e, "SP")
            blk.sync(syh)
            return
        if dbg_src == "ROWS":
            def gr(e):
                b.inc(e.dma_start(out=out_x[0:48, :], in_=scr["rows"][:, :]),
                      "SP", 16)
            blk.gpsimd(gr)

            def syr(e):
                b.w(e, "SP")
            blk.sync(syr)
            return
        if dbg_src is None:
            def syn(e):
                b.w(e, "SP")
            blk.sync(syn)
            return
        if dbg_src == "DV":
            def gdv(e):
                b.inc(e.dma_start(out=out_x[0:128, 0:512], in_=dv[:]), "SP", 16)
                b.inc(e.dma_start(out=out_x[0:128, 512:512 + NCV], in_=cv[:]),
                      "SP", 16)
            blk.gpsimd(gdv)

            def syf2(e):
                b.w(e, "SP")
            blk.sync(syf2)
            return
        src_t = dbg_src if dbg_src is not None else scr["x1"]

        def gf(e):
            for m in range(KD):
                b.inc(e.dma_start(out=out_x[128 * m:128 * (m + 1), :],
                                  in_=src_t[128 * m:128 * (m + 1), :]), "SP", 16)
        blk.gpsimd(gf)

        def syf(e):
            b.w(e, "SP")
        blk.sync(syf)

    if upto < 99:
        fin(_DBG[0] if _DBG[0] in ("DV", "ROWS", "h1a", "h1b", "h1c", "h1d") else
            (scr[_DBG[0]] if _DBG[0] else
             {1: scr["x1"], 2: scr["x2"], 3: scr["hin"], 9: scr["x3"],
              10: scr["x1"]}.get(upto, scr["x1"])))
    else:
        fin(None)
    return nc


_DBG = [None]


def kernel(**inputs):
    maps, _ = _prep_inputs(inputs)
    with allow_ici_all_to_all():
        nc = _build(_UPTO[0])
        res = run_bass_kernel_spmd(nc, maps, core_ids=list(range(NCORE)))
    outs = [np.asarray(r["out_x"], np.float32).T for r in res.results]
    return np.concatenate(outs, axis=0)[None]


_UPTO = [99]
